# version 18
# speedup vs baseline: 1.0995x; 1.0333x over previous
from contextlib import ExitStack

import numpy as np
import concourse.bass as bass
import concourse.mybir as mybir
from concourse.bass_utils import run_bass_kernel_spmd

F32 = mybir.dt.float32
BF16 = mybir.dt.bfloat16
SIM_WAITS = False
F8E4 = mybir.dt.float8e4
ALU = mybir.AluOpType
AF = mybir.ActivationFunctionType
AX = mybir.AxisListType.X

T = 2048
D = 1024
H = 16
NT = T // 128
ND = D // 128
NB = 8
BAND = T // NB
TH = T // 2
SCALE_A = 6.0 ** (-0.5)
EPS = 1e-5
N_CORES = 8

PAIRS = [(0, 1), (0, 2), (0, 3), (1, 2), (1, 3), (2, 3)]
J6SIGN = [1.0, -1.0, 1.0, 1.0, -1.0, 1.0]
PASSES3 = [(0, 0), (0, 1), (1, 0)]


class Sem:
    def __init__(self, handle):
        self.h = handle
        self.v = 0

    def inc(self, inst, k=1):
        inst.then_inc(self.h, k)
        self.v += k
        return self.v

    def dma(self, inst):
        return self.inc(inst, 16)


def build_nc(n_stages=4):
    nc = bass.Bass("TRN2", target_bir_lowering=False)

    x_in = nc.dram_tensor("x", [T, D], F32, kind="ExternalInput").ap()
    W1q_d = nc.dram_tensor("W1q", [2 * D, 64], F32, kind="ExternalInput").ap()
    W2q_d = nc.dram_tensor("W2q", [2 * D, 64], F32, kind="ExternalInput").ap()
    W1k_d = nc.dram_tensor("W1k", [D, 64], F32, kind="ExternalInput").ap()
    W2k_d = nc.dram_tensor("W2k", [D, 64], F32, kind="ExternalInput").ap()
    Wv_d = nc.dram_tensor("Wv", [D, D], F32, kind="ExternalInput").ap()
    bv_d = nc.dram_tensor("bv", [1, D], F32, kind="ExternalInput").ap()
    Wo_d = nc.dram_tensor("Wo", [D, D], F32, kind="ExternalInput").ap()
    bo_d = nc.dram_tensor("bo", [1, D], F32, kind="ExternalInput").ap()
    Wqkv_d = nc.dram_tensor("Wqkv", [D, 3 * D], F32, kind="ExternalInput").ap()
    Wf_d = nc.dram_tensor("Wf", [D, D], F32, kind="ExternalInput").ap()
    bf_d = nc.dram_tensor("bf", [1, D], F32, kind="ExternalInput").ap()
    out_d = nc.dram_tensor("out", [T, D], F32, kind="ExternalOutput").ap()
    cc_in = [nc.dram_tensor(f"cc_in{i}", [1, 128], F32) for i in range(2)]
    cc_out = [nc.dram_tensor(f"cc_out{i}", [1, 128], F32) for i in range(2)]

    ident_c = nc.inline_tensor(np.eye(128, dtype=np.float32), name="identc").ap()
    _G = np.zeros((128, 8 * 96), dtype=np.float32)
    for h_ in range(16):
        for c_ in range(6):
            a_, b_ = PAIRS[c_]
            _G[4 * h_ + a_, 0 * 96 + 6 * h_ + c_] = 1.0
            _G[64 + 4 * h_ + b_, 1 * 96 + 6 * h_ + c_] = 1.0
            _G[4 * h_ + b_, 2 * 96 + 6 * h_ + c_] = 1.0
            _G[64 + 4 * h_ + a_, 3 * 96 + 6 * h_ + c_] = 1.0
            ap_, bp_ = PAIRS[5 - c_]
            s_ = float(J6SIGN[c_])
            _G[4 * h_ + ap_, 4 * 96 + 6 * h_ + c_] = s_
            _G[64 + 4 * h_ + bp_, 5 * 96 + 6 * h_ + c_] = 1.0
            _G[4 * h_ + bp_, 6 * 96 + 6 * h_ + c_] = s_
            _G[64 + 4 * h_ + ap_, 7 * 96 + 6 * h_ + c_] = 1.0
    gmat_c = nc.inline_tensor(_G, name="gmatc").ap()
    triu_c = nc.inline_tensor(np.triu(np.ones((128, 128), dtype=np.float32)),
                              name="triuc").ap()


    st = ExitStack()
    e = st.enter_context

    xres = [e(nc.sbuf_tensor(f"xres{i}", [128, D], F32)) for i in range(NT)]
    idf = e(nc.sbuf_tensor("idf", [128, 128], F32))
    idb = e(nc.sbuf_tensor("idb", [128, 128], BF16))
    triu = e(nc.sbuf_tensor("triu", [128, 128], F32))
    mu = e(nc.sbuf_tensor("mu", [128, NT], F32))
    var_e = e(nc.sbuf_tensor("var_e", [128, NT], F32))
    rstd = e(nc.sbuf_tensor("rstd", [128, NT], F32))
    nsc = e(nc.sbuf_tensor("nsc", [128, NT], F32))
    bnst = e(nc.sbuf_tensor("bnst", [128, 12], F32))
    bnmv = e(nc.sbuf_tensor("bnmv", [128, 2], F32))
    pT = [e(nc.sbuf_tensor(f"pT{p}", [128, 512], F32)) for p in range(2)]
    ev32 = e(nc.sbuf_tensor("ev32", [128, 512], F32))
    sc_row = e(nc.sbuf_tensor("sc_row", [1, 128], F32))
    scal = e(nc.sbuf_tensor("scal", [1, 8], F32))
    amax_acc = e(nc.sbuf_tensor("amax_acc", [128, 1], F32))
    sx_t = e(nc.sbuf_tensor("sx_t", [128, 1], F32))
    rs_t = e(nc.sbuf_tensor("rs_t", [128, 1], F32))
    rd1 = e(nc.sbuf_tensor("rd1", [128, 1], F32))

    ps_big = e(nc.psum_tensor("ps_big", [128, T], F32))
    ps_o = e(nc.psum_tensor("ps_o", [128, T], F32))
    def _slot(ap):
        return (ap, ap.bitcast(BF16))

    SLOT_B2 = _slot(ps_big[:, 1024:1152])
    SLOT_B3 = _slot(ps_big[:, 1536:1664])
    SLOT_B0 = _slot(ps_big[:, 0:128])
    SLOT_B1 = _slot(ps_big[:, 512:640])
    SLOTS4 = [SLOT_B2, SLOT_B3, SLOT_B0, SLOT_B1]
    SLOTS2 = [SLOT_B2, SLOT_B3]
    SLOTS_A = SLOTS4

    DLS, DWS, PE, VE, SE, CC, DCC = (Sem(e(nc.semaphore(n))) for n in
                                     ["dls", "dws", "pe", "ve", "se", "cc", "dcc"])
    sem_s = Sem(e(nc.semaphore("sem_s")))
    sem_exp = Sem(e(nc.semaphore("sem_exp")))
    sem_msk = Sem(e(nc.semaphore("sem_msk")))
    sem_pv = Sem(e(nc.semaphore("sem_pv")))

    block = e(nc.Block())
    on_sy, on_pe, on_ve, on_se, on_gp = (block.sync, block.tensor, block.vector,
                                         block.scalar, block.gpsimd)

    @on_sy
    def _(sync):
        for i in range(NT):
            DLS.dma(sync.dma_start(out=xres[i][:], in_=x_in[128 * i:128 * (i + 1), :]))
        DWS.dma(sync.dma_start(out=idf[:], in_=ident_c))
        DWS.dma(sync.dma_start(out=triu[:], in_=triu_c))
    x_loaded = DLS.v
    const_loaded = DWS.v

    @on_ve
    def _(vector):
        vector.wait_ge(DWS.h, const_loaded)
        vector.wait_ge(VE.h, VE.v)
        VE.inc(vector.tensor_copy(idb[:], idf[:]))
        vector.wait_ge(VE.h, VE.v)
        m = VE.inc(vector.memset(scal[:], 0.0))
        vector.wait_ge(VE.h, m)
        vector.wait_ge(VE.h, VE.v)
        VE.inc(vector.tensor_scalar_add(scal[:, 0:1], scal[:, 0:1], 448.0))

    def ln_stats(first=False):
        @on_ve
        def _(vector):
            if first:
                vector.wait_ge(DLS.h, x_loaded)
            for i in range(NT):
                vector.wait_ge(VE.h, VE.v)
                a = VE.inc(vector.bn_stats(bnst[:, 0:6], xres[i][:, 0:512]))
                vector.wait_ge(VE.h, VE.v)
                b = VE.inc(vector.bn_stats(bnst[:, 6:12], xres[i][:, 512:1024]))
                vector.wait_ge(VE.h, b)
                vector.wait_ge(VE.h, VE.v)
                c = VE.inc(vector.bn_aggr(bnmv[:], bnst[:]))
                vector.wait_ge(VE.h, c)
                vector.wait_ge(VE.h, VE.v)
                VE.inc(vector.tensor_copy(mu[:, i:i + 1], bnmv[:, 0:1]))
                vector.wait_ge(VE.h, VE.v)
                d = VE.inc(vector.tensor_scalar_add(var_e[:, i:i + 1], bnmv[:, 1:2], EPS))
                vector.wait_ge(VE.h, d)
        ve_at = VE.v

        @on_se
        def _(scalar):
            scalar.wait_ge(VE.h, ve_at)
            s1_ = SE.inc(scalar.activation(rstd[:], var_e[:], AF.Ln))
            scalar.wait_ge(SE.h, s1_)
            SE.inc(scalar.activation(rstd[:], rstd[:], AF.Exp, scale=-0.5))
        se_at = SE.v

        @on_ve
        def _(vector):
            vector.wait_ge(SE.h, se_at)
            for _ in range(2):
                vector.wait_ge(VE.h, VE.v)
                a = VE.inc(vector.tensor_mul(nsc[:], rstd[:], rstd[:]))
                vector.wait_ge(VE.h, a)
                vector.wait_ge(VE.h, VE.v)
                b = VE.inc(vector.tensor_mul(nsc[:], nsc[:], var_e[:]))
                vector.wait_ge(VE.h, b)
                vector.wait_ge(VE.h, VE.v)
                c = VE.inc(vector.tensor_scalar(nsc[:], nsc[:], -0.5, 1.5,
                                                ALU.mult, ALU.add))
                vector.wait_ge(VE.h, c)
                vector.wait_ge(VE.h, VE.v)
                d = VE.inc(vector.tensor_mul(rstd[:], rstd[:], nsc[:]))
                vector.wait_ge(VE.h, d)
        return VE.v

    tp = {"n": 0, "pe": [], "ve": [], "slots": SLOTS_A, "last": {}}

    def twait(eng, tok):
        if isinstance(tok, tuple):
            eng.wait_ge(SE.h if tok[0] else VE.h, tok[1])
        elif tok:
            eng.wait_ge(VE.h, tok)

    def switch_slots(new_slots):
        tp["slots"] = new_slots

    def emit_transpose(src_ap, ident, m_out, pe_wait=None, evac_fn=None,
                       dst_ap=None, n_out=128, se_evac=False, gmat=None):
        k = tp["n"]
        slot = tp["slots"][k % len(tp["slots"])]
        sid = id(slot)
        war_tok = tp["last"].get(sid)
        view = (slot[1] if src_ap.dtype == BF16 else slot[0])[0:m_out, 0:n_out]

        @on_pe
        def _(tensor, k=k, src=src_ap, ident=ident, view=view, war_tok=war_tok,
              gmat=gmat):
            twait(tensor, war_tok)
            if pe_wait is not None:
                pe_wait(tensor)
            if gmat is not None:
                tp["pe"].append(PE.inc(tensor.matmul(view, src, gmat,
                                                     start=True, stop=True)))
            else:
                tp["pe"].append(PE.inc(tensor.transpose(view, src, ident)))

        if se_evac:
            @on_se
            def _(scalar, k=k, dst=dst_ap, view=view):
                scalar.wait_ge(PE.h, tp["pe"][k])
                scalar.wait_ge(SE.h, SE.v)
                tok = (True, SE.inc(scalar.activation(dst, view, AF.Copy)))
                tp["ve"].append(tok)
                tp["last"][sid] = tok
        else:
            @on_ve
            def _(vector, k=k, dst=dst_ap, view=view):
                vector.wait_ge(PE.h, tp["pe"][k])
                if evac_fn is not None:
                    tok = (False, evac_fn(vector, view))
                else:
                    tok = (False, VE.inc(vector.tensor_copy(dst, view)))
                tp["ve"].append(tok)
                tp["last"][sid] = tok
        tp["n"] += 1

    def tp_last_ve():
        return tp["ve"][-1] if tp["ve"] else 0

    def tp_last_pe():
        return tp["pe"][-1] if tp["pe"] else 0

    if n_stages >= 1 and n_stages not in (-2, -3):
        s1 = ExitStack()
        e1 = s1.enter_context
        vph = [e1(nc.sbuf_tensor(f"vph{i}", [128, H * 65], F32)) for i in range(NT)]
        bv_hc = e1(nc.sbuf_tensor("bv_hc", [64, H], F32))
        sv_f = e1(nc.sbuf_tensor("sv_f", [128, ND], F32))

        sRq = ExitStack()
        eRq = sRq.enter_context
        q12_f = eRq(nc.sbuf_tensor("q12_f", [128, T], F32, side="right"))
        k12_f = eRq(nc.sbuf_tensor("k12_f", [128, T], F32, side="right"))

        sRw = ExitStack()
        eRw = sRw.enter_context
        wqf = eRw(nc.sbuf_tensor("wqf", [128, 16 * 128], F32, side="right"))
        wkf = eRw(nc.sbuf_tensor("wkf", [128, 8 * 128], F32, side="right"))

        ln1_rstd = ln_stats(first=True)

        @on_sy
        def _(sync):
            sync.wait_ge(DWS.h, DWS.v)
            for c in range(16):
                DWS.dma(sync.dma_start(out=wqf[:, 128 * c:128 * c + 64],
                                       in_=W1q_d[128 * c:128 * (c + 1), :]))
                DWS.dma(sync.dma_start(out=wqf[:, 128 * c + 64:128 * (c + 1)],
                                       in_=W2q_d[128 * c:128 * (c + 1), :]))
            sync.wait_ge(DWS.h, DWS.v)
            for c in range(8):
                DWS.dma(sync.dma_start(out=wkf[:, 128 * c:128 * c + 64],
                                       in_=W1k_d[128 * c:128 * (c + 1), :]))
                DWS.dma(sync.dma_start(out=wkf[:, 128 * c + 64:128 * (c + 1)],
                                       in_=W2k_d[128 * c:128 * (c + 1), :]))
        wqk_loaded = DWS.v

        for h in range(H):
            @on_sy
            def _(sync, h=h):
                if h > 0:
                    sync.wait_ge(PE.h, tp["pe"][-1])
                sync.wait_ge(DWS.h, DWS.v)
                DWS.dma(sync.dma_start(out=sc_row[0:1, 0:64],
                                       in_=bv_d[0:1, 64 * h:64 * h + 64]))
            bvh_dma = DWS.v

            def evacB(vector, view, h=h):
                vector.wait_ge(VE.h, VE.v)
                return VE.inc(vector.tensor_copy(bv_hc[:, h:h + 1], view[:, 0:1]))
            emit_transpose(sc_row[0:1, 0:64], idf[0:1, 0:1], 64,
                           pe_wait=(lambda t, v=bvh_dma: t.wait_ge(DWS.h, v)),
                           evac_fn=evacB, n_out=1)
        bv_ready = tp_last_ve()

        sRl = ExitStack()
        eRl = sRl.enter_context
        lnf = [eRl(nc.sbuf_tensor(f"lnf{c}", [128, TH], F32, side="right"))
               for c in range(ND)]
        wvs = eRl(nc.sbuf_tensor("wvs", [128, 8 * 128], F32, side="right"))

        @on_ve
        def _(vector):
            vector.wait_ge(DLS.h, x_loaded)
            for i in range(NT):
                vector.wait_ge(VE.h, VE.v)
                VE.inc(vector.memset(vph[i][:], 1.0))
        vph_init = VE.v

        wv_gate = [None]
        q12_done_half = []
        for half in range(2):
            t0 = TH * half
            for iloc in range(NT // 2):
                i = (NT // 2) * half + iloc

                if half == 1 and iloc == 0:
                    @on_ve
                    def _(vector, _t0k=tp_last_ve()):
                        twait(vector, _t0k)
                        for c in range(ND):
                            vector.wait_ge(VE.h, VE.v)
                            VE.inc(vector.tensor_copy(sv_f[:, c:c + 1],
                                                      lnf[c][:, TH - 1:TH]))
                        vector.wait_ge(PE.h, half_consumed[0])
                for sub in range(2):
                    @on_ve
                    def _(vector, i=i, half=half, iloc=iloc, sub=sub,
                          _wp=tp_last_pe()):
                        if half == 0 and iloc == 0 and sub == 0:
                            vector.wait_ge(VE.h, ln1_rstd)
                        if _wp:
                            vector.wait_ge(PE.h, _wp)
                        vector.wait_ge(VE.h, VE.v)
                        w = VE.inc(vector.tensor_scalar(
                            ev32[:], xres[i][:, 512 * sub:512 * (sub + 1)],
                            mu[:, i:i + 1], rstd[:, i:i + 1],
                            ALU.subtract, ALU.mult))
                        vector.wait_ge(VE.h, w)
                    ve_split = VE.v
                    for cl in range(4):
                        c = 4 * sub + cl
                        emit_transpose(ev32[:, 128 * cl:128 * (cl + 1)], idf[:], 128,
                                       pe_wait=(lambda t, v=ve_split: t.wait_ge(VE.h, v)),
                                       dst_ap=lnf[c][:, 128 * iloc:128 * (iloc + 1)],
                                       se_evac=False)
            ln_done = tp_last_ve()

            @on_pe
            def _(tensor, half=half, t0=t0, ln_done=ln_done):
                twait(tensor, ln_done)
                tensor.wait_ge(DWS.h, wqk_loaded)

                for c in range(8):
                    for nq in range(2):
                        n0l = 512 * nq
                        n0g = t0 + n0l
                        tensor.matmul(ps_big[:, n0g:n0g + 512],
                                      wqf[:, 128 * c:128 * (c + 1)],
                                      lnf[c][:, n0l:n0l + 512],
                                      start=(c == 0), stop=False)
                for c in range(8):
                    ws = wqf[:, 128 * (c + 8):128 * (c + 9)]
                    xt = lnf[c]
                    for nq in range(2):
                        n0l = 512 * nq
                        n0g = t0 + n0l
                        stopf = (c == 7)
                        if n0g == 0:
                            tensor.matmul(ps_big[:, 1:512], ws,
                                          xt[:, 0:511], start=False, stop=stopf)
                        elif n0l == 0:
                            tensor.matmul(ps_big[:, t0:t0 + 1], ws,
                                          sv_f[:, c:c + 1],
                                          start=False, stop=False)
                            tensor.matmul(ps_big[:, t0 + 1:t0 + 512], ws,
                                          xt[:, 0:511], start=False, stop=stopf)
                        else:
                            tensor.matmul(ps_big[:, n0g:n0g + 512], ws,
                                          xt[:, n0l - 1:n0l + 511],
                                          start=False, stop=stopf)
                PE.inc(tensor.matmul(ps_o[0:1, 1536:1537], triu[0:1, 0:1],
                                     triu[0:1, 0:1], start=True, stop=True))
            q12_pe = PE.v

            @on_ve
            def _(vector, t0=t0, q12_pe=q12_pe):
                vector.wait_ge(PE.h, q12_pe)
                vector.wait_ge(VE.h, VE.v)
                VE.inc(vector.tensor_copy(q12_f[:, t0:t0 + TH], ps_big[:, t0:t0 + TH]))
            q12_ev = VE.v

            @on_pe
            def _(tensor, t0=t0, q12_ev=q12_ev):
                tensor.wait_ge(VE.h, q12_ev)
                for c in range(8):
                    for nq in range(2):
                        n0l = 512 * nq
                        tensor.matmul(ps_big[:, t0 + n0l:t0 + n0l + 512],
                                      wkf[:, 128 * c:128 * (c + 1)],
                                      lnf[c][:, n0l:n0l + 512],
                                      start=(c == 0), stop=(c == 7))
                PE.inc(tensor.matmul(ps_o[0:1, 1536:1537], triu[0:1, 0:1],
                                     triu[0:1, 0:1], start=True, stop=True))
            k12_pe = PE.v

            @on_ve
            def _(vector, t0=t0, k12_pe=k12_pe):
                vector.wait_ge(PE.h, k12_pe)
                vector.wait_ge(VE.h, VE.v)
                VE.inc(vector.tensor_copy(k12_f[:, t0:t0 + TH], ps_big[:, t0:t0 + TH]))
            k12_ev = VE.v

            for p in range(H // 2):
                @on_sy
                def _(sync, p=p, half=half, _wg=wv_gate[0]):
                    if _wg is not None:
                        sync.wait_ge(PE.h, _wg)
                    sync.wait_ge(DWS.h, DWS.v)
                    for c in range(8):
                        DWS.dma(sync.dma_start(
                            out=wvs[:, 128 * c:128 * (c + 1)],
                            in_=Wv_d[128 * c:128 * (c + 1), 128 * p:128 * (p + 1)]))
                wv_loaded = DWS.v

                @on_pe
                def _(tensor, wv_loaded=wv_loaded, p=p, half=half, k12_ev=k12_ev):
                    tensor.wait_ge(DWS.h, wv_loaded)
                    if p == 0:
                        tensor.wait_ge(VE.h, k12_ev)
                    for c in range(8):
                        for nq in range(2):
                            n0l = 512 * nq
                            tensor.matmul(ps_o[:, n0l:n0l + 512],
                                          wvs[:, 128 * c:128 * (c + 1)],
                                          lnf[c][:, n0l:n0l + 512],
                                          start=(c == 0), stop=(c == 7))
                    PE.inc(tensor.matmul(ps_big[0:1, 256:257], triu[0:1, 0:1],
                                         triu[0:1, 0:1], start=True, stop=True))
                vp_pe = PE.v
                wv_gate[0] = vp_pe

                for nq in range(2):
                    @on_ve
                    def _(vector, nq=nq, vp_pe=vp_pe):
                        vector.wait_ge(PE.h, vp_pe)
                        if nq == 0:
                            vector.wait_ge(VE.h, vph_init)
                        vector.wait_ge(VE.h, VE.v)
                        VE.inc(vector.tensor_copy(ev32[:], ps_o[:, 512 * nq:512 * (nq + 1)]))
                    ev_done = VE.v
                    for tloc in range(4):
                        i = (NT // 2) * half + 4 * nq + tloc
                        hA, hB = 2 * p, 2 * p + 1

                        def evacV(vector, view, i=i, hA=hA, hB=hB):
                            vector.wait_ge(VE.h, VE.v)
                            VE.inc(vector.tensor_copy(
                                vph[i][:, 65 * hA:65 * hA + 64], view[:, 0:64]))
                            vector.wait_ge(VE.h, VE.v)
                            return VE.inc(vector.tensor_copy(
                                vph[i][:, 65 * hB:65 * hB + 64], view[:, 64:128]))
                        emit_transpose(ev32[:, 128 * tloc:128 * (tloc + 1)], idf[:], 128,
                                       pe_wait=(lambda t, v=ev_done: t.wait_ge(VE.h, v)),
                                       evac_fn=evacV)
            half_consumed = [PE.v]
        vprime_done = tp_last_ve()

        sRl.close()
        sRw.close()

        sRe = ExitStack()
        eRe = sRe.enter_context
        rep = [eRe(nc.sbuf_tensor(f"rep{r}", [128, 96], F32, side="right"))
               for r in range(2)]
        gmats = eRe(nc.sbuf_tensor("gmats", [128, 8 * 96], F32, side="right"))
        L32 = eRe(nc.sbuf_tensor("L32", [128, 96], F32, side="right"))
        Lsq = eRe(nc.sbuf_tensor("Lsq", [128, 96], F32, side="right"))
        Lss = eRe(nc.sbuf_tensor("Lss", [128, 16], F32, side="right"))
        Lrs = eRe(nc.sbuf_tensor("Lrs", [128, 16], F32, side="right"))
        Lt1 = eRe(nc.sbuf_tensor("Lt1", [128, 16], F32, side="right"))
        Lh_s = eRe(nc.sbuf_tensor("Lh_s", [128, 96], BF16, side="right"))
        Ll_s = eRe(nc.sbuf_tensor("Ll_s", [128, 96], BF16, side="right"))
        stk_big = eRe(nc.sbuf_tensor("stk_big", [128, 576], BF16, side="right"))

        lqstk = [e1(nc.sbuf_tensor(f"lqstk{g}", [128, T], BF16)) for g in range(6)]
        jkstk = [e1(nc.sbuf_tensor(f"jkstk{g}", [128, T], BF16)) for g in range(6)]

        @on_ve
        def _(vector):
            vector.wait_ge(VE.h, VE.v)
            VE.inc(vector.memset(stk_big[:], 0.0))
        stk_init = VE.v

        @on_sy
        def _(sync):
            sync.wait_ge(DWS.h, DWS.v)
            DWS.dma(sync.dma_start(out=gmats[:], in_=gmat_c))
        gm_ld = DWS.v

        def emit_exterior_tile(i):
            for which in ("q", "k"):
                fsrc = q12_f if which == "q" else k12_f
                goff = 0 if which == "q" else 4
                for r in range(2):
                    def evacT(vector, view, r=r):
                        vector.wait_ge(VE.h, VE.v)
                        return VE.inc(vector.tensor_copy(rep[r][:], view))
                    emit_transpose(fsrc[:, 128 * i:128 * (i + 1)], idf[:], 128,
                                   pe_wait=(lambda t: (twait(t, vprime_done),
                                                       t.wait_ge(DWS.h, gm_ld))),
                                   evac_fn=evacT, n_out=96,
                                   gmat=gmats[:, 96 * (goff + r):96 * (goff + r + 1)])
                tqk = tp_last_ve()

                @on_ve
                def _(vector, tqk=tqk):
                    twait(vector, tqk)
                    vector.wait_ge(VE.h, VE.v)
                    p1 = VE.inc(vector.tensor_mul(L32[:], rep[0][:], rep[1][:]))
                    vector.wait_ge(VE.h, p1)

                for r in range(2):
                    def evacT(vector, view, r=r):
                        vector.wait_ge(VE.h, VE.v)
                        return VE.inc(vector.tensor_copy(rep[r][:], view))
                    emit_transpose(fsrc[:, 128 * i:128 * (i + 1)], idf[:], 128,
                                   pe_wait=(lambda t: t.wait_ge(DWS.h, gm_ld)),
                                   evac_fn=evacT, n_out=96,
                                   gmat=gmats[:, 96 * (goff + r + 2):96 * (goff + r + 3)])
                tqk2 = tp_last_ve()

                @on_ve
                def _(vector, tqk2=tqk2):
                    twait(vector, tqk2)
                    vector.wait_ge(VE.h, VE.v)
                    p2 = VE.inc(vector.tensor_mul(Lsq[:], rep[0][:], rep[1][:]))
                    vector.wait_ge(VE.h, p2)
                    vector.wait_ge(VE.h, VE.v)
                    p3 = VE.inc(vector.tensor_sub(L32[:], L32[:], Lsq[:]))
                    vector.wait_ge(VE.h, p3)
                    vector.wait_ge(VE.h, VE.v)
                    w3 = VE.inc(vector.tensor_mul(Lsq[:], L32[:], L32[:]))
                    vector.wait_ge(VE.h, w3)
                    vector.wait_ge(VE.h, VE.v)
                    w4 = VE.inc(vector.tensor_add(Lss[:], Lsq[:, 0:96:6], Lsq[:, 1:96:6]))
                    vector.wait_ge(VE.h, w4)
                    for c in range(2, 6):
                        vector.wait_ge(VE.h, VE.v)
                        w5 = VE.inc(vector.tensor_add(Lss[:], Lss[:], Lsq[:, c:96:6]))
                        vector.wait_ge(VE.h, w5)
                    vector.wait_ge(VE.h, VE.v)
                    w6 = VE.inc(vector.tensor_scalar_max(Lss[:], Lss[:], 1e-24))
                    vector.wait_ge(VE.h, w6)
                ve_a = VE.v

                @on_se
                def _(scalar, ve_a=ve_a):
                    scalar.wait_ge(VE.h, ve_a)
                    s1_ = SE.inc(scalar.activation(Lrs[:], Lss[:], AF.Ln,
                                                   scale=2.0 ** 40))
                    scalar.wait_ge(SE.h, s1_)
                    SE.inc(scalar.activation(Lrs[:], Lrs[:], AF.Exp, scale=-0.5))
                se_a = SE.v

                @on_ve
                def _(vector, se_a=se_a, which=which):
                    vector.wait_ge(SE.h, se_a)
                    vector.wait_ge(VE.h, VE.v)
                    u_ = VE.inc(vector.tensor_scalar_mul(Lrs[:], Lrs[:], 2.0 ** 20))
                    vector.wait_ge(VE.h, u_)
                    for _ in range(1):
                        vector.wait_ge(VE.h, VE.v)
                        a2 = VE.inc(vector.tensor_mul(Lt1[:], Lrs[:], Lrs[:]))
                        vector.wait_ge(VE.h, a2)
                        vector.wait_ge(VE.h, VE.v)
                        b2 = VE.inc(vector.tensor_mul(Lt1[:], Lt1[:], Lss[:]))
                        vector.wait_ge(VE.h, b2)
                        vector.wait_ge(VE.h, VE.v)
                        c2 = VE.inc(vector.tensor_scalar(Lt1[:], Lt1[:], -0.5, 1.5,
                                                         ALU.mult, ALU.add))
                        vector.wait_ge(VE.h, c2)
                        vector.wait_ge(VE.h, VE.v)
                        d2 = VE.inc(vector.tensor_mul(Lrs[:], Lrs[:], Lt1[:]))
                        vector.wait_ge(VE.h, d2)
                    if which == "q":
                        vector.wait_ge(VE.h, VE.v)
                        s2 = VE.inc(vector.tensor_scalar_mul(Lrs[:], Lrs[:], SCALE_A))
                        vector.wait_ge(VE.h, s2)
                    vector.wait_ge(VE.h, VE.v)
                    jm = VE.inc(vector.tensor_mul(
                        L32.rearrange("p (h c) -> p h c", c=6),
                        L32.rearrange("p (h c) -> p h c", c=6),
                        Lrs.rearrange("p (h u) -> p h u", u=1)
                        .broadcast_to([128, 16, 6])))
                    vector.wait_ge(VE.h, jm)
                    vector.wait_ge(VE.h, VE.v)
                    wv_ = VE.inc(vector.tensor_copy(Lh_s[:], L32[:]))
                    vector.wait_ge(VE.h, wv_)
                    vector.wait_ge(VE.h, VE.v)
                    VE.inc(vector.tensor_sub(Ll_s[:], L32[:], Lh_s[:]))
                ve_b = VE.v

                blocks = [0, 0, 1, 1] if which == "q" else [0, 1, 0, 1]
                stks = lqstk if which == "q" else jkstk

                @on_ve
                def _(vector, ve_b=ve_b, blocks=blocks, _pg=tp_last_pe()):
                    vector.wait_ge(VE.h, max(ve_b, stk_init))
                    if _pg:
                        vector.wait_ge(PE.h, _pg)
                    dstv = stk_big.rearrange("p (gh s) -> p gh s", s=32)
                    for b_i, half_ in enumerate(blocks):
                        srcT = Lh_s if half_ == 0 else Ll_s
                        vector.wait_ge(VE.h, VE.v)
                        VE.inc(vector.tensor_copy(
                            dstv[:, 0:16, 6 * b_i:6 * b_i + 6],
                            srcT.rearrange("p (h c) -> p h c", c=6)))
                asm_ve = VE.v

                for g in range(6):
                    def evacS(vector, view, stks=stks, g=g, i=i):
                        vector.wait_ge(VE.h, VE.v)
                        return VE.inc(vector.tensor_copy(
                            stks[g][0:96, 128 * i:128 * (i + 1)], view))
                    emit_transpose(stk_big[:, 96 * g:96 * (g + 1)], idb[:], 96,
                                   pe_wait=(lambda t, v=asm_ve: t.wait_ge(VE.h, v)),
                                   evac_fn=evacS)

        for _i in range(NT):
            emit_exterior_tile(_i)
        stacks_done = tp_last_ve()
        ext_pe = tp_last_pe()
        sRe.close()
        sRq.close()

        sRb = ExitStack()
        eRb = sRb.enter_context
        o_f = [eRb(nc.sbuf_tensor(f"o_f{c}", [128, BAND], F32, side="right"))
               for c in range(ND)]
        rds = eRb(nc.sbuf_tensor("rds", [128, BAND], F32, side="right"))
        os_f = eRb(nc.sbuf_tensor("os_f", [64, BAND], F32, side="right"))
        wsc = eRb(nc.sbuf_tensor("wsc", [128, 128], F32, side="right"))
        bo_t = eRb(nc.sbuf_tensor("bo_t", [128, D], F32, side="right"))
        wof_s = [eRb(nc.sbuf_tensor(f"wof_s{p}", [128, 128], F32, side="right"))
                 for p in range(4)]

        @on_sy
        def _(sync):
            sync.wait_ge(PE.h, ext_pe)
            sync.wait_ge(DWS.h, DWS.v)
            DWS.dma(sync.dma_start(out=bo_t[0:1, :], in_=bo_d))
        bo_row = DWS.v

        @on_pe
        def _(tensor):
            tensor.wait_ge(DWS.h, bo_row)
            for halfq in range(2):
                PE.inc(tensor.matmul(ps_o[:, 1024 + 512 * halfq:1024 + 512 * (halfq + 1)],
                                     triu[0:1, :], bo_t[0:1, 512 * halfq:512 * (halfq + 1)],
                                     start=True, stop=True))
        bo_pe = PE.v

        @on_ve
        def _(vector):
            vector.wait_ge(PE.h, bo_pe)
            vector.wait_ge(VE.h, VE.v)
            VE.inc(vector.tensor_copy(bo_t[:, :], ps_o[:, 1024:2048]))
        bo_ready = VE.v

        switch_slots(SLOTS2)
        o_written = {}
        gitem = [0]
        msk_count_at = {}
        wo_gate = [{}]
        mv_gate = [None]

        for b in range(NB):
            q0 = BAND * b
            for h in range(H):
                g, s_ = h // 3, h % 3
                jmax = 2 * b + 1
                items = [(j, max(0, 128 * j - q0), BAND - max(0, 128 * j - q0))
                         for j in range(jmax + 1)]
                base_n = gitem[0]
                prev_o = o_written.get((b, h - 1) if h > 0 else (b - 1, H - 1))
                _mc = sem_msk.v
                for it, (j, off, N) in enumerate(items):
                    if 128 * j >= q0:
                        _mc += 1
                        msk_count_at[base_n + it] = _mc

                @on_pe
                def _(tensor, b=b, h=h, g=g, s_=s_, q0=q0, items=items,
                      base_n=base_n, prev_o=prev_o):
                    if b == 0 and h == 0:
                        twait(tensor, stacks_done)
                        twait(tensor, vprime_done)
                        tensor.wait_ge(VE.h, bo_ready)

                    def emit_S(it):
                        j, off, N = items[it]
                        n = base_n + it
                        par = n % 2
                        if n >= 2:
                            tensor.wait_ge(sem_exp.h, n - 1)
                        sem_s.inc(tensor.matmul(
                            ps_big[:, 512 * par:512 * par + N],
                            jkstk[g][32 * s_:32 * s_ + 24, 128 * j:128 * (j + 1)],
                            lqstk[g][32 * s_:32 * s_ + 24, q0 + off:q0 + BAND],
                            start=True, stop=True))

                    def emit_PV(it):
                        j, off, N = items[it]
                        n = base_n + it
                        par = n % 2
                        tensor.wait_ge(sem_exp.h, n + 1)
                        if 128 * j >= q0:
                            tensor.wait_ge(sem_msk.h, msk_count_at[n])
                        if it == 0 and prev_o is not None:
                            twait(tensor, prev_o)
                        sem_pv.inc(tensor.matmul(
                            ps_o[0:65, off:BAND],
                            vph[j][:, 65 * h:65 * h + 65],
                            pT[par][:, 0:N],
                            start=(it == 0), stop=(it == len(items) - 1)))

                    emit_S(0)
                    for it in range(1, len(items)):
                        emit_S(it)
                        emit_PV(it - 1)
                    emit_PV(len(items) - 1)
                gitem[0] += len(items)

                @on_se
                def _(scalar, items=items, base_n=base_n):
                    for it, (j, off, N) in enumerate(items):
                        n = base_n + it
                        par = n % 2
                        scalar.wait_ge(sem_s.h, n + 1)
                        if n >= 2:
                            scalar.wait_ge(sem_pv.h, n - 1)
                        sem_exp.inc(scalar.activation(
                            pT[par][:, 0:N], ps_big[:, 512 * par:512 * par + N],
                            AF.Exp))

                @on_ve
                def _(vector, items=items, base_n=base_n, q0=q0):
                    for it, (j, off, N) in enumerate(items):
                        n = base_n + it
                        if 128 * j < q0:
                            continue
                        par = n % 2
                        vector.wait_ge(sem_exp.h, n + 1)
                        vector.wait_ge(VE.h, VE.v)
                        sem_msk.inc(vector.tensor_mul(
                            pT[par][:, 0:128], pT[par][:, 0:128], triu[:]))

                @on_ve
                def _(vector):
                    vector.wait_ge(sem_pv.h, sem_pv.v)
                    vector.wait_ge(VE.h, VE.v)
                    r = VE.inc(vector.reciprocal(rds[64:65, :], ps_o[64:65, 0:BAND]))
                    vector.wait_ge(VE.h, r)
                rd_ready = VE.v

                @on_pe
                def _(tensor, rd_ready=rd_ready):
                    tensor.wait_ge(VE.h, rd_ready)
                    PE.inc(tensor.matmul(ps_o[0:64, 1024:1024 + BAND],
                                         triu[64:65, 64:128], rds[64:65, :],
                                         start=True, stop=True))
                bc_pe = PE.v

                @on_ve
                def _(vector, h=h, bc_pe=bc_pe):
                    vector.wait_ge(PE.h, bc_pe)
                    c = h // 2
                    rb = (h % 2) * 64
                    vector.wait_ge(VE.h, VE.v)
                    a = VE.inc(vector.tensor_copy(rds[0:64, :], ps_o[0:64, 1024:1024 + BAND]))
                    vector.wait_ge(VE.h, a)
                    vector.wait_ge(VE.h, VE.v)
                    m1 = VE.inc(vector.tensor_mul(
                        ev32[0:64, 0:BAND], ps_o[0:64, 0:BAND], rds[0:64, :]))
                    vector.wait_ge(VE.h, m1)
                    vector.wait_ge(VE.h, VE.v)
                    if rb == 0:
                        VE.inc(vector.tensor_scalar_add(
                            o_f[c][0:64, :], ev32[0:64, 0:BAND],
                            bv_hc[:, h:h + 1]))
                    else:
                        VE.inc(vector.tensor_scalar_add(
                            os_f[:], ev32[0:64, 0:BAND],
                            bv_hc[:, h:h + 1]))
                norm_ve = VE.v

                if h % 2 == 1:
                    mvv = ps_big[:, 1664:1792]

                    for hc in range(2):
                        @on_pe
                        def _(tensor, norm_ve=norm_ve, _tv=tp_last_ve(),
                              hc=hc, _mg=mv_gate[0]):
                            tensor.wait_ge(VE.h, norm_ve)
                            twait(tensor, _tv)
                            if _mg:
                                tensor.wait_ge(VE.h, _mg)
                            PE.inc(tensor.matmul(
                                mvv[64:128, 0:128], idf[0:64, 0:64],
                                os_f[:, 128 * hc:128 * (hc + 1)],
                                start=True, stop=True))
                        mv1 = PE.v

                        @on_ve
                        def _(vector, mv1=mv1, h=h, hc=hc):
                            c = h // 2
                            vector.wait_ge(PE.h, mv1)
                            vector.wait_ge(VE.h, VE.v)
                            VE.inc(vector.tensor_copy(
                                o_f[c][64:128, 128 * hc:128 * (hc + 1)],
                                mvv[64:128, 0:128]))
                        mv_gate[0] = VE.v
                o_written[(b, h)] = VE.v

            for f in range(ND):
                for c in range(8):
                    par = c % 4

                    @on_sy
                    def _(sync, f=f, c=c, par=par, b=b):
                        gate = wo_gate[0].get(par)
                        sync.wait_ge(PE.h, ext_pe if gate is None else gate)
                        sync.wait_ge(DWS.h, DWS.v)
                        DWS.dma(sync.dma_start(
                            out=wof_s[par][:],
                            in_=Wo_d[128 * c:128 * (c + 1), 128 * f:128 * (f + 1)]))
                    wo_ld = DWS.v

                    @on_pe
                    def _(tensor, wo_ld=wo_ld, c=c, par=par, b=b):
                        tensor.wait_ge(DWS.h, wo_ld)
                        if c == 0:
                            tensor.wait_ge(VE.h, o_written[(b, H - 1)])
                        PE.inc(tensor.matmul(ps_o[:, 1536:1536 + BAND],
                                             wof_s[par][:], o_f[c][:],
                                             start=(c == 0), stop=(c == 7)))
                    wo_gate[0][par] = PE.v
                wo_pe = PE.v

                @on_ve
                def _(vector, wo_pe=wo_pe):
                    vector.wait_ge(PE.h, wo_pe)
                    vector.wait_ge(VE.h, VE.v)
                    VE.inc(vector.tensor_copy(ev32[:, 0:BAND], ps_o[:, 1536:1536 + BAND]))
                wo_ev = VE.v

                for tloc in range(BAND // 128):
                    i = (BAND // 128) * b + tloc

                    def evacW(vector, view, i=i, f=f):
                        vector.wait_ge(VE.h, VE.v)
                        t1 = VE.inc(vector.tensor_add(
                            wsc[:], view, bo_t[:, 128 * f:128 * (f + 1)]))
                        vector.wait_ge(VE.h, t1)
                        vector.wait_ge(VE.h, VE.v)
                        return VE.inc(vector.tensor_add(
                            xres[i][:, 128 * f:128 * (f + 1)],
                            xres[i][:, 128 * f:128 * (f + 1)], wsc[:]))
                    emit_transpose(ev32[:, 128 * tloc:128 * (tloc + 1)], idf[:], 128,
                                   pe_wait=(lambda t, v=wo_ev: t.wait_ge(VE.h, v)),
                                   evac_fn=evacW)
        sRb.close()
        s1.close()


    attn_done_gates = []

    if n_stages >= 2 or n_stages == -2:
        switch_slots(SLOTS4)
        s2 = ExitStack()
        e2 = s2.enter_context
        wqT = [e2(nc.sbuf_tensor(f"wqT{c}", [128, D], BF16)) for c in range(ND)]
        amax_t2 = e2(nc.sbuf_tensor("amax_t2", [128, 1], F32))
        bf_t = e2(nc.sbuf_tensor("bf_t", [128, D], F32))

        sW = ExitStack()
        eW = sW.enter_context
        wstage = eW(nc.sbuf_tensor("wstage", [128, 8 * 1024], F32, side="right"))
        wqrows = eW(nc.sbuf_tensor("wqrows", [128, 8 * 1024], BF16, side="right"))
        qys = eW(nc.sbuf_tensor("qys", [128, 1024], F32, side="right"))
        qy2s = eW(nc.sbuf_tensor("qy2s", [128, 1024], F32, side="right"))
        qcls = eW(nc.sbuf_tensor("qcls", [128, 1024], F32, side="right"))
        qf8as = eW(nc.sbuf_tensor("qf8as", [128, 1024], F8E4, side="right"))
        qf8bhs = eW(nc.sbuf_tensor("qf8bhs", [128, 1024], F8E4, side="right"))
        qf8bs = eW(nc.sbuf_tensor("qf8bs", [128, 1024], F32, side="right"))
        qmasks = eW(nc.sbuf_tensor("qmasks", [128, 1024], mybir.dt.uint8, side="right"))

        @on_sy
        def _(sync):
            sync.wait_ge(DWS.h, DWS.v)
            sync.wait_ge(VE.h, VE.v)
            for c in range(8):
                DWS.dma(sync.dma_start(out=wstage[:, 1024 * c:1024 * (c + 1)],
                                       in_=Wf_d[128 * c:128 * (c + 1), :]))
        wf_loaded = DWS.v

        @on_ve
        def _(vector):
            vector.wait_ge(DWS.h, wf_loaded)
            vector.wait_ge(VE.h, VE.v)
            VE.inc(vector.reduce_max(amax_acc[:], wstage[:, 0:1024], axis=AX,
                                     apply_absolute_value=True))
            for c in range(1, 8):
                vector.wait_ge(VE.h, VE.v)
                VE.inc(vector.reduce_max(amax_t2[:], wstage[:, 1024 * c:1024 * (c + 1)],
                                         axis=AX, apply_absolute_value=True))
                vector.wait_ge(VE.h, VE.v)
                VE.inc(vector.tensor_max(amax_acc[:], amax_acc[:], amax_t2[:]))
        aw_ve = VE.v

        def evacAW(vector, view):
            vector.wait_ge(VE.h, VE.v)
            return VE.inc(vector.tensor_copy(sc_row[0:1, :], view))
        emit_transpose(amax_acc[:], idf[:], 1,
                       pe_wait=(lambda t: t.wait_ge(VE.h, aw_ve)),
                       evac_fn=evacAW, n_out=128)

        @on_ve
        def _(vector):
            vector.wait_ge(VE.h, VE.v)
            VE.inc(vector.reduce_max(scal[:, 1:2], sc_row[:], axis=AX))
            vector.wait_ge(VE.h, VE.v)
            VE.inc(vector.reciprocal(scal[:, 7:8], scal[:, 1:2]))
            vector.wait_ge(VE.h, VE.v)
            VE.inc(vector.tensor_scalar_mul(scal[:, 2:3], scal[:, 7:8], 448.0))
        sw_ve = VE.v

        @on_pe
        def _(tensor):
            tensor.wait_ge(VE.h, sw_ve)
            PE.inc(tensor.matmul(ps_o[:, 0:1], triu[0:1, :], scal[0:1, 2:3],
                                 start=True, stop=True))
        swb_pe = PE.v

        @on_ve
        def _(vector):
            vector.wait_ge(PE.h, swb_pe)
            vector.wait_ge(VE.h, VE.v)
            VE.inc(vector.tensor_copy(rd1[:], ps_o[:, 0:1]))
        sw_ready = VE.v

        @on_ve
        def _(vector):
            vector.wait_ge(VE.h, sw_ready)
            for c in range(8):
                W_ = wstage[:, 1024 * c:1024 * (c + 1)]
                Q_ = wqrows[:, 1024 * c:1024 * (c + 1)]
                vector.wait_ge(VE.h, VE.v)
                VE.inc(vector.tensor_scalar_mul(qys[:], W_, rd1[:]))
                vector.wait_ge(VE.h, VE.v)
                VE.inc(vector.tensor_scalar_mul(qy2s[:], qys[:], 0.5))
                vector.wait_ge(VE.h, VE.v)
                VE.inc(vector.tensor_scalar(qcls[:], qys[:], 1.0, -1.0, ALU.min, ALU.max))
                vector.wait_ge(VE.h, VE.v)
                VE.inc(vector.tensor_copy(qf8as[:], qcls[:]))
                vector.wait_ge(VE.h, VE.v)
                VE.inc(vector.tensor_copy(qf8bhs[:], qy2s[:]))
                vector.wait_ge(VE.h, VE.v)
                VE.inc(vector.tensor_scalar_mul(qf8bs[:], qf8bhs[:], 2.0))
                vector.wait_ge(VE.h, VE.v)
                VE.inc(vector.tensor_mul(qmasks[:], qys[:], qys[:]))
                vector.wait_ge(VE.h, VE.v)
                VE.inc(vector.tensor_single_scalar(qmasks[:], qmasks[:], 1.0, ALU.is_lt))
                vector.wait_ge(VE.h, VE.v)
                VE.inc(vector.tensor_copy(qcls[:], qf8as[:]))
                vector.wait_ge(VE.h, VE.v)
                VE.inc(vector.select(Q_, qmasks[:], qcls[:], qf8bs[:], add_drain=True))
        wq_quant = VE.v

        for c1 in range(8):
            for c2 in range(8):
                emit_transpose(wqrows[:, 1024 * c1 + 128 * c2:1024 * c1 + 128 * (c2 + 1)],
                               idb[:], 128,
                               pe_wait=(lambda t: t.wait_ge(VE.h, wq_quant)),
                               dst_ap=wqT[c2][:, 128 * c1:128 * (c1 + 1)],
                               se_evac=False)
        wqT_done = tp_last_ve()

        @on_sy
        def _(sync):
            sync.wait_ge(DWS.h, DWS.v)
            DWS.dma(sync.dma_start(out=bf_t[0:1, :], in_=bf_d))
        bfr = DWS.v

        @on_pe
        def _(tensor):
            tensor.wait_ge(DWS.h, bfr)
            twait(tensor, wqT_done)
            for halfq in range(2):
                PE.inc(tensor.matmul(ps_o[:, 1024 + 512 * halfq:1024 + 512 * (halfq + 1)],
                                     triu[0:1, :], bf_t[0:1, 512 * halfq:512 * (halfq + 1)],
                                     start=True, stop=True))
        bf_pe = PE.v

        @on_ve
        def _(vector):
            vector.wait_ge(PE.h, bf_pe)
            vector.wait_ge(VE.h, VE.v)
            VE.inc(vector.tensor_copy(bf_t[:, :], ps_o[:, 1024:2048]))
        bf_ready = VE.v
        sW.close()

        def emit_ffn(cc_idx):
            switch_slots(SLOTS4)
            sF = ExitStack()
            eF = sF.enter_context
            qy = eF(nc.sbuf_tensor(f"qy{cc_idx}", [128, D], F32, side="right"))
            qy2 = [eF(nc.sbuf_tensor(f"qy2{cc_idx}_{p}", [128, D], F32, side="right"))
                   for p in range(2)]
            qcl = [eF(nc.sbuf_tensor(f"qcl{cc_idx}_{p}", [128, D], F32, side="right"))
                   for p in range(2)]
            qf8a = eF(nc.sbuf_tensor(f"qf8a{cc_idx}", [128, D], F8E4, side="right"))
            qf8bh = eF(nc.sbuf_tensor(f"qf8bh{cc_idx}", [128, D], F8E4, side="right"))
            qf8b = [eF(nc.sbuf_tensor(f"qf8b{cc_idx}_{p}", [128, D], F32, side="right"))
                    for p in range(2)]
            qmask = [eF(nc.sbuf_tensor(f"qmask{cc_idx}_{p}", [128, D], mybir.dt.uint8,
                                       side="right")) for p in range(2)]
            negb = eF(nc.sbuf_tensor(f"negb{cc_idx}", [128, 2 * NT], F32, side="right"))
            xq_t = [eF(nc.sbuf_tensor(f"xq_t{cc_idx}_{i}", [128, D], BF16, side="right"))
                    for i in range(NT)]
            xq_f = [eF(nc.sbuf_tensor(f"xq_f{cc_idx}_{c}", [128, T], BF16, side="right"))
                    for c in range(ND)]

            ln_stats(first=(n_stages == -2 and cc_idx == 0))
            rstd_at = VE.v

            @on_ve
            def _(vector):
                vector.wait_ge(VE.h, rstd_at)
                vector.wait_ge(VE.h, VE.v)
                a = VE.inc(vector.tensor_mul(negb[:, 0:NT], mu[:], rstd[:]))
                vector.wait_ge(VE.h, a)
                vector.wait_ge(VE.h, VE.v)
                VE.inc(vector.tensor_scalar_mul(negb[:, 0:NT], negb[:, 0:NT], -1.0))
            negb_ve = VE.v

            red_tok = {}
            for k in range(2 * NT):
                i, sub = k // 2, k % 2
                par = k % 2
                qv = qy[:, 512 * par:512 * (par + 1)]

                @on_se
                def _(scalar, i=i, sub=sub, k=k, qv=qv):
                    if k == 0:
                        scalar.wait_ge(VE.h, negb_ve)
                    if k >= 2:
                        scalar.wait_ge(VE.h, red_tok[k - 2])
                    scalar.wait_ge(SE.h, SE.v)
                    SE.inc(scalar.activation(
                        qv, xres[i][:, 512 * sub:512 * (sub + 1)], AF.Identity,
                        scale=rstd[:, i:i + 1], bias=negb[:, i:i + 1]))
                ap_se = SE.v

                @on_ve
                def _(vector, k=k, ap_se=ap_se, qv=qv):
                    vector.wait_ge(SE.h, ap_se)
                    vector.wait_ge(VE.h, VE.v)
                    if k == 0:
                        r = VE.inc(vector.reduce_max(amax_acc[:], qv, axis=AX,
                                                     apply_absolute_value=True))
                    else:
                        r = VE.inc(vector.reduce_max(amax_t2[:], qv, axis=AX,
                                                     apply_absolute_value=True))
                        vector.wait_ge(VE.h, r)
                        vector.wait_ge(VE.h, VE.v)
                        r = VE.inc(vector.tensor_max(amax_acc[:], amax_acc[:],
                                                     amax_t2[:]))
                red_tok[k] = VE.v
            ax_ve = VE.v

            def evacAX(vector, view):
                vector.wait_ge(VE.h, VE.v)
                return VE.inc(vector.tensor_copy(sc_row[0:1, :], view))
            emit_transpose(amax_acc[:], idf[:], 1,
                           pe_wait=(lambda t: t.wait_ge(VE.h, ax_ve)),
                           evac_fn=evacAX, n_out=128)

            @on_ve
            def _(vector):
                vector.wait_ge(VE.h, VE.v)
                VE.inc(vector.reduce_max(scal[:, 3:4], sc_row[:], axis=AX))
                vector.wait_ge(VE.h, VE.v)
                VE.inc(vector.tensor_scalar_mul(sc_row[:], triu[0:1, :], scal[0:1, 3:4]))
            ccrow_ve = VE.v

            @on_sy
            def _(sync):
                sync.wait_ge(VE.h, ccrow_ve)
                sync.wait_ge(DCC.h, DCC.v)
                DCC.dma(sync.dma_start(out=cc_in[cc_idx].ap(), in_=sc_row[:]))
            ccin_dma = DCC.v

            @on_gp
            def _(gpsimd):
                gpsimd.wait_ge(DCC.h, ccin_dma)
                CC.inc(gpsimd.collective_compute(
                    "AllReduce", ALU.max,
                    replica_groups=[list(range(N_CORES))],
                    ins=[cc_in[cc_idx].ap()],
                    outs=[cc_out[cc_idx].ap()]))
            cc_at = CC.v

            @on_sy
            def _(sync):
                sync.wait_ge(CC.h, cc_at)
                sync.wait_ge(DCC.h, DCC.v)
                DCC.dma(sync.dma_start(out=sc_row[:], in_=cc_out[cc_idx].ap()))
            ccout_dma = DCC.v

            @on_ve
            def _(vector):
                vector.wait_ge(DCC.h, ccout_dma)
                vector.wait_ge(VE.h, VE.v)
                VE.inc(vector.reciprocal(scal[:, 7:8], sc_row[0:1, 0:1]))
                vector.wait_ge(VE.h, VE.v)
                VE.inc(vector.tensor_scalar_mul(scal[:, 4:5], scal[:, 7:8], 448.0))
                vector.wait_ge(VE.h, VE.v)
                VE.inc(vector.tensor_tensor(scal[:, 5:6], sc_row[0:1, 0:1],
                                            scal[:, 1:2], ALU.mult))
                vector.wait_ge(VE.h, VE.v)
                VE.inc(vector.tensor_scalar_mul(scal[:, 6:7], scal[:, 5:6],
                                                1.0 / (448.0 * 448.0)))
            sxrs_ve = VE.v

            @on_pe
            def _(tensor):
                tensor.wait_ge(VE.h, sxrs_ve)
                PE.inc(tensor.matmul(ps_o[:, 0:1], triu[0:1, :], scal[0:1, 4:5],
                                     start=True, stop=True))
                PE.inc(tensor.matmul(ps_o[:, 512:513], triu[0:1, :], scal[0:1, 6:7],
                                     start=True, stop=True))
            sxb_pe = PE.v

            @on_ve
            def _(vector):
                vector.wait_ge(PE.h, sxb_pe)
                vector.wait_ge(VE.h, VE.v)
                VE.inc(vector.tensor_copy(sx_t[:], ps_o[:, 0:1]))
                vector.wait_ge(VE.h, VE.v)
                VE.inc(vector.tensor_copy(rs_t[:], ps_o[:, 512:513]))
                vector.wait_ge(VE.h, VE.v)
                VE.inc(vector.tensor_scalar_mul(nsc[:], rstd[:], sx_t[:]))
                vector.wait_ge(VE.h, VE.v)
                b1 = VE.inc(vector.tensor_scalar_mul(negb[:, NT:2 * NT],
                                                     negb[:, 0:NT], sx_t[:]))
                vector.wait_ge(VE.h, b1)
            base_ve = VE.v

            clamp_tok = {}
            sel_tok = {}
            ep_gate = [None, None]
            for i in range(NT):
                par = i % 2

                @on_se
                def _(scalar, i=i, par=par):
                    scalar.wait_ge(VE.h, base_ve)
                    if i >= 1:
                        scalar.wait_ge(VE.h, clamp_tok[i - 1])
                    if i >= 2:
                        scalar.wait_ge(VE.h, sel_tok[i - 2])
                    scalar.wait_ge(SE.h, SE.v)
                    SE.inc(scalar.activation(qy[:], xres[i][:], AF.Identity,
                                             scale=nsc[:, i:i + 1],
                                             bias=negb[:, NT + i:NT + i + 1]))
                    scalar.wait_ge(SE.h, SE.v)
                    SE.inc(scalar.activation(qf8bh[:], qy[:], AF.Identity,
                                             scale=0.5))
                    scalar.wait_ge(SE.h, SE.v)
                    SE.inc(scalar.activation(qf8b[par][:], qf8bh[:], AF.Identity,
                                             scale=2.0))
                se_b = SE.v

                @on_ve
                def _(vector, i=i, par=par, se_b=se_b):
                    vector.wait_ge(SE.h, se_b)
                    vector.wait_ge(VE.h, VE.v)
                    sq = VE.inc(vector.tensor_mul(qy2[par][:], qy[:], qy[:]))
                    vector.wait_ge(VE.h, sq)
                    vector.wait_ge(VE.h, VE.v)
                    VE.inc(vector.tensor_scalar(qcl[par][:], qy[:], 1.0, -1.0,
                                                ALU.min, ALU.max))
                clamp_tok[i] = VE.v

                @on_se
                def _(scalar, i=i, par=par):
                    scalar.wait_ge(VE.h, clamp_tok[i])
                    scalar.wait_ge(SE.h, SE.v)
                    SE.inc(scalar.activation(qf8a[:], qcl[par][:], AF.Identity))
                    scalar.wait_ge(SE.h, SE.v)
                    SE.inc(scalar.activation(qcl[par][:], qf8a[:], AF.Identity))
                se_d = SE.v

                @on_ve
                def _(vector, i=i, par=par, se_d=se_d):
                    vector.wait_ge(SE.h, se_d)
                    vector.wait_ge(VE.h, VE.v)
                    m = VE.inc(vector.tensor_single_scalar(
                        qmask[par][:], qy2[par][:], 1.0, ALU.is_lt))
                    vector.wait_ge(VE.h, m)
                    vector.wait_ge(VE.h, VE.v)
                    VE.inc(vector.select(xq_t[i][:], qmask[par][:], qcl[par][:],
                                         qf8b[par][:], add_drain=True))
                sel_tok[i] = VE.v

                for c in range(ND):
                    emit_transpose(xq_t[i][:, 128 * c:128 * (c + 1)], idb[:], 128,
                                   pe_wait=(lambda t, v=sel_tok[i]:
                                            t.wait_ge(VE.h, v)),
                                   dst_ap=xq_f[c][:, 128 * i:128 * (i + 1)],
                                   se_evac=False)
                xqf_tok = tp_last_ve()

                o0 = 1024 * par

                @on_pe
                def _(tensor, i=i, par=par, o0=o0, xqf_tok=xqf_tok):
                    twait(tensor, xqf_tok)
                    g = ep_gate[par]
                    if g is not None:
                        tensor.wait_ge(VE.h, g)
                    for c in range(ND):
                        for half in range(2):
                            tensor.matmul(
                                ps_o[:, o0 + 512 * half:o0 + 512 * (half + 1)],
                                xq_f[c][:, 128 * i:128 * (i + 1)],
                                wqT[c][:, 512 * half:512 * (half + 1)],
                                start=(c == 0), stop=(c == ND - 1))
                    PE.inc(tensor.matmul(ps_big[0:1, 256:257], triu[0:1, 0:1],
                                         triu[0:1, 0:1], start=True, stop=True))
                mm_pe = PE.v

                @on_ve
                def _(vector, i=i, o0=o0, mm_pe=mm_pe):
                    vector.wait_ge(PE.h, mm_pe)
                    for half in range(2):
                        vector.wait_ge(VE.h, VE.v)
                        VE.inc(vector.scalar_tensor_tensor(
                            ev32[:], ps_o[:, o0 + 512 * half:o0 + 512 * (half + 1)],
                            rs_t[:], bf_t[:, 512 * half:512 * (half + 1)],
                            ALU.mult, ALU.add))
                        vector.wait_ge(VE.h, VE.v)
                        VE.inc(vector.tensor_add(
                            xres[i][:, 512 * half:512 * (half + 1)],
                            xres[i][:, 512 * half:512 * (half + 1)], ev32[:]))
                ep_gate[par] = VE.v
            sF.close()

        if n_stages != -3:
            emit_ffn(0)

    if n_stages >= 3 or n_stages == -3:
        switch_slots(SLOTS4)
        s3 = ExitStack()
        e3 = s3.enter_context
        ln3f = [e3(nc.sbuf_tensor(f"ln3f{c}", [128, T], F32)) for c in range(ND)]
        wqk_s = e3(nc.sbuf_tensor("wqk_s", [128, 1024], F32))
        hsc32 = e3(nc.sbuf_tensor("hsc32", [128, T], F32))
        hq_sp = e3(nc.sbuf_tensor("hq_sp", [128, T], BF16))
        hk_sp = e3(nc.sbuf_tensor("hk_sp", [128, T], BF16))
        hqd_h = e3(nc.sbuf_tensor("hqd_h", [128, T], BF16))
        hqd_l = e3(nc.sbuf_tensor("hqd_l", [128, T], BF16))
        hkstk = e3(nc.sbuf_tensor("hkstk", [128, T], BF16))
        vbs = [e3(nc.sbuf_tensor(f"vbs{k}", [128, NT * 65], F32)) for k in range(2)]

        ln_stats(first=(n_stages == -3))
        ln3_rstd = VE.v

        for i in range(NT):
            for sub in range(2):
                @on_ve
                def _(vector, i=i, sub=sub, _wp=tp_last_pe()):
                    if i == 0 and sub == 0:
                        vector.wait_ge(VE.h, ln3_rstd)
                    if _wp:
                        vector.wait_ge(PE.h, _wp)
                    vector.wait_ge(VE.h, VE.v)
                    VE.inc(vector.tensor_scalar(
                        ev32[:], xres[i][:, 512 * sub:512 * (sub + 1)],
                        mu[:, i:i + 1], rstd[:, i:i + 1],
                        ALU.subtract, ALU.mult))
                ve3 = VE.v
                for cl in range(4):
                    c = 4 * sub + cl
                    emit_transpose(ev32[:, 128 * cl:128 * (cl + 1)], idf[:], 128,
                                   pe_wait=(lambda t, v=ve3: t.wait_ge(VE.h, v)),
                                   dst_ap=ln3f[c][:, 128 * i:128 * (i + 1)],
                                   se_evac=False)
        ln3_done = tp_last_ve()

        wb_gate = [None]
        qk_ve_gate = [None]
        qk_pe_gate = [None]
        dup_gate = [None]
        gitem_b = [gitem[0] if n_stages >= 1 else 0]
        sem_s_base = [sem_s.v]
        sem_pv_base = [sem_pv.v]
        o_gate_b = [None]
        switch_slots(SLOTS2)
        for p in range(H // 2):
            hA, hB = 2 * p, 2 * p + 1
            @on_sy
            def _(sync, p=p, hA=hA, hB=hB, _pg=qk_pe_gate[0]):
                if _pg is not None:
                    sync.wait_ge(PE.h, _pg)
                if p == 0:
                    twait(sync, ln3_done)
                sync.wait_ge(DWS.h, DWS.v)
                for c in range(8):
                    DWS.dma(sync.dma_start(
                        out=wqk_s[:, 128 * c:128 * c + 64],
                        in_=Wqkv_d[128 * c:128 * (c + 1), 192 * hA + 128:192 * hA + 192]))
                    DWS.dma(sync.dma_start(
                        out=wqk_s[:, 128 * c + 64:128 * (c + 1)],
                        in_=Wqkv_d[128 * c:128 * (c + 1), 192 * hB + 128:192 * hB + 192]))
            wv_l = DWS.v

            @on_ve
            def _(vector, wv_l=wv_l, p=p, _pv=sem_pv.v, _pg=qk_pe_gate[0]):
                vector.wait_ge(DWS.h, wv_l)
                vector.wait_ge(sem_pv.h, _pv)
                if _pg is not None:
                    vector.wait_ge(PE.h, _pg)
                if p == 0:
                    twait(vector, ln3_done)
                vector.wait_ge(VE.h, VE.v)
                VE.inc(vector.memset(vbs[0][:], 1.0))
                vector.wait_ge(VE.h, VE.v)
                VE.inc(vector.memset(vbs[1][:], 1.0))
            wvb_split = VE.v

            @on_pe
            def _(tensor, wvb_split=wvb_split, wv_l=wv_l):
                tensor.wait_ge(VE.h, wvb_split)
                tensor.wait_ge(DWS.h, wv_l)
                for c in range(8):
                    for nq in range(4):
                        n0 = 512 * nq
                        tensor.matmul(ps_o[:, n0:n0 + 512],
                                      wqk_s[:, 128 * c:128 * (c + 1)],
                                      ln3f[c][:, n0:n0 + 512],
                                      start=(c == 0), stop=(c == 7))
                PE.inc(tensor.matmul(ps_big[0:1, 256:257], triu[0:1, 0:1],
                                     triu[0:1, 0:1], start=True, stop=True))
            vb_pe = PE.v
            qk_pe_gate[0] = vb_pe

            for nq in range(4):
                @on_ve
                def _(vector, nq=nq, vb_pe=vb_pe):
                    vector.wait_ge(PE.h, vb_pe)
                    vector.wait_ge(VE.h, VE.v)
                    VE.inc(vector.tensor_copy(ev32[:], ps_o[:, 512 * nq:512 * (nq + 1)]))
                evd = VE.v
                for tloc in range(4):
                    i = 4 * nq + tloc

                    def evacVB(vector, view, i=i):
                        vector.wait_ge(VE.h, VE.v)
                        VE.inc(vector.tensor_copy(
                            vbs[0][:, 65 * i:65 * i + 64], view[:, 0:64]))
                        vector.wait_ge(VE.h, VE.v)
                        return VE.inc(vector.tensor_copy(
                            vbs[1][:, 65 * i:65 * i + 64], view[:, 64:128]))
                    emit_transpose(ev32[:, 128 * tloc:128 * (tloc + 1)], idf[:], 128,
                                   pe_wait=(lambda t, v=evd: t.wait_ge(VE.h, v)),
                                   evac_fn=evacVB)
            vb_done = tp_last_ve()

            for hi, h in enumerate([hA, hB]):
                @on_sy
                def _(sync, h=h, vb_done=vb_done, _pg=qk_pe_gate[0]):
                    twait(sync, vb_done)
                    if _pg is not None:
                        sync.wait_ge(PE.h, _pg)
                    sync.wait_ge(DWS.h, DWS.v)
                    for c in range(8):
                        DWS.dma(sync.dma_start(
                            out=wqk_s[:, 128 * c:128 * (c + 1)],
                            in_=Wqkv_d[128 * c:128 * (c + 1), 192 * h:192 * h + 128]))
                wqk_ld = DWS.v

                @on_pe
                def _(tensor, wqk_ld=wqk_ld, _xv=sem_exp.v, _tv=tp_last_ve()):
                    tensor.wait_ge(DWS.h, wqk_ld)
                    tensor.wait_ge(sem_exp.h, _xv)
                    twait(tensor, _tv)
                    for c in range(8):
                        for nq in range(4):
                            n0 = 512 * nq
                            tensor.matmul(ps_big[:, n0:n0 + 512],
                                          wqk_s[:, 128 * c:128 * (c + 1)],
                                          ln3f[c][:, n0:n0 + 512],
                                          start=(c == 0), stop=(c == 7))
                    PE.inc(tensor.matmul(ps_o[0:1, 256:257], triu[0:1, 0:1],
                                         triu[0:1, 0:1], start=True, stop=True))
                qk_pe = PE.v
                qk_pe_gate[0] = qk_pe

                @on_ve
                def _(vector, qk_pe=qk_pe):
                    vector.wait_ge(PE.h, qk_pe)
                    vector.wait_ge(VE.h, VE.v)
                    VE.inc(vector.tensor_scalar_mul(hsc32[0:64, :], ps_big[0:64, :], 0.125))
                    vector.wait_ge(VE.h, VE.v)
                    VE.inc(vector.tensor_copy(hsc32[64:128, :], ps_big[64:128, :]))
                    vector.wait_ge(VE.h, VE.v)
                    VE.inc(vector.tensor_copy(hq_sp[0:64, :], hsc32[0:64, :]))
                    vector.wait_ge(VE.h, VE.v)
                    VE.inc(vector.tensor_sub(hk_sp[0:64, :], hsc32[0:64, :],
                                             hq_sp[0:64, :]))
                    vector.wait_ge(VE.h, VE.v)
                    VE.inc(vector.tensor_copy(hk_sp[64:128, :], hsc32[64:128, :]))
                    vector.wait_ge(VE.h, VE.v)
                    VE.inc(vector.tensor_sub(hq_sp[64:128, :], hsc32[64:128, :],
                                             hk_sp[64:128, :]))
                split_ve = VE.v

                for (dst, srcs) in [
                        (hqd_h, (hq_sp[0:64, :], hq_sp[0:64, :], idb[0:64, 0:64])),
                        (hqd_l, (hk_sp[0:64, :], hk_sp[0:64, :], idb[0:64, 0:64])),
                        (hkstk, (hk_sp[64:128, :], hq_sp[64:128, :], idb[64:128, 64:128])),
                ]:
                    src_top, src_bot, identd = srcs

                    @on_pe
                    def _(tensor, split_ve=split_ve, src_top=src_top,
                          src_bot=src_bot, identd=identd, _xv=sem_exp.v):
                        tensor.wait_ge(VE.h, max(split_ve, dup_gate[0] or 0))
                        tensor.wait_ge(sem_exp.h, _xv)
                        for nq in range(4):
                            n0 = 512 * nq
                            tensor.matmul(ps_big[0:64, n0:n0 + 512], identd,
                                          src_top[:, n0:n0 + 512],
                                          start=True, stop=True)
                            tensor.matmul(ps_big[64:128, n0:n0 + 512], identd,
                                          src_bot[:, n0:n0 + 512],
                                          start=True, stop=True,
                                          skip_group_check=True)
                        PE.inc(tensor.matmul(ps_o[0:1, 256:257], triu[0:1, 0:1],
                                             triu[0:1, 0:1], start=True, stop=True))
                    dup_pe = PE.v

                    @on_ve
                    def _(vector, dup_pe=dup_pe, dst=dst):
                        vector.wait_ge(PE.h, dup_pe)
                        vector.wait_ge(VE.h, VE.v)
                        VE.inc(vector.tensor_copy(dst[:], ps_big[:]))
                    dup_gate[0] = VE.v
                stk_ready = VE.v

                items_b = [(j, nchunk) for j in range(NT) for nchunk in range(4)]
                base_n = gitem_b[0]
                prev_o = o_gate_b[0]

                @on_pe
                def _(tensor, h=h, hi=hi, items_b=items_b, base_n=base_n,
                      prev_o=prev_o, stk_ready=stk_ready):
                    tensor.wait_ge(VE.h, stk_ready)

                    def emit_S(it):
                        j, nchunk = items_b[it]
                        n = base_n + it
                        par = n % 2
                        if n >= 2:
                            tensor.wait_ge(sem_exp.h, n - 1)
                        tensor.matmul(
                            ps_big[:, 512 * par:512 * (par + 1)],
                            hkstk[:, 128 * j:128 * (j + 1)],
                            hqd_h[:, 512 * nchunk:512 * (nchunk + 1)],
                            start=True, stop=False)
                        sem_s.inc(tensor.matmul(
                            ps_big[:, 512 * par:512 * (par + 1)],
                            hkstk[:, 128 * j:128 * (j + 1)],
                            hqd_l[:, 512 * nchunk:512 * (nchunk + 1)],
                            start=False, stop=True))

                    def emit_PV(it):
                        j, nchunk = items_b[it]
                        n = base_n + it
                        par = n % 2
                        tensor.wait_ge(sem_exp.h, n + 1)
                        if it == 0 and prev_o is not None:
                            twait(tensor, prev_o)
                        sem_pv.inc(tensor.matmul(
                            ps_o[0:65, 512 * nchunk:512 * nchunk + 512],
                            vbs[hi][:, 65 * j:65 * j + 65],
                            pT[par][:, 0:512],
                            start=(j == 0), stop=(j == NT - 1)))

                    emit_S(0)
                    for it in range(1, len(items_b)):
                        emit_S(it)
                        emit_PV(it - 1)
                    emit_PV(len(items_b) - 1)
                gitem_b[0] += len(items_b)

                @on_se
                def _(scalar, items_b=items_b, base_n=base_n):
                    for it in range(len(items_b)):
                        n = base_n + it
                        par = n % 2
                        scalar.wait_ge(sem_s.h, sem_s_base[0] + it + 1)
                        if n >= 2:
                            scalar.wait_ge(sem_pv.h, sem_pv_base[0] + it - 1)
                        sem_exp.inc(scalar.activation(
                            pT[par][:, 0:512], ps_big[:, 512 * par:512 * (par + 1)],
                            AF.Exp))
                sem_s_base[0] += len(items_b)
                sem_pv_base[0] += len(items_b)

                @on_ve
                def _(vector, h=h):
                    vector.wait_ge(sem_pv.h, sem_pv.v)
                    vector.wait_ge(VE.h, VE.v)
                o_pv_done = VE.v
                for qc in range(4):
                    @on_ve
                    def _(vector, qc=qc):
                        vector.wait_ge(sem_pv.h, sem_pv.v)
                        vector.wait_ge(VE.h, VE.v)
                        VE.inc(vector.tensor_copy(
                            ev32[0:65, :], ps_o[0:65, 512 * qc:512 * (qc + 1)]))
                    evo = VE.v
                    for tloc in range(4):
                        i = 4 * qc + tloc

                        def evacOB(vector, view, i=i, h=h):
                            vector.wait_ge(VE.h, VE.v)
                            r = VE.inc(vector.reciprocal(rd1[:], view[:, 64:65]))
                            vector.wait_ge(VE.h, r)
                            return VE.inc(vector.scalar_tensor_tensor(
                                xres[i][:, 64 * h:64 * h + 64], view[:, 0:64],
                                rd1[:], xres[i][:, 64 * h:64 * h + 64],
                                ALU.mult, ALU.add))
                        emit_transpose(ev32[0:65, 128 * tloc:128 * (tloc + 1)],
                                       idf[0:65, 0:65], 128,
                                       pe_wait=(lambda t, v=evo: t.wait_ge(VE.h, v)),
                                       evac_fn=evacOB, n_out=65)
                o_gate_b[0] = tp_last_ve()
            wb_gate[0] = VE.v
        s3.close()

    if n_stages >= 4:
        emit_ffn(1)
    if n_stages >= 2 or n_stages == -2:
        s2.close()

    @on_sy
    def _(sync):
        sync.wait_ge(VE.h, VE.v)
        sync.wait_ge(PE.h, PE.v)
        sync.wait_ge(SE.h, SE.v)
        sync.wait_ge(sem_pv.h, sem_pv.v)
        sync.wait_ge(sem_exp.h, sem_exp.v)
        sync.wait_ge(sem_msk.h, sem_msk.v)
        for i in range(NT):
            DCC.dma(sync.dma_start(out=out_d[128 * i:128 * (i + 1), :],
                                   in_=xres[i][:]))
        sync.wait_ge(DCC.h, DCC.v)

    st.close()
    return nc


def make_in_maps(inputs):
    x = np.ascontiguousarray(np.asarray(inputs["L0_attn_a_x"], dtype=np.float32))
    w = {
        "W1q": np.ascontiguousarray(inputs["L0_attn_a_W1_q"], dtype=np.float32),
        "W2q": np.ascontiguousarray(inputs["L0_attn_a_W2_q"], dtype=np.float32),
        "W1k": np.ascontiguousarray(inputs["L0_attn_a_W1_k"], dtype=np.float32),
        "W2k": np.ascontiguousarray(inputs["L0_attn_a_W2_k"], dtype=np.float32),
        "Wv": np.ascontiguousarray(inputs["L0_attn_a_W_v"], dtype=np.float32),
        "bv": np.ascontiguousarray(np.asarray(inputs["L0_attn_a_b_v"]).reshape(1, D), dtype=np.float32),
        "Wo": np.ascontiguousarray(inputs["L0_attn_a_W_o"], dtype=np.float32),
        "bo": np.ascontiguousarray(np.asarray(inputs["L0_attn_a_b_o"]).reshape(1, D), dtype=np.float32),
        "Wqkv": np.ascontiguousarray(inputs["L0_attn_b_MhWqkv"], dtype=np.float32),
        "Wf": np.ascontiguousarray(inputs["L0_ffn_weight"], dtype=np.float32),
        "bf": np.ascontiguousarray(np.asarray(inputs["L0_ffn_bias"]).reshape(1, D), dtype=np.float32),
    }
    return [dict(w, x=np.ascontiguousarray(x[c])) for c in range(N_CORES)]


def kernel(**inputs):
    nc = build_nc(4)
    in_maps = make_in_maps(inputs)
    res = run_bass_kernel_spmd(nc, in_maps, core_ids=list(range(N_CORES)))
    return np.stack([np.asarray(r["out"]) for r in res.results], axis=0)


# version 19
# speedup vs baseline: 1.1045x; 1.0046x over previous
from contextlib import ExitStack

import numpy as np
import concourse.bass as bass
import concourse.mybir as mybir
from concourse.bass_utils import run_bass_kernel_spmd

F32 = mybir.dt.float32
BF16 = mybir.dt.bfloat16
SIM_WAITS = False
F8E4 = mybir.dt.float8e4
ALU = mybir.AluOpType
AF = mybir.ActivationFunctionType
AX = mybir.AxisListType.X

T = 2048
D = 1024
H = 16
NT = T // 128
ND = D // 128
NB = 8
BAND = T // NB
TH = T // 2
SCALE_A = 6.0 ** (-0.5)
EPS = 1e-5
N_CORES = 8

PAIRS = [(0, 1), (0, 2), (0, 3), (1, 2), (1, 3), (2, 3)]
J6SIGN = [1.0, -1.0, 1.0, 1.0, -1.0, 1.0]
PASSES3 = [(0, 0), (0, 1), (1, 0)]


class Sem:
    def __init__(self, handle):
        self.h = handle
        self.v = 0

    def inc(self, inst, k=1):
        inst.then_inc(self.h, k)
        self.v += k
        return self.v

    def dma(self, inst):
        return self.inc(inst, 16)


def build_nc(n_stages=4):
    nc = bass.Bass("TRN2", target_bir_lowering=False)

    x_in = nc.dram_tensor("x", [T, D], F32, kind="ExternalInput").ap()
    W1q_d = nc.dram_tensor("W1q", [2 * D, 64], F32, kind="ExternalInput").ap()
    W2q_d = nc.dram_tensor("W2q", [2 * D, 64], F32, kind="ExternalInput").ap()
    W1k_d = nc.dram_tensor("W1k", [D, 64], F32, kind="ExternalInput").ap()
    W2k_d = nc.dram_tensor("W2k", [D, 64], F32, kind="ExternalInput").ap()
    Wv_d = nc.dram_tensor("Wv", [D, D], F32, kind="ExternalInput").ap()
    bv_d = nc.dram_tensor("bv", [1, D], F32, kind="ExternalInput").ap()
    Wo_d = nc.dram_tensor("Wo", [D, D], F32, kind="ExternalInput").ap()
    bo_d = nc.dram_tensor("bo", [1, D], F32, kind="ExternalInput").ap()
    Wqkv_d = nc.dram_tensor("Wqkv", [D, 3 * D], F32, kind="ExternalInput").ap()
    Wf_d = nc.dram_tensor("Wf", [D, D], F32, kind="ExternalInput").ap()
    bf_d = nc.dram_tensor("bf", [1, D], F32, kind="ExternalInput").ap()
    out_d = nc.dram_tensor("out", [T, D], F32, kind="ExternalOutput").ap()
    cc_in = [nc.dram_tensor(f"cc_in{i}", [1, 128], F32) for i in range(2)]
    cc_out = [nc.dram_tensor(f"cc_out{i}", [1, 128], F32) for i in range(2)]

    ident_c = nc.inline_tensor(np.eye(128, dtype=np.float32), name="identc").ap()
    _G = np.zeros((128, 8 * 96), dtype=np.float32)
    for h_ in range(16):
        for c_ in range(6):
            a_, b_ = PAIRS[c_]
            _G[4 * h_ + a_, 0 * 96 + 6 * h_ + c_] = 1.0
            _G[64 + 4 * h_ + b_, 1 * 96 + 6 * h_ + c_] = 1.0
            _G[4 * h_ + b_, 2 * 96 + 6 * h_ + c_] = 1.0
            _G[64 + 4 * h_ + a_, 3 * 96 + 6 * h_ + c_] = 1.0
            ap_, bp_ = PAIRS[5 - c_]
            s_ = float(J6SIGN[c_])
            _G[4 * h_ + ap_, 4 * 96 + 6 * h_ + c_] = s_
            _G[64 + 4 * h_ + bp_, 5 * 96 + 6 * h_ + c_] = 1.0
            _G[4 * h_ + bp_, 6 * 96 + 6 * h_ + c_] = s_
            _G[64 + 4 * h_ + ap_, 7 * 96 + 6 * h_ + c_] = 1.0
    gmat_c = nc.inline_tensor(_G, name="gmatc").ap()
    triu_c = nc.inline_tensor(np.triu(np.ones((128, 128), dtype=np.float32)),
                              name="triuc").ap()


    st = ExitStack()
    e = st.enter_context

    xres = [e(nc.sbuf_tensor(f"xres{i}", [128, D], F32)) for i in range(NT)]
    idf = e(nc.sbuf_tensor("idf", [128, 128], F32))
    idb = e(nc.sbuf_tensor("idb", [128, 128], BF16))
    triu = e(nc.sbuf_tensor("triu", [128, 128], F32))
    mu = e(nc.sbuf_tensor("mu", [128, NT], F32))
    var_e = e(nc.sbuf_tensor("var_e", [128, NT], F32))
    rstd = e(nc.sbuf_tensor("rstd", [128, NT], F32))
    nsc = e(nc.sbuf_tensor("nsc", [128, NT], F32))
    bnst = e(nc.sbuf_tensor("bnst", [128, 12], F32))
    bnmv = e(nc.sbuf_tensor("bnmv", [128, 2], F32))
    pT = [e(nc.sbuf_tensor(f"pT{p}", [128, 512], F32)) for p in range(2)]
    ev32 = e(nc.sbuf_tensor("ev32", [128, 512], F32))
    sc_row = e(nc.sbuf_tensor("sc_row", [1, 128], F32))
    scal = e(nc.sbuf_tensor("scal", [1, 8], F32))
    amax_acc = e(nc.sbuf_tensor("amax_acc", [128, 1], F32))
    sx_t = e(nc.sbuf_tensor("sx_t", [128, 1], F32))
    rs_t = e(nc.sbuf_tensor("rs_t", [128, 1], F32))
    rd1 = e(nc.sbuf_tensor("rd1", [128, 1], F32))

    ps_big = e(nc.psum_tensor("ps_big", [128, T], F32))
    ps_o = e(nc.psum_tensor("ps_o", [128, T], F32))
    def _slot(ap):
        return (ap, ap.bitcast(BF16))

    SLOT_B2 = _slot(ps_big[:, 1024:1152])
    SLOT_B3 = _slot(ps_big[:, 1536:1664])
    SLOT_B0 = _slot(ps_big[:, 0:128])
    SLOT_B1 = _slot(ps_big[:, 512:640])
    SLOTS4 = [SLOT_B2, SLOT_B3, SLOT_B0, SLOT_B1]
    SLOTS2 = [SLOT_B2, SLOT_B3]
    SLOTS_A = SLOTS4

    DLS, DWS, PE, VE, SE, CC, DCC = (Sem(e(nc.semaphore(n))) for n in
                                     ["dls", "dws", "pe", "ve", "se", "cc", "dcc"])
    sem_s = Sem(e(nc.semaphore("sem_s")))
    sem_exp = Sem(e(nc.semaphore("sem_exp")))
    sem_msk = Sem(e(nc.semaphore("sem_msk")))
    sem_pv = Sem(e(nc.semaphore("sem_pv")))

    block = e(nc.Block())
    on_sy, on_pe, on_ve, on_se, on_gp = (block.sync, block.tensor, block.vector,
                                         block.scalar, block.gpsimd)

    @on_sy
    def _(sync):
        for i in range(NT):
            DLS.dma(sync.dma_start(out=xres[i][:], in_=x_in[128 * i:128 * (i + 1), :]))
        DWS.dma(sync.dma_start(out=idf[:], in_=ident_c))
        DWS.dma(sync.dma_start(out=triu[:], in_=triu_c))
    x_loaded = DLS.v
    const_loaded = DWS.v

    @on_ve
    def _(vector):
        vector.wait_ge(DWS.h, const_loaded)
        vector.wait_ge(VE.h, VE.v)
        VE.inc(vector.tensor_copy(idb[:], idf[:]))
        vector.wait_ge(VE.h, VE.v)
        m = VE.inc(vector.memset(scal[:], 0.0))
        vector.wait_ge(VE.h, m)
        vector.wait_ge(VE.h, VE.v)
        VE.inc(vector.tensor_scalar_add(scal[:, 0:1], scal[:, 0:1], 448.0))

    def ln_stats(first=False):
        @on_ve
        def _(vector):
            if first:
                vector.wait_ge(DLS.h, x_loaded)
            for i in range(NT):
                vector.wait_ge(VE.h, VE.v)
                a = VE.inc(vector.bn_stats(bnst[:, 0:6], xres[i][:, 0:512]))
                vector.wait_ge(VE.h, VE.v)
                b = VE.inc(vector.bn_stats(bnst[:, 6:12], xres[i][:, 512:1024]))
                vector.wait_ge(VE.h, b)
                vector.wait_ge(VE.h, VE.v)
                c = VE.inc(vector.bn_aggr(bnmv[:], bnst[:]))
                vector.wait_ge(VE.h, c)
                vector.wait_ge(VE.h, VE.v)
                VE.inc(vector.tensor_copy(mu[:, i:i + 1], bnmv[:, 0:1]))
                vector.wait_ge(VE.h, VE.v)
                d = VE.inc(vector.tensor_scalar_add(var_e[:, i:i + 1], bnmv[:, 1:2], EPS))
                vector.wait_ge(VE.h, d)
        ve_at = VE.v

        @on_se
        def _(scalar):
            scalar.wait_ge(VE.h, ve_at)
            s1_ = SE.inc(scalar.activation(rstd[:], var_e[:], AF.Ln))
            scalar.wait_ge(SE.h, s1_)
            SE.inc(scalar.activation(rstd[:], rstd[:], AF.Exp, scale=-0.5))
        se_at = SE.v

        @on_ve
        def _(vector):
            vector.wait_ge(SE.h, se_at)
            for _ in range(2):
                vector.wait_ge(VE.h, VE.v)
                a = VE.inc(vector.tensor_mul(nsc[:], rstd[:], rstd[:]))
                vector.wait_ge(VE.h, a)
                vector.wait_ge(VE.h, VE.v)
                b = VE.inc(vector.tensor_mul(nsc[:], nsc[:], var_e[:]))
                vector.wait_ge(VE.h, b)
                vector.wait_ge(VE.h, VE.v)
                c = VE.inc(vector.tensor_scalar(nsc[:], nsc[:], -0.5, 1.5,
                                                ALU.mult, ALU.add))
                vector.wait_ge(VE.h, c)
                vector.wait_ge(VE.h, VE.v)
                d = VE.inc(vector.tensor_mul(rstd[:], rstd[:], nsc[:]))
                vector.wait_ge(VE.h, d)
        return VE.v

    tp = {"n": 0, "pe": [], "ve": [], "slots": SLOTS_A, "last": {}}

    def twait(eng, tok):
        if isinstance(tok, tuple):
            eng.wait_ge(SE.h if tok[0] else VE.h, tok[1])
        elif tok:
            eng.wait_ge(VE.h, tok)

    def switch_slots(new_slots):
        tp["slots"] = new_slots

    def emit_transpose(src_ap, ident, m_out, pe_wait=None, evac_fn=None,
                       dst_ap=None, n_out=128, se_evac=False, gmat=None):
        k = tp["n"]
        slot = tp["slots"][k % len(tp["slots"])]
        sid = id(slot)
        war_tok = tp["last"].get(sid)
        view = (slot[1] if src_ap.dtype == BF16 else slot[0])[0:m_out, 0:n_out]

        @on_pe
        def _(tensor, k=k, src=src_ap, ident=ident, view=view, war_tok=war_tok,
              gmat=gmat):
            twait(tensor, war_tok)
            if pe_wait is not None:
                pe_wait(tensor)
            if gmat is not None:
                tp["pe"].append(PE.inc(tensor.matmul(view, src, gmat,
                                                     start=True, stop=True)))
            else:
                tp["pe"].append(PE.inc(tensor.transpose(view, src, ident)))

        if se_evac:
            @on_se
            def _(scalar, k=k, dst=dst_ap, view=view):
                scalar.wait_ge(PE.h, tp["pe"][k])
                scalar.wait_ge(SE.h, SE.v)
                tok = (True, SE.inc(scalar.activation(dst, view, AF.Copy)))
                tp["ve"].append(tok)
                tp["last"][sid] = tok
        else:
            @on_ve
            def _(vector, k=k, dst=dst_ap, view=view):
                vector.wait_ge(PE.h, tp["pe"][k])
                if evac_fn is not None:
                    tok = (False, evac_fn(vector, view))
                else:
                    tok = (False, VE.inc(vector.tensor_copy(dst, view)))
                tp["ve"].append(tok)
                tp["last"][sid] = tok
        tp["n"] += 1

    def tp_last_ve():
        return tp["ve"][-1] if tp["ve"] else 0

    def tp_last_pe():
        return tp["pe"][-1] if tp["pe"] else 0

    if n_stages >= 1 and n_stages not in (-2, -3):
        s1 = ExitStack()
        e1 = s1.enter_context
        vph = [e1(nc.sbuf_tensor(f"vph{i}", [128, H * 65], F32)) for i in range(NT)]
        bv_hc = e1(nc.sbuf_tensor("bv_hc", [64, H], F32))
        sv_f = e1(nc.sbuf_tensor("sv_f", [128, ND], F32))

        sRq = ExitStack()
        eRq = sRq.enter_context
        q12_f = eRq(nc.sbuf_tensor("q12_f", [128, T], F32, side="right"))
        k12_f = eRq(nc.sbuf_tensor("k12_f", [128, T], F32, side="right"))

        sRw = ExitStack()
        eRw = sRw.enter_context
        wqf = eRw(nc.sbuf_tensor("wqf", [128, 16 * 128], F32, side="right"))
        wkf = eRw(nc.sbuf_tensor("wkf", [128, 8 * 128], F32, side="right"))

        ln1_rstd = ln_stats(first=True)

        @on_sy
        def _(sync):
            sync.wait_ge(DWS.h, DWS.v)
            for c in range(16):
                DWS.dma(sync.dma_start(out=wqf[:, 128 * c:128 * c + 64],
                                       in_=W1q_d[128 * c:128 * (c + 1), :]))
                DWS.dma(sync.dma_start(out=wqf[:, 128 * c + 64:128 * (c + 1)],
                                       in_=W2q_d[128 * c:128 * (c + 1), :]))
            sync.wait_ge(DWS.h, DWS.v)
            for c in range(8):
                DWS.dma(sync.dma_start(out=wkf[:, 128 * c:128 * c + 64],
                                       in_=W1k_d[128 * c:128 * (c + 1), :]))
                DWS.dma(sync.dma_start(out=wkf[:, 128 * c + 64:128 * (c + 1)],
                                       in_=W2k_d[128 * c:128 * (c + 1), :]))
        wqk_loaded = DWS.v

        for h in range(H):
            @on_sy
            def _(sync, h=h):
                if h > 0:
                    sync.wait_ge(PE.h, tp["pe"][-1])
                sync.wait_ge(DWS.h, DWS.v)
                DWS.dma(sync.dma_start(out=sc_row[0:1, 0:64],
                                       in_=bv_d[0:1, 64 * h:64 * h + 64]))
            bvh_dma = DWS.v

            def evacB(vector, view, h=h):
                vector.wait_ge(VE.h, VE.v)
                return VE.inc(vector.tensor_copy(bv_hc[:, h:h + 1], view[:, 0:1]))
            emit_transpose(sc_row[0:1, 0:64], idf[0:1, 0:1], 64,
                           pe_wait=(lambda t, v=bvh_dma: t.wait_ge(DWS.h, v)),
                           evac_fn=evacB, n_out=1)
        bv_ready = tp_last_ve()

        sRl = ExitStack()
        eRl = sRl.enter_context
        lnf = [eRl(nc.sbuf_tensor(f"lnf{c}", [128, TH], F32, side="right"))
               for c in range(ND)]
        wvs = eRl(nc.sbuf_tensor("wvs", [128, 8 * 128], F32, side="right"))

        @on_ve
        def _(vector):
            vector.wait_ge(DLS.h, x_loaded)
            for i in range(NT):
                vector.wait_ge(VE.h, VE.v)
                VE.inc(vector.memset(vph[i][:], 1.0))
        vph_init = VE.v

        wv_gate = [None]
        q12_done_half = []
        for half in range(2):
            t0 = TH * half
            for iloc in range(NT // 2):
                i = (NT // 2) * half + iloc

                if half == 1 and iloc == 0:
                    @on_ve
                    def _(vector, _t0k=tp_last_ve()):
                        twait(vector, _t0k)
                        for c in range(ND):
                            vector.wait_ge(VE.h, VE.v)
                            VE.inc(vector.tensor_copy(sv_f[:, c:c + 1],
                                                      lnf[c][:, TH - 1:TH]))
                        vector.wait_ge(PE.h, half_consumed[0])
                for sub in range(2):
                    @on_ve
                    def _(vector, i=i, half=half, iloc=iloc, sub=sub,
                          _wp=tp_last_pe()):
                        if half == 0 and iloc == 0 and sub == 0:
                            vector.wait_ge(VE.h, ln1_rstd)
                        if _wp:
                            vector.wait_ge(PE.h, _wp)
                        vector.wait_ge(VE.h, VE.v)
                        w = VE.inc(vector.tensor_scalar(
                            ev32[:], xres[i][:, 512 * sub:512 * (sub + 1)],
                            mu[:, i:i + 1], rstd[:, i:i + 1],
                            ALU.subtract, ALU.mult))
                        vector.wait_ge(VE.h, w)
                    ve_split = VE.v
                    for cl in range(4):
                        c = 4 * sub + cl
                        emit_transpose(ev32[:, 128 * cl:128 * (cl + 1)], idf[:], 128,
                                       pe_wait=(lambda t, v=ve_split: t.wait_ge(VE.h, v)),
                                       dst_ap=lnf[c][:, 128 * iloc:128 * (iloc + 1)],
                                       se_evac=False)
            ln_done = tp_last_ve()

            @on_pe
            def _(tensor, half=half, t0=t0, ln_done=ln_done):
                twait(tensor, ln_done)
                tensor.wait_ge(DWS.h, wqk_loaded)

                for c in range(8):
                    for nq in range(2):
                        n0l = 512 * nq
                        n0g = t0 + n0l
                        tensor.matmul(ps_big[:, n0g:n0g + 512],
                                      wqf[:, 128 * c:128 * (c + 1)],
                                      lnf[c][:, n0l:n0l + 512],
                                      start=(c == 0), stop=False)
                for c in range(8):
                    ws = wqf[:, 128 * (c + 8):128 * (c + 9)]
                    xt = lnf[c]
                    for nq in range(2):
                        n0l = 512 * nq
                        n0g = t0 + n0l
                        stopf = (c == 7)
                        if n0g == 0:
                            tensor.matmul(ps_big[:, 1:512], ws,
                                          xt[:, 0:511], start=False, stop=stopf)
                        elif n0l == 0:
                            tensor.matmul(ps_big[:, t0:t0 + 1], ws,
                                          sv_f[:, c:c + 1],
                                          start=False, stop=False)
                            tensor.matmul(ps_big[:, t0 + 1:t0 + 512], ws,
                                          xt[:, 0:511], start=False, stop=stopf)
                        else:
                            tensor.matmul(ps_big[:, n0g:n0g + 512], ws,
                                          xt[:, n0l - 1:n0l + 511],
                                          start=False, stop=stopf)
                PE.inc(tensor.matmul(ps_o[0:1, 1536:1537], triu[0:1, 0:1],
                                     triu[0:1, 0:1], start=True, stop=True))
            q12_pe = PE.v

            @on_ve
            def _(vector, t0=t0, q12_pe=q12_pe):
                vector.wait_ge(PE.h, q12_pe)
                vector.wait_ge(VE.h, VE.v)
                VE.inc(vector.tensor_copy(q12_f[:, t0:t0 + TH], ps_big[:, t0:t0 + TH]))
            q12_ev = VE.v

            @on_pe
            def _(tensor, t0=t0, q12_ev=q12_ev):
                tensor.wait_ge(VE.h, q12_ev)
                for c in range(8):
                    for nq in range(2):
                        n0l = 512 * nq
                        tensor.matmul(ps_big[:, t0 + n0l:t0 + n0l + 512],
                                      wkf[:, 128 * c:128 * (c + 1)],
                                      lnf[c][:, n0l:n0l + 512],
                                      start=(c == 0), stop=(c == 7))
                PE.inc(tensor.matmul(ps_o[0:1, 1536:1537], triu[0:1, 0:1],
                                     triu[0:1, 0:1], start=True, stop=True))
            k12_pe = PE.v

            @on_ve
            def _(vector, t0=t0, k12_pe=k12_pe):
                vector.wait_ge(PE.h, k12_pe)
                vector.wait_ge(VE.h, VE.v)
                VE.inc(vector.tensor_copy(k12_f[:, t0:t0 + TH], ps_big[:, t0:t0 + TH]))
            k12_ev = VE.v

            for p in range(H // 2):
                @on_sy
                def _(sync, p=p, half=half, _wg=wv_gate[0]):
                    if _wg is not None:
                        sync.wait_ge(PE.h, _wg)
                    sync.wait_ge(DWS.h, DWS.v)
                    for c in range(8):
                        DWS.dma(sync.dma_start(
                            out=wvs[:, 128 * c:128 * (c + 1)],
                            in_=Wv_d[128 * c:128 * (c + 1), 128 * p:128 * (p + 1)]))
                wv_loaded = DWS.v

                @on_pe
                def _(tensor, wv_loaded=wv_loaded, p=p, half=half, k12_ev=k12_ev):
                    tensor.wait_ge(DWS.h, wv_loaded)
                    if p == 0:
                        tensor.wait_ge(VE.h, k12_ev)
                    for c in range(8):
                        for nq in range(2):
                            n0l = 512 * nq
                            tensor.matmul(ps_o[:, n0l:n0l + 512],
                                          wvs[:, 128 * c:128 * (c + 1)],
                                          lnf[c][:, n0l:n0l + 512],
                                          start=(c == 0), stop=(c == 7))
                    PE.inc(tensor.matmul(ps_big[0:1, 256:257], triu[0:1, 0:1],
                                         triu[0:1, 0:1], start=True, stop=True))
                vp_pe = PE.v
                wv_gate[0] = vp_pe

                for nq in range(2):
                    @on_ve
                    def _(vector, nq=nq, vp_pe=vp_pe):
                        vector.wait_ge(PE.h, vp_pe)
                        if nq == 0:
                            vector.wait_ge(VE.h, vph_init)
                        vector.wait_ge(VE.h, VE.v)
                        VE.inc(vector.tensor_copy(ev32[:], ps_o[:, 512 * nq:512 * (nq + 1)]))
                    ev_done = VE.v
                    for tloc in range(4):
                        i = (NT // 2) * half + 4 * nq + tloc
                        hA, hB = 2 * p, 2 * p + 1

                        def evacV(vector, view, i=i, hA=hA, hB=hB):
                            vector.wait_ge(VE.h, VE.v)
                            VE.inc(vector.tensor_copy(
                                vph[i][:, 65 * hA:65 * hA + 64], view[:, 0:64]))
                            vector.wait_ge(VE.h, VE.v)
                            return VE.inc(vector.tensor_copy(
                                vph[i][:, 65 * hB:65 * hB + 64], view[:, 64:128]))
                        emit_transpose(ev32[:, 128 * tloc:128 * (tloc + 1)], idf[:], 128,
                                       pe_wait=(lambda t, v=ev_done: t.wait_ge(VE.h, v)),
                                       evac_fn=evacV)
            half_consumed = [PE.v]
        vprime_done = tp_last_ve()

        sRl.close()
        sRw.close()

        sRe = ExitStack()
        eRe = sRe.enter_context
        rep = [eRe(nc.sbuf_tensor(f"rep{r}", [128, 96], F32, side="right"))
               for r in range(2)]
        gmats = eRe(nc.sbuf_tensor("gmats", [128, 8 * 96], F32, side="right"))
        L32 = eRe(nc.sbuf_tensor("L32", [128, 96], F32, side="right"))
        Lsq = eRe(nc.sbuf_tensor("Lsq", [128, 96], F32, side="right"))
        Lss = eRe(nc.sbuf_tensor("Lss", [128, 16], F32, side="right"))
        Lrs = eRe(nc.sbuf_tensor("Lrs", [128, 16], F32, side="right"))
        Lt1 = eRe(nc.sbuf_tensor("Lt1", [128, 16], F32, side="right"))
        Lh_s = eRe(nc.sbuf_tensor("Lh_s", [128, 96], BF16, side="right"))
        Ll_s = eRe(nc.sbuf_tensor("Ll_s", [128, 96], BF16, side="right"))
        stk_big = eRe(nc.sbuf_tensor("stk_big", [128, 576], BF16, side="right"))

        lqstk = [e1(nc.sbuf_tensor(f"lqstk{g}", [128, T], BF16)) for g in range(6)]
        jkstk = [e1(nc.sbuf_tensor(f"jkstk{g}", [128, T], BF16)) for g in range(6)]

        @on_ve
        def _(vector):
            vector.wait_ge(VE.h, VE.v)
            VE.inc(vector.memset(stk_big[:], 0.0))
        stk_init = VE.v

        @on_sy
        def _(sync):
            sync.wait_ge(DWS.h, DWS.v)
            DWS.dma(sync.dma_start(out=gmats[:], in_=gmat_c))
        gm_ld = DWS.v

        def emit_exterior_tile(i):
            for which in ("q", "k"):
                fsrc = q12_f if which == "q" else k12_f
                goff = 0 if which == "q" else 4
                for r in range(2):
                    def evacT(vector, view, r=r):
                        vector.wait_ge(VE.h, VE.v)
                        return VE.inc(vector.tensor_copy(rep[r][:], view))
                    emit_transpose(fsrc[:, 128 * i:128 * (i + 1)], idf[:], 128,
                                   pe_wait=(lambda t: (twait(t, vprime_done),
                                                       t.wait_ge(DWS.h, gm_ld))),
                                   evac_fn=evacT, n_out=96,
                                   gmat=gmats[:, 96 * (goff + r):96 * (goff + r + 1)])
                tqk = tp_last_ve()

                @on_ve
                def _(vector, tqk=tqk):
                    twait(vector, tqk)
                    vector.wait_ge(VE.h, VE.v)
                    p1 = VE.inc(vector.tensor_mul(L32[:], rep[0][:], rep[1][:]))
                    vector.wait_ge(VE.h, p1)

                for r in range(2):
                    def evacT(vector, view, r=r):
                        vector.wait_ge(VE.h, VE.v)
                        return VE.inc(vector.tensor_copy(rep[r][:], view))
                    emit_transpose(fsrc[:, 128 * i:128 * (i + 1)], idf[:], 128,
                                   pe_wait=(lambda t: t.wait_ge(DWS.h, gm_ld)),
                                   evac_fn=evacT, n_out=96,
                                   gmat=gmats[:, 96 * (goff + r + 2):96 * (goff + r + 3)])
                tqk2 = tp_last_ve()

                @on_ve
                def _(vector, tqk2=tqk2):
                    twait(vector, tqk2)
                    vector.wait_ge(VE.h, VE.v)
                    p2 = VE.inc(vector.tensor_mul(Lsq[:], rep[0][:], rep[1][:]))
                    vector.wait_ge(VE.h, p2)
                    vector.wait_ge(VE.h, VE.v)
                    p3 = VE.inc(vector.tensor_sub(L32[:], L32[:], Lsq[:]))
                    vector.wait_ge(VE.h, p3)
                    vector.wait_ge(VE.h, VE.v)
                    w3 = VE.inc(vector.tensor_mul(Lsq[:], L32[:], L32[:]))
                    vector.wait_ge(VE.h, w3)
                    vector.wait_ge(VE.h, VE.v)
                    w4 = VE.inc(vector.tensor_add(Lss[:], Lsq[:, 0:96:6], Lsq[:, 1:96:6]))
                    vector.wait_ge(VE.h, w4)
                    for c in range(2, 6):
                        vector.wait_ge(VE.h, VE.v)
                        w5 = VE.inc(vector.tensor_add(Lss[:], Lss[:], Lsq[:, c:96:6]))
                        vector.wait_ge(VE.h, w5)
                    vector.wait_ge(VE.h, VE.v)
                    w6 = VE.inc(vector.tensor_scalar_max(Lss[:], Lss[:], 1e-24))
                    vector.wait_ge(VE.h, w6)
                ve_a = VE.v

                @on_se
                def _(scalar, ve_a=ve_a):
                    scalar.wait_ge(VE.h, ve_a)
                    s1_ = SE.inc(scalar.activation(Lrs[:], Lss[:], AF.Ln,
                                                   scale=2.0 ** 40))
                    scalar.wait_ge(SE.h, s1_)
                    SE.inc(scalar.activation(Lrs[:], Lrs[:], AF.Exp, scale=-0.5))
                se_a = SE.v

                @on_ve
                def _(vector, se_a=se_a, which=which):
                    vector.wait_ge(SE.h, se_a)
                    vector.wait_ge(VE.h, VE.v)
                    u_ = VE.inc(vector.tensor_scalar_mul(Lrs[:], Lrs[:], 2.0 ** 20))
                    vector.wait_ge(VE.h, u_)
                    for _ in range(1):
                        vector.wait_ge(VE.h, VE.v)
                        a2 = VE.inc(vector.tensor_mul(Lt1[:], Lrs[:], Lrs[:]))
                        vector.wait_ge(VE.h, a2)
                        vector.wait_ge(VE.h, VE.v)
                        b2 = VE.inc(vector.tensor_mul(Lt1[:], Lt1[:], Lss[:]))
                        vector.wait_ge(VE.h, b2)
                        vector.wait_ge(VE.h, VE.v)
                        c2 = VE.inc(vector.tensor_scalar(Lt1[:], Lt1[:], -0.5, 1.5,
                                                         ALU.mult, ALU.add))
                        vector.wait_ge(VE.h, c2)
                        vector.wait_ge(VE.h, VE.v)
                        d2 = VE.inc(vector.tensor_mul(Lrs[:], Lrs[:], Lt1[:]))
                        vector.wait_ge(VE.h, d2)
                    if which == "q":
                        vector.wait_ge(VE.h, VE.v)
                        s2 = VE.inc(vector.tensor_scalar_mul(Lrs[:], Lrs[:], SCALE_A))
                        vector.wait_ge(VE.h, s2)
                    vector.wait_ge(VE.h, VE.v)
                    jm = VE.inc(vector.tensor_mul(
                        L32.rearrange("p (h c) -> p h c", c=6),
                        L32.rearrange("p (h c) -> p h c", c=6),
                        Lrs.rearrange("p (h u) -> p h u", u=1)
                        .broadcast_to([128, 16, 6])))
                    vector.wait_ge(VE.h, jm)
                    vector.wait_ge(VE.h, VE.v)
                    wv_ = VE.inc(vector.tensor_copy(Lh_s[:], L32[:]))
                    vector.wait_ge(VE.h, wv_)
                    vector.wait_ge(VE.h, VE.v)
                    VE.inc(vector.tensor_sub(Ll_s[:], L32[:], Lh_s[:]))
                ve_b = VE.v

                blocks = [0, 0, 1, 1] if which == "q" else [0, 1, 0, 1]
                stks = lqstk if which == "q" else jkstk

                @on_ve
                def _(vector, ve_b=ve_b, blocks=blocks, _pg=tp_last_pe()):
                    vector.wait_ge(VE.h, max(ve_b, stk_init))
                    if _pg:
                        vector.wait_ge(PE.h, _pg)
                    dstv = stk_big.rearrange("p (gh s) -> p gh s", s=32)
                    for b_i, half_ in enumerate(blocks):
                        srcT = Lh_s if half_ == 0 else Ll_s
                        vector.wait_ge(VE.h, VE.v)
                        VE.inc(vector.tensor_copy(
                            dstv[:, 0:16, 6 * b_i:6 * b_i + 6],
                            srcT.rearrange("p (h c) -> p h c", c=6)))
                asm_ve = VE.v

                for g in range(6):
                    def evacS(vector, view, stks=stks, g=g, i=i):
                        vector.wait_ge(VE.h, VE.v)
                        return VE.inc(vector.tensor_copy(
                            stks[g][0:96, 128 * i:128 * (i + 1)], view))
                    emit_transpose(stk_big[:, 96 * g:96 * (g + 1)], idb[:], 96,
                                   pe_wait=(lambda t, v=asm_ve: t.wait_ge(VE.h, v)),
                                   evac_fn=evacS)

        for _i in range(NT):
            emit_exterior_tile(_i)
        stacks_done = tp_last_ve()
        ext_pe = tp_last_pe()
        sRe.close()
        sRq.close()

        sRb = ExitStack()
        eRb = sRb.enter_context
        o_f = [eRb(nc.sbuf_tensor(f"o_f{c}", [128, BAND], F32, side="right"))
               for c in range(ND)]
        rds = eRb(nc.sbuf_tensor("rds", [128, BAND], F32, side="right"))
        os_f = eRb(nc.sbuf_tensor("os_f", [64, BAND], F32, side="right"))
        wsc = eRb(nc.sbuf_tensor("wsc", [128, 128], F32, side="right"))
        bo_t = eRb(nc.sbuf_tensor("bo_t", [128, D], F32, side="right"))
        wof_s = [eRb(nc.sbuf_tensor(f"wof_s{p}", [128, 128], F32, side="right"))
                 for p in range(4)]

        @on_sy
        def _(sync):
            sync.wait_ge(PE.h, ext_pe)
            sync.wait_ge(DWS.h, DWS.v)
            DWS.dma(sync.dma_start(out=bo_t[0:1, :], in_=bo_d))
        bo_row = DWS.v

        @on_pe
        def _(tensor):
            tensor.wait_ge(DWS.h, bo_row)
            for halfq in range(2):
                PE.inc(tensor.matmul(ps_o[:, 1024 + 512 * halfq:1024 + 512 * (halfq + 1)],
                                     triu[0:1, :], bo_t[0:1, 512 * halfq:512 * (halfq + 1)],
                                     start=True, stop=True))
        bo_pe = PE.v

        @on_ve
        def _(vector):
            vector.wait_ge(PE.h, bo_pe)
            vector.wait_ge(VE.h, VE.v)
            VE.inc(vector.tensor_copy(bo_t[:, :], ps_o[:, 1024:2048]))
        bo_ready = VE.v

        switch_slots(SLOTS2)
        o_written = {}
        gitem = [0]
        msk_count_at = {}
        wo_gate = [{}]
        mv_gate = [None]

        for b in range(NB):
            q0 = BAND * b
            for h in range(H):
                g, s_ = h // 3, h % 3
                jmax = 2 * b + 1
                items = [(j, max(0, 128 * j - q0), BAND - max(0, 128 * j - q0))
                         for j in range(jmax + 1)]
                base_n = gitem[0]
                prev_o = o_written.get((b, h - 1) if h > 0 else (b - 1, H - 1))
                _mc = sem_msk.v
                for it, (j, off, N) in enumerate(items):
                    if 128 * j >= q0:
                        _mc += 1
                        msk_count_at[base_n + it] = _mc

                @on_pe
                def _(tensor, b=b, h=h, g=g, s_=s_, q0=q0, items=items,
                      base_n=base_n, prev_o=prev_o):
                    if b == 0 and h == 0:
                        twait(tensor, stacks_done)
                        twait(tensor, vprime_done)
                        tensor.wait_ge(VE.h, bo_ready)

                    def emit_S(it):
                        j, off, N = items[it]
                        n = base_n + it
                        par = n % 2
                        if n >= 2:
                            tensor.wait_ge(sem_exp.h, n - 1)
                        sem_s.inc(tensor.matmul(
                            ps_big[:, 512 * par:512 * par + N],
                            jkstk[g][32 * s_:32 * s_ + 24, 128 * j:128 * (j + 1)],
                            lqstk[g][32 * s_:32 * s_ + 24, q0 + off:q0 + BAND],
                            start=True, stop=True))

                    def emit_PV(it):
                        j, off, N = items[it]
                        n = base_n + it
                        par = n % 2
                        tensor.wait_ge(sem_exp.h, n + 1)
                        if 128 * j >= q0:
                            tensor.wait_ge(sem_msk.h, msk_count_at[n])
                        if it == 0 and prev_o is not None:
                            twait(tensor, prev_o)
                        sem_pv.inc(tensor.matmul(
                            ps_o[0:65, off:BAND],
                            vph[j][:, 65 * h:65 * h + 65],
                            pT[par][:, 0:N],
                            start=(it == 0), stop=(it == len(items) - 1)))

                    emit_S(0)
                    for it in range(1, len(items)):
                        emit_S(it)
                        emit_PV(it - 1)
                    emit_PV(len(items) - 1)
                gitem[0] += len(items)

                @on_se
                def _(scalar, items=items, base_n=base_n):
                    for it, (j, off, N) in enumerate(items):
                        n = base_n + it
                        par = n % 2
                        scalar.wait_ge(sem_s.h, n + 1)
                        if n >= 2:
                            scalar.wait_ge(sem_pv.h, n - 1)
                        sem_exp.inc(scalar.activation(
                            pT[par][:, 0:N], ps_big[:, 512 * par:512 * par + N],
                            AF.Exp))

                @on_ve
                def _(vector, items=items, base_n=base_n, q0=q0):
                    for it, (j, off, N) in enumerate(items):
                        n = base_n + it
                        if 128 * j < q0:
                            continue
                        par = n % 2
                        vector.wait_ge(sem_exp.h, n + 1)
                        vector.wait_ge(VE.h, VE.v)
                        sem_msk.inc(vector.tensor_mul(
                            pT[par][:, 0:128], pT[par][:, 0:128], triu[:]))

                @on_ve
                def _(vector):
                    vector.wait_ge(sem_pv.h, sem_pv.v)
                    vector.wait_ge(VE.h, VE.v)
                    r = VE.inc(vector.reciprocal(rds[64:65, :], ps_o[64:65, 0:BAND]))
                    vector.wait_ge(VE.h, r)
                rd_ready = VE.v

                @on_pe
                def _(tensor, rd_ready=rd_ready):
                    tensor.wait_ge(VE.h, rd_ready)
                    PE.inc(tensor.matmul(ps_o[0:64, 1024:1024 + BAND],
                                         triu[64:65, 64:128], rds[64:65, :],
                                         start=True, stop=True))
                bc_pe = PE.v

                @on_ve
                def _(vector, h=h, bc_pe=bc_pe):
                    vector.wait_ge(PE.h, bc_pe)
                    c = h // 2
                    rb = (h % 2) * 64
                    vector.wait_ge(VE.h, VE.v)
                    a = VE.inc(vector.tensor_copy(rds[0:64, :], ps_o[0:64, 1024:1024 + BAND]))
                    vector.wait_ge(VE.h, a)
                    vector.wait_ge(VE.h, VE.v)
                    m1 = VE.inc(vector.tensor_mul(
                        ev32[0:64, 0:BAND], ps_o[0:64, 0:BAND], rds[0:64, :]))
                    vector.wait_ge(VE.h, m1)
                    vector.wait_ge(VE.h, VE.v)
                    if rb == 0:
                        VE.inc(vector.tensor_scalar_add(
                            o_f[c][0:64, :], ev32[0:64, 0:BAND],
                            bv_hc[:, h:h + 1]))
                    else:
                        VE.inc(vector.tensor_scalar_add(
                            os_f[:], ev32[0:64, 0:BAND],
                            bv_hc[:, h:h + 1]))
                norm_ve = VE.v

                if h % 2 == 1:
                    mvv = ps_big[:, 1664:1792]

                    for hc in range(2):
                        @on_pe
                        def _(tensor, norm_ve=norm_ve, _tv=tp_last_ve(),
                              hc=hc, _mg=mv_gate[0]):
                            tensor.wait_ge(VE.h, norm_ve)
                            twait(tensor, _tv)
                            if _mg:
                                tensor.wait_ge(VE.h, _mg)
                            PE.inc(tensor.matmul(
                                mvv[64:128, 0:128], idf[0:64, 0:64],
                                os_f[:, 128 * hc:128 * (hc + 1)],
                                start=True, stop=True))
                        mv1 = PE.v

                        @on_ve
                        def _(vector, mv1=mv1, h=h, hc=hc):
                            c = h // 2
                            vector.wait_ge(PE.h, mv1)
                            vector.wait_ge(VE.h, VE.v)
                            VE.inc(vector.tensor_copy(
                                o_f[c][64:128, 128 * hc:128 * (hc + 1)],
                                mvv[64:128, 0:128]))
                        mv_gate[0] = VE.v
                o_written[(b, h)] = VE.v

            for f in range(ND):
                for c in range(8):
                    par = c % 4

                    @on_sy
                    def _(sync, f=f, c=c, par=par, b=b):
                        gate = wo_gate[0].get(par)
                        sync.wait_ge(PE.h, ext_pe if gate is None else gate)
                        sync.wait_ge(DWS.h, DWS.v)
                        DWS.dma(sync.dma_start(
                            out=wof_s[par][:],
                            in_=Wo_d[128 * c:128 * (c + 1), 128 * f:128 * (f + 1)]))
                    wo_ld = DWS.v

                    @on_pe
                    def _(tensor, wo_ld=wo_ld, c=c, par=par, b=b):
                        tensor.wait_ge(DWS.h, wo_ld)
                        if c == 0:
                            tensor.wait_ge(VE.h, o_written[(b, H - 1)])
                        PE.inc(tensor.matmul(ps_o[:, 1536:1536 + BAND],
                                             wof_s[par][:], o_f[c][:],
                                             start=(c == 0), stop=(c == 7)))
                    wo_gate[0][par] = PE.v
                wo_pe = PE.v

                @on_ve
                def _(vector, wo_pe=wo_pe):
                    vector.wait_ge(PE.h, wo_pe)
                    vector.wait_ge(VE.h, VE.v)
                    VE.inc(vector.tensor_copy(ev32[:, 0:BAND], ps_o[:, 1536:1536 + BAND]))
                wo_ev = VE.v

                for tloc in range(BAND // 128):
                    i = (BAND // 128) * b + tloc

                    def evacW(vector, view, i=i, f=f):
                        vector.wait_ge(VE.h, VE.v)
                        t1 = VE.inc(vector.tensor_add(
                            wsc[:], view, bo_t[:, 128 * f:128 * (f + 1)]))
                        vector.wait_ge(VE.h, t1)
                        vector.wait_ge(VE.h, VE.v)
                        return VE.inc(vector.tensor_add(
                            xres[i][:, 128 * f:128 * (f + 1)],
                            xres[i][:, 128 * f:128 * (f + 1)], wsc[:]))
                    emit_transpose(ev32[:, 128 * tloc:128 * (tloc + 1)], idf[:], 128,
                                   pe_wait=(lambda t, v=wo_ev: t.wait_ge(VE.h, v)),
                                   evac_fn=evacW)
        sRb.close()
        s1.close()


    attn_done_gates = []

    if n_stages >= 2 or n_stages == -2:
        switch_slots(SLOTS4)
        s2 = ExitStack()
        e2 = s2.enter_context
        wqT = [e2(nc.sbuf_tensor(f"wqT{c}", [128, D], BF16)) for c in range(ND)]
        amax_t2 = e2(nc.sbuf_tensor("amax_t2", [128, 1], F32))
        bf_t = e2(nc.sbuf_tensor("bf_t", [128, D], F32))

        sW = ExitStack()
        eW = sW.enter_context
        wstage = eW(nc.sbuf_tensor("wstage", [128, 8 * 1024], F32, side="right"))
        wqrows = eW(nc.sbuf_tensor("wqrows", [128, 8 * 1024], BF16, side="right"))
        qys = eW(nc.sbuf_tensor("qys", [128, 1024], F32, side="right"))
        qy2s = eW(nc.sbuf_tensor("qy2s", [128, 1024], F32, side="right"))
        qcls = eW(nc.sbuf_tensor("qcls", [128, 1024], F32, side="right"))
        qf8as = eW(nc.sbuf_tensor("qf8as", [128, 1024], F8E4, side="right"))
        qf8bhs = eW(nc.sbuf_tensor("qf8bhs", [128, 1024], F8E4, side="right"))
        qf8bs = eW(nc.sbuf_tensor("qf8bs", [128, 1024], F32, side="right"))
        qmasks = eW(nc.sbuf_tensor("qmasks", [128, 1024], mybir.dt.uint8, side="right"))

        @on_sy
        def _(sync):
            sync.wait_ge(DWS.h, DWS.v)
            sync.wait_ge(VE.h, VE.v)
            for c in range(8):
                DWS.dma(sync.dma_start(out=wstage[:, 1024 * c:1024 * (c + 1)],
                                       in_=Wf_d[128 * c:128 * (c + 1), :]))
        wf_loaded = DWS.v

        @on_ve
        def _(vector):
            vector.wait_ge(DWS.h, wf_loaded)
            vector.wait_ge(VE.h, VE.v)
            VE.inc(vector.reduce_max(amax_acc[:], wstage[:, 0:1024], axis=AX,
                                     apply_absolute_value=True))
            for c in range(1, 8):
                vector.wait_ge(VE.h, VE.v)
                VE.inc(vector.reduce_max(amax_t2[:], wstage[:, 1024 * c:1024 * (c + 1)],
                                         axis=AX, apply_absolute_value=True))
                vector.wait_ge(VE.h, VE.v)
                VE.inc(vector.tensor_max(amax_acc[:], amax_acc[:], amax_t2[:]))
        aw_ve = VE.v

        def evacAW(vector, view):
            vector.wait_ge(VE.h, VE.v)
            return VE.inc(vector.tensor_copy(sc_row[0:1, :], view))
        emit_transpose(amax_acc[:], idf[:], 1,
                       pe_wait=(lambda t: t.wait_ge(VE.h, aw_ve)),
                       evac_fn=evacAW, n_out=128)

        @on_ve
        def _(vector):
            vector.wait_ge(VE.h, VE.v)
            VE.inc(vector.reduce_max(scal[:, 1:2], sc_row[:], axis=AX))
            vector.wait_ge(VE.h, VE.v)
            VE.inc(vector.reciprocal(scal[:, 7:8], scal[:, 1:2]))
            vector.wait_ge(VE.h, VE.v)
            VE.inc(vector.tensor_scalar_mul(scal[:, 2:3], scal[:, 7:8], 448.0))
        sw_ve = VE.v

        @on_pe
        def _(tensor):
            tensor.wait_ge(VE.h, sw_ve)
            PE.inc(tensor.matmul(ps_o[:, 0:1], triu[0:1, :], scal[0:1, 2:3],
                                 start=True, stop=True))
        swb_pe = PE.v

        @on_ve
        def _(vector):
            vector.wait_ge(PE.h, swb_pe)
            vector.wait_ge(VE.h, VE.v)
            VE.inc(vector.tensor_copy(rd1[:], ps_o[:, 0:1]))
        sw_ready = VE.v

        @on_ve
        def _(vector):
            vector.wait_ge(VE.h, sw_ready)
            for c in range(8):
                W_ = wstage[:, 1024 * c:1024 * (c + 1)]
                Q_ = wqrows[:, 1024 * c:1024 * (c + 1)]
                vector.wait_ge(VE.h, VE.v)
                VE.inc(vector.tensor_scalar_mul(qys[:], W_, rd1[:]))
                vector.wait_ge(VE.h, VE.v)
                VE.inc(vector.tensor_scalar_mul(qy2s[:], qys[:], 0.5))
                vector.wait_ge(VE.h, VE.v)
                VE.inc(vector.tensor_scalar(qcls[:], qys[:], 1.0, -1.0, ALU.min, ALU.max))
                vector.wait_ge(VE.h, VE.v)
                VE.inc(vector.tensor_copy(qf8as[:], qcls[:]))
                vector.wait_ge(VE.h, VE.v)
                VE.inc(vector.tensor_copy(qf8bhs[:], qy2s[:]))
                vector.wait_ge(VE.h, VE.v)
                VE.inc(vector.tensor_scalar_mul(qf8bs[:], qf8bhs[:], 2.0))
                vector.wait_ge(VE.h, VE.v)
                VE.inc(vector.tensor_mul(qmasks[:], qys[:], qys[:]))
                vector.wait_ge(VE.h, VE.v)
                VE.inc(vector.tensor_single_scalar(qmasks[:], qmasks[:], 1.0, ALU.is_lt))
                vector.wait_ge(VE.h, VE.v)
                VE.inc(vector.tensor_copy(qcls[:], qf8as[:]))
                vector.wait_ge(VE.h, VE.v)
                VE.inc(vector.select(Q_, qmasks[:], qcls[:], qf8bs[:], add_drain=True))
        wq_quant = VE.v

        for c1 in range(8):
            for c2 in range(8):
                emit_transpose(wqrows[:, 1024 * c1 + 128 * c2:1024 * c1 + 128 * (c2 + 1)],
                               idb[:], 128,
                               pe_wait=(lambda t: t.wait_ge(VE.h, wq_quant)),
                               dst_ap=wqT[c2][:, 128 * c1:128 * (c1 + 1)],
                               se_evac=False)
        wqT_done = tp_last_ve()

        @on_sy
        def _(sync):
            sync.wait_ge(DWS.h, DWS.v)
            DWS.dma(sync.dma_start(out=bf_t[0:1, :], in_=bf_d))
        bfr = DWS.v

        @on_pe
        def _(tensor):
            tensor.wait_ge(DWS.h, bfr)
            twait(tensor, wqT_done)
            for halfq in range(2):
                PE.inc(tensor.matmul(ps_o[:, 1024 + 512 * halfq:1024 + 512 * (halfq + 1)],
                                     triu[0:1, :], bf_t[0:1, 512 * halfq:512 * (halfq + 1)],
                                     start=True, stop=True))
        bf_pe = PE.v

        @on_ve
        def _(vector):
            vector.wait_ge(PE.h, bf_pe)
            vector.wait_ge(VE.h, VE.v)
            VE.inc(vector.tensor_copy(bf_t[:, :], ps_o[:, 1024:2048]))
        bf_ready = VE.v
        sW.close()

        def emit_ffn(cc_idx):
            switch_slots(SLOTS4)
            sF = ExitStack()
            eF = sF.enter_context
            qy = eF(nc.sbuf_tensor(f"qy{cc_idx}", [128, D], F32, side="right"))
            qy2 = [eF(nc.sbuf_tensor(f"qy2{cc_idx}_{p}", [128, D], F32, side="right"))
                   for p in range(2)]
            qcl = [eF(nc.sbuf_tensor(f"qcl{cc_idx}_{p}", [128, D], F32, side="right"))
                   for p in range(2)]
            qf8a = eF(nc.sbuf_tensor(f"qf8a{cc_idx}", [128, D], F8E4, side="right"))
            qf8bh = eF(nc.sbuf_tensor(f"qf8bh{cc_idx}", [128, D], F8E4, side="right"))
            qf8b = [eF(nc.sbuf_tensor(f"qf8b{cc_idx}_{p}", [128, D], F32, side="right"))
                    for p in range(2)]
            qmask = [eF(nc.sbuf_tensor(f"qmask{cc_idx}_{p}", [128, D], mybir.dt.uint8,
                                       side="right")) for p in range(2)]
            negb = eF(nc.sbuf_tensor(f"negb{cc_idx}", [128, 2 * NT], F32, side="right"))
            xq_t = [eF(nc.sbuf_tensor(f"xq_t{cc_idx}_{i}", [128, D], BF16, side="right"))
                    for i in range(NT)]
            xq_f = [eF(nc.sbuf_tensor(f"xq_f{cc_idx}_{c}", [128, T], BF16, side="right"))
                    for c in range(ND)]

            ln_stats(first=(n_stages == -2 and cc_idx == 0))
            rstd_at = VE.v

            @on_ve
            def _(vector):
                vector.wait_ge(VE.h, rstd_at)
                vector.wait_ge(VE.h, VE.v)
                a = VE.inc(vector.tensor_mul(negb[:, 0:NT], mu[:], rstd[:]))
                vector.wait_ge(VE.h, a)
                vector.wait_ge(VE.h, VE.v)
                VE.inc(vector.tensor_scalar_mul(negb[:, 0:NT], negb[:, 0:NT], -1.0))
            negb_ve = VE.v

            red_tok = {}
            for k in range(2 * NT):
                i, sub = k // 2, k % 2
                par = k % 2
                qv = qy[:, 512 * par:512 * (par + 1)]

                @on_se
                def _(scalar, i=i, sub=sub, k=k, qv=qv):
                    if k == 0:
                        scalar.wait_ge(VE.h, negb_ve)
                    if k >= 2:
                        scalar.wait_ge(VE.h, red_tok[k - 2])
                    scalar.wait_ge(SE.h, SE.v)
                    SE.inc(scalar.activation(
                        qv, xres[i][:, 512 * sub:512 * (sub + 1)], AF.Identity,
                        scale=rstd[:, i:i + 1], bias=negb[:, i:i + 1]))
                ap_se = SE.v

                @on_ve
                def _(vector, k=k, ap_se=ap_se, qv=qv):
                    vector.wait_ge(SE.h, ap_se)
                    vector.wait_ge(VE.h, VE.v)
                    if k == 0:
                        r = VE.inc(vector.reduce_max(amax_acc[:], qv, axis=AX,
                                                     apply_absolute_value=True))
                    else:
                        r = VE.inc(vector.reduce_max(amax_t2[:], qv, axis=AX,
                                                     apply_absolute_value=True))
                        vector.wait_ge(VE.h, r)
                        vector.wait_ge(VE.h, VE.v)
                        r = VE.inc(vector.tensor_max(amax_acc[:], amax_acc[:],
                                                     amax_t2[:]))
                red_tok[k] = VE.v
            ax_ve = VE.v

            def evacAX(vector, view):
                vector.wait_ge(VE.h, VE.v)
                return VE.inc(vector.tensor_copy(sc_row[0:1, :], view))
            emit_transpose(amax_acc[:], idf[:], 1,
                           pe_wait=(lambda t: t.wait_ge(VE.h, ax_ve)),
                           evac_fn=evacAX, n_out=128)

            @on_ve
            def _(vector):
                vector.wait_ge(VE.h, VE.v)
                VE.inc(vector.reduce_max(scal[:, 3:4], sc_row[:], axis=AX))
                vector.wait_ge(VE.h, VE.v)
                VE.inc(vector.tensor_scalar_mul(sc_row[:], triu[0:1, :], scal[0:1, 3:4]))
            ccrow_ve = VE.v

            @on_sy
            def _(sync):
                sync.wait_ge(VE.h, ccrow_ve)
                sync.wait_ge(DCC.h, DCC.v)
                DCC.dma(sync.dma_start(out=cc_in[cc_idx].ap(), in_=sc_row[:]))
            ccin_dma = DCC.v

            @on_gp
            def _(gpsimd):
                gpsimd.wait_ge(DCC.h, ccin_dma)
                CC.inc(gpsimd.collective_compute(
                    "AllReduce", ALU.max,
                    replica_groups=[list(range(N_CORES))],
                    ins=[cc_in[cc_idx].ap()],
                    outs=[cc_out[cc_idx].ap()]))
            cc_at = CC.v

            @on_sy
            def _(sync):
                sync.wait_ge(CC.h, cc_at)
                sync.wait_ge(DCC.h, DCC.v)
                DCC.dma(sync.dma_start(out=sc_row[:], in_=cc_out[cc_idx].ap()))
            ccout_dma = DCC.v

            @on_ve
            def _(vector):
                vector.wait_ge(DCC.h, ccout_dma)
                vector.wait_ge(VE.h, VE.v)
                VE.inc(vector.reciprocal(scal[:, 7:8], sc_row[0:1, 0:1]))
                vector.wait_ge(VE.h, VE.v)
                VE.inc(vector.tensor_scalar_mul(scal[:, 4:5], scal[:, 7:8], 448.0))
                vector.wait_ge(VE.h, VE.v)
                VE.inc(vector.tensor_tensor(scal[:, 5:6], sc_row[0:1, 0:1],
                                            scal[:, 1:2], ALU.mult))
                vector.wait_ge(VE.h, VE.v)
                VE.inc(vector.tensor_scalar_mul(scal[:, 6:7], scal[:, 5:6],
                                                1.0 / (448.0 * 448.0)))
            sxrs_ve = VE.v

            @on_pe
            def _(tensor):
                tensor.wait_ge(VE.h, sxrs_ve)
                PE.inc(tensor.matmul(ps_o[:, 0:1], triu[0:1, :], scal[0:1, 4:5],
                                     start=True, stop=True))
                PE.inc(tensor.matmul(ps_o[:, 512:513], triu[0:1, :], scal[0:1, 6:7],
                                     start=True, stop=True))
            sxb_pe = PE.v

            @on_ve
            def _(vector):
                vector.wait_ge(PE.h, sxb_pe)
                vector.wait_ge(VE.h, VE.v)
                VE.inc(vector.tensor_copy(sx_t[:], ps_o[:, 0:1]))
                vector.wait_ge(VE.h, VE.v)
                VE.inc(vector.tensor_copy(rs_t[:], ps_o[:, 512:513]))
                vector.wait_ge(VE.h, VE.v)
                VE.inc(vector.tensor_scalar_mul(nsc[:], rstd[:], sx_t[:]))
                vector.wait_ge(VE.h, VE.v)
                b1 = VE.inc(vector.tensor_scalar_mul(negb[:, NT:2 * NT],
                                                     negb[:, 0:NT], sx_t[:]))
                vector.wait_ge(VE.h, b1)
            base_ve = VE.v

            clamp_tok = {}
            sel_tok = {}
            ep_gate = [None, None]
            for i in range(NT):
                par = i % 2

                @on_se
                def _(scalar, i=i, par=par):
                    scalar.wait_ge(VE.h, base_ve)
                    if i >= 1:
                        scalar.wait_ge(VE.h, clamp_tok[i - 1])
                    if i >= 2:
                        scalar.wait_ge(VE.h, sel_tok[i - 2])
                    scalar.wait_ge(SE.h, SE.v)
                    SE.inc(scalar.activation(qy[:], xres[i][:], AF.Identity,
                                             scale=nsc[:, i:i + 1],
                                             bias=negb[:, NT + i:NT + i + 1]))
                    scalar.wait_ge(SE.h, SE.v)
                    SE.inc(scalar.activation(qf8bh[:], qy[:], AF.Identity,
                                             scale=0.5))
                    scalar.wait_ge(SE.h, SE.v)
                    SE.inc(scalar.activation(qf8b[par][:], qf8bh[:], AF.Identity,
                                             scale=2.0))
                se_b = SE.v

                @on_ve
                def _(vector, i=i, par=par, se_b=se_b):
                    vector.wait_ge(SE.h, se_b)
                    vector.wait_ge(VE.h, VE.v)
                    sq = VE.inc(vector.tensor_mul(qy2[par][:], qy[:], qy[:]))
                    vector.wait_ge(VE.h, sq)
                    vector.wait_ge(VE.h, VE.v)
                    VE.inc(vector.tensor_scalar(qcl[par][:], qy[:], 1.0, -1.0,
                                                ALU.min, ALU.max))
                clamp_tok[i] = VE.v

                @on_se
                def _(scalar, i=i, par=par):
                    scalar.wait_ge(VE.h, clamp_tok[i])
                    scalar.wait_ge(SE.h, SE.v)
                    SE.inc(scalar.activation(qf8a[:], qcl[par][:], AF.Identity))
                    scalar.wait_ge(SE.h, SE.v)
                    SE.inc(scalar.activation(qcl[par][:], qf8a[:], AF.Identity))
                se_d = SE.v

                @on_ve
                def _(vector, i=i, par=par, se_d=se_d):
                    vector.wait_ge(SE.h, se_d)
                    vector.wait_ge(VE.h, VE.v)
                    m = VE.inc(vector.tensor_single_scalar(
                        qmask[par][:], qy2[par][:], 1.0, ALU.is_lt))
                    vector.wait_ge(VE.h, m)
                    vector.wait_ge(VE.h, VE.v)
                    VE.inc(vector.select(xq_t[i][:], qmask[par][:], qcl[par][:],
                                         qf8b[par][:], add_drain=True))
                sel_tok[i] = VE.v

                for c in range(ND):
                    emit_transpose(xq_t[i][:, 128 * c:128 * (c + 1)], idb[:], 128,
                                   pe_wait=(lambda t, v=sel_tok[i]:
                                            t.wait_ge(VE.h, v)),
                                   dst_ap=xq_f[c][:, 128 * i:128 * (i + 1)],
                                   se_evac=False)
                xqf_tok = tp_last_ve()

                o0 = 1024 * par

                @on_pe
                def _(tensor, i=i, par=par, o0=o0, xqf_tok=xqf_tok):
                    twait(tensor, xqf_tok)
                    g = ep_gate[par]
                    if g is not None:
                        tensor.wait_ge(VE.h, g)
                    for c in range(ND):
                        for half in range(2):
                            tensor.matmul(
                                ps_o[:, o0 + 512 * half:o0 + 512 * (half + 1)],
                                xq_f[c][:, 128 * i:128 * (i + 1)],
                                wqT[c][:, 512 * half:512 * (half + 1)],
                                start=(c == 0), stop=(c == ND - 1))
                    PE.inc(tensor.matmul(ps_big[0:1, 256:257], triu[0:1, 0:1],
                                         triu[0:1, 0:1], start=True, stop=True))
                mm_pe = PE.v

                @on_ve
                def _(vector, i=i, o0=o0, mm_pe=mm_pe):
                    vector.wait_ge(PE.h, mm_pe)
                    for half in range(2):
                        vector.wait_ge(VE.h, VE.v)
                        VE.inc(vector.scalar_tensor_tensor(
                            ev32[:], ps_o[:, o0 + 512 * half:o0 + 512 * (half + 1)],
                            rs_t[:], bf_t[:, 512 * half:512 * (half + 1)],
                            ALU.mult, ALU.add))
                        vector.wait_ge(VE.h, VE.v)
                        VE.inc(vector.tensor_add(
                            xres[i][:, 512 * half:512 * (half + 1)],
                            xres[i][:, 512 * half:512 * (half + 1)], ev32[:]))
                ep_gate[par] = VE.v
            sF.close()

        if n_stages != -3:
            emit_ffn(0)

    if n_stages >= 3 or n_stages == -3:
        switch_slots(SLOTS4)
        s3 = ExitStack()
        e3 = s3.enter_context
        ln3f = [e3(nc.sbuf_tensor(f"ln3f{c}", [128, T], F32)) for c in range(ND)]
        wqk_s = e3(nc.sbuf_tensor("wqk_s", [128, 1024], F32))
        hq_sp = e3(nc.sbuf_tensor("hq_sp", [128, T], BF16))
        hk_sp = e3(nc.sbuf_tensor("hk_sp", [128, T], BF16))
        hqd_h = e3(nc.sbuf_tensor("hqd_h", [128, T], BF16))
        hqd_l = e3(nc.sbuf_tensor("hqd_l", [128, T], BF16))
        hkstk = e3(nc.sbuf_tensor("hkstk", [128, T], BF16))
        vbs = [e3(nc.sbuf_tensor(f"vbs{k}", [128, NT * 65], F32)) for k in range(2)]

        ln_stats(first=(n_stages == -3))
        ln3_rstd = VE.v

        for i in range(NT):
            for sub in range(2):
                @on_ve
                def _(vector, i=i, sub=sub, _wp=tp_last_pe()):
                    if i == 0 and sub == 0:
                        vector.wait_ge(VE.h, ln3_rstd)
                    if _wp:
                        vector.wait_ge(PE.h, _wp)
                    vector.wait_ge(VE.h, VE.v)
                    VE.inc(vector.tensor_scalar(
                        ev32[:], xres[i][:, 512 * sub:512 * (sub + 1)],
                        mu[:, i:i + 1], rstd[:, i:i + 1],
                        ALU.subtract, ALU.mult))
                ve3 = VE.v
                for cl in range(4):
                    c = 4 * sub + cl
                    emit_transpose(ev32[:, 128 * cl:128 * (cl + 1)], idf[:], 128,
                                   pe_wait=(lambda t, v=ve3: t.wait_ge(VE.h, v)),
                                   dst_ap=ln3f[c][:, 128 * i:128 * (i + 1)],
                                   se_evac=False)
        ln3_done = tp_last_ve()

        wb_gate = [None]
        qk_ve_gate = [None]
        qk_pe_gate = [None]
        dup_gate = [None]
        gitem_b = [gitem[0] if n_stages >= 1 else 0]
        sem_s_base = [sem_s.v]
        sem_pv_base = [sem_pv.v]
        o_gate_b = [None]
        switch_slots(SLOTS2)
        for p in range(H // 2):
            hA, hB = 2 * p, 2 * p + 1
            @on_sy
            def _(sync, p=p, hA=hA, hB=hB, _pg=qk_pe_gate[0]):
                if _pg is not None:
                    sync.wait_ge(PE.h, _pg)
                if p == 0:
                    twait(sync, ln3_done)
                sync.wait_ge(DWS.h, DWS.v)
                for c in range(8):
                    DWS.dma(sync.dma_start(
                        out=wqk_s[:, 128 * c:128 * c + 64],
                        in_=Wqkv_d[128 * c:128 * (c + 1), 192 * hA + 128:192 * hA + 192]))
                    DWS.dma(sync.dma_start(
                        out=wqk_s[:, 128 * c + 64:128 * (c + 1)],
                        in_=Wqkv_d[128 * c:128 * (c + 1), 192 * hB + 128:192 * hB + 192]))
            wv_l = DWS.v

            @on_ve
            def _(vector, wv_l=wv_l, p=p, _pv=sem_pv.v, _pg=qk_pe_gate[0]):
                vector.wait_ge(DWS.h, wv_l)
                vector.wait_ge(sem_pv.h, _pv)
                if _pg is not None:
                    vector.wait_ge(PE.h, _pg)
                if p == 0:
                    twait(vector, ln3_done)
                vector.wait_ge(VE.h, VE.v)
                VE.inc(vector.memset(vbs[0][:], 1.0))
                vector.wait_ge(VE.h, VE.v)
                VE.inc(vector.memset(vbs[1][:], 1.0))
            wvb_split = VE.v

            @on_pe
            def _(tensor, wvb_split=wvb_split, wv_l=wv_l):
                tensor.wait_ge(VE.h, wvb_split)
                tensor.wait_ge(DWS.h, wv_l)
                for c in range(8):
                    for nq in range(4):
                        n0 = 512 * nq
                        tensor.matmul(ps_o[:, n0:n0 + 512],
                                      wqk_s[:, 128 * c:128 * (c + 1)],
                                      ln3f[c][:, n0:n0 + 512],
                                      start=(c == 0), stop=(c == 7))
                PE.inc(tensor.matmul(ps_big[0:1, 256:257], triu[0:1, 0:1],
                                     triu[0:1, 0:1], start=True, stop=True))
            vb_pe = PE.v
            qk_pe_gate[0] = vb_pe

            for nq in range(4):
                @on_ve
                def _(vector, nq=nq, vb_pe=vb_pe):
                    vector.wait_ge(PE.h, vb_pe)
                    vector.wait_ge(VE.h, VE.v)
                    VE.inc(vector.tensor_copy(ev32[:], ps_o[:, 512 * nq:512 * (nq + 1)]))
                evd = VE.v
                for tloc in range(4):
                    i = 4 * nq + tloc

                    def evacVB(vector, view, i=i):
                        vector.wait_ge(VE.h, VE.v)
                        VE.inc(vector.tensor_copy(
                            vbs[0][:, 65 * i:65 * i + 64], view[:, 0:64]))
                        vector.wait_ge(VE.h, VE.v)
                        return VE.inc(vector.tensor_copy(
                            vbs[1][:, 65 * i:65 * i + 64], view[:, 64:128]))
                    emit_transpose(ev32[:, 128 * tloc:128 * (tloc + 1)], idf[:], 128,
                                   pe_wait=(lambda t, v=evd: t.wait_ge(VE.h, v)),
                                   evac_fn=evacVB)
            vb_done = tp_last_ve()

            for hi, h in enumerate([hA, hB]):
                @on_sy
                def _(sync, h=h, vb_done=vb_done, _pg=qk_pe_gate[0]):
                    twait(sync, vb_done)
                    if _pg is not None:
                        sync.wait_ge(PE.h, _pg)
                    sync.wait_ge(DWS.h, DWS.v)
                    for c in range(8):
                        DWS.dma(sync.dma_start(
                            out=wqk_s[:, 128 * c:128 * (c + 1)],
                            in_=Wqkv_d[128 * c:128 * (c + 1), 192 * h:192 * h + 128]))
                wqk_ld = DWS.v

                @on_pe
                def _(tensor, wqk_ld=wqk_ld, _xv=sem_exp.v, _tv=tp_last_ve()):
                    tensor.wait_ge(DWS.h, wqk_ld)
                    tensor.wait_ge(sem_exp.h, _xv)
                    twait(tensor, _tv)
                    for c in range(8):
                        for nq in range(4):
                            n0 = 512 * nq
                            tensor.matmul(ps_big[:, n0:n0 + 512],
                                          wqk_s[:, 128 * c:128 * (c + 1)],
                                          ln3f[c][:, n0:n0 + 512],
                                          start=(c == 0), stop=(c == 7))
                    PE.inc(tensor.matmul(ps_o[0:1, 256:257], triu[0:1, 0:1],
                                         triu[0:1, 0:1], start=True, stop=True))
                qk_pe = PE.v
                qk_pe_gate[0] = qk_pe

                @on_ve
                def _(vector, qk_pe=qk_pe):
                    vector.wait_ge(PE.h, qk_pe)
                    vector.wait_ge(VE.h, VE.v)
                    a_ = VE.inc(vector.tensor_scalar_mul(hq_sp[0:64, :],
                                                         ps_big[0:64, :], 0.125))
                    vector.wait_ge(VE.h, a_)
                    vector.wait_ge(VE.h, VE.v)
                    VE.inc(vector.scalar_tensor_tensor(
                        hk_sp[0:64, :], ps_big[0:64, :], 0.125, hq_sp[0:64, :],
                        ALU.mult, ALU.subtract))
                    vector.wait_ge(VE.h, VE.v)
                    b_ = VE.inc(vector.tensor_copy(hk_sp[64:128, :],
                                                   ps_big[64:128, :]))
                    vector.wait_ge(VE.h, b_)
                    vector.wait_ge(VE.h, VE.v)
                    VE.inc(vector.tensor_sub(hq_sp[64:128, :], ps_big[64:128, :],
                                             hk_sp[64:128, :]))
                split_ve = VE.v

                for (dst, srcs) in [
                        (hqd_h, (hq_sp[0:64, :], hq_sp[0:64, :], idb[0:64, 0:64])),
                        (hqd_l, (hk_sp[0:64, :], hk_sp[0:64, :], idb[0:64, 0:64])),
                        (hkstk, (hk_sp[64:128, :], hq_sp[64:128, :], idb[64:128, 64:128])),
                ]:
                    src_top, src_bot, identd = srcs

                    @on_pe
                    def _(tensor, split_ve=split_ve, src_top=src_top,
                          src_bot=src_bot, identd=identd, _xv=sem_exp.v):
                        tensor.wait_ge(VE.h, max(split_ve, dup_gate[0] or 0))
                        tensor.wait_ge(sem_exp.h, _xv)
                        for nq in range(4):
                            n0 = 512 * nq
                            tensor.matmul(ps_big[0:64, n0:n0 + 512], identd,
                                          src_top[:, n0:n0 + 512],
                                          start=True, stop=True)
                            tensor.matmul(ps_big[64:128, n0:n0 + 512], identd,
                                          src_bot[:, n0:n0 + 512],
                                          start=True, stop=True,
                                          skip_group_check=True)
                        PE.inc(tensor.matmul(ps_o[0:1, 256:257], triu[0:1, 0:1],
                                             triu[0:1, 0:1], start=True, stop=True))
                    dup_pe = PE.v

                    @on_ve
                    def _(vector, dup_pe=dup_pe, dst=dst):
                        vector.wait_ge(PE.h, dup_pe)
                        vector.wait_ge(VE.h, VE.v)
                        VE.inc(vector.tensor_copy(dst[:], ps_big[:]))
                    dup_gate[0] = VE.v
                stk_ready = VE.v

                items_b = [(j, nchunk) for j in range(NT) for nchunk in range(4)]
                base_n = gitem_b[0]
                prev_o = o_gate_b[0]

                @on_pe
                def _(tensor, h=h, hi=hi, items_b=items_b, base_n=base_n,
                      prev_o=prev_o, stk_ready=stk_ready):
                    tensor.wait_ge(VE.h, stk_ready)

                    def emit_S(it):
                        j, nchunk = items_b[it]
                        n = base_n + it
                        par = n % 2
                        if n >= 2:
                            tensor.wait_ge(sem_exp.h, n - 1)
                        tensor.matmul(
                            ps_big[:, 512 * par:512 * (par + 1)],
                            hkstk[:, 128 * j:128 * (j + 1)],
                            hqd_h[:, 512 * nchunk:512 * (nchunk + 1)],
                            start=True, stop=False)
                        sem_s.inc(tensor.matmul(
                            ps_big[:, 512 * par:512 * (par + 1)],
                            hkstk[:, 128 * j:128 * (j + 1)],
                            hqd_l[:, 512 * nchunk:512 * (nchunk + 1)],
                            start=False, stop=True))

                    def emit_PV(it):
                        j, nchunk = items_b[it]
                        n = base_n + it
                        par = n % 2
                        tensor.wait_ge(sem_exp.h, n + 1)
                        if it == 0 and prev_o is not None:
                            twait(tensor, prev_o)
                        sem_pv.inc(tensor.matmul(
                            ps_o[0:65, 512 * nchunk:512 * nchunk + 512],
                            vbs[hi][:, 65 * j:65 * j + 65],
                            pT[par][:, 0:512],
                            start=(j == 0), stop=(j == NT - 1)))

                    emit_S(0)
                    for it in range(1, len(items_b)):
                        emit_S(it)
                        emit_PV(it - 1)
                    emit_PV(len(items_b) - 1)
                gitem_b[0] += len(items_b)

                @on_se
                def _(scalar, items_b=items_b, base_n=base_n):
                    for it in range(len(items_b)):
                        n = base_n + it
                        par = n % 2
                        scalar.wait_ge(sem_s.h, sem_s_base[0] + it + 1)
                        if n >= 2:
                            scalar.wait_ge(sem_pv.h, sem_pv_base[0] + it - 1)
                        sem_exp.inc(scalar.activation(
                            pT[par][:, 0:512], ps_big[:, 512 * par:512 * (par + 1)],
                            AF.Exp))
                sem_s_base[0] += len(items_b)
                sem_pv_base[0] += len(items_b)

                @on_ve
                def _(vector, h=h):
                    vector.wait_ge(sem_pv.h, sem_pv.v)
                    vector.wait_ge(VE.h, VE.v)
                o_pv_done = VE.v
                for qc in range(4):
                    @on_ve
                    def _(vector, qc=qc):
                        vector.wait_ge(sem_pv.h, sem_pv.v)
                        vector.wait_ge(VE.h, VE.v)
                        VE.inc(vector.tensor_copy(
                            ev32[0:65, :], ps_o[0:65, 512 * qc:512 * (qc + 1)]))
                    evo = VE.v
                    for tloc in range(4):
                        i = 4 * qc + tloc

                        def evacOB(vector, view, i=i, h=h):
                            vector.wait_ge(VE.h, VE.v)
                            r = VE.inc(vector.reciprocal(rd1[:], view[:, 64:65]))
                            vector.wait_ge(VE.h, r)
                            return VE.inc(vector.scalar_tensor_tensor(
                                xres[i][:, 64 * h:64 * h + 64], view[:, 0:64],
                                rd1[:], xres[i][:, 64 * h:64 * h + 64],
                                ALU.mult, ALU.add))
                        emit_transpose(ev32[0:65, 128 * tloc:128 * (tloc + 1)],
                                       idf[0:65, 0:65], 128,
                                       pe_wait=(lambda t, v=evo: t.wait_ge(VE.h, v)),
                                       evac_fn=evacOB, n_out=65)
                o_gate_b[0] = tp_last_ve()
            wb_gate[0] = VE.v
        s3.close()

    if n_stages >= 4:
        emit_ffn(1)
    if n_stages >= 2 or n_stages == -2:
        s2.close()

    @on_sy
    def _(sync):
        sync.wait_ge(VE.h, VE.v)
        sync.wait_ge(PE.h, PE.v)
        sync.wait_ge(SE.h, SE.v)
        sync.wait_ge(sem_pv.h, sem_pv.v)
        sync.wait_ge(sem_exp.h, sem_exp.v)
        sync.wait_ge(sem_msk.h, sem_msk.v)
        for i in range(NT):
            DCC.dma(sync.dma_start(out=out_d[128 * i:128 * (i + 1), :],
                                   in_=xres[i][:]))
        sync.wait_ge(DCC.h, DCC.v)

    st.close()
    return nc


def make_in_maps(inputs):
    x = np.ascontiguousarray(np.asarray(inputs["L0_attn_a_x"], dtype=np.float32))
    w = {
        "W1q": np.ascontiguousarray(inputs["L0_attn_a_W1_q"], dtype=np.float32),
        "W2q": np.ascontiguousarray(inputs["L0_attn_a_W2_q"], dtype=np.float32),
        "W1k": np.ascontiguousarray(inputs["L0_attn_a_W1_k"], dtype=np.float32),
        "W2k": np.ascontiguousarray(inputs["L0_attn_a_W2_k"], dtype=np.float32),
        "Wv": np.ascontiguousarray(inputs["L0_attn_a_W_v"], dtype=np.float32),
        "bv": np.ascontiguousarray(np.asarray(inputs["L0_attn_a_b_v"]).reshape(1, D), dtype=np.float32),
        "Wo": np.ascontiguousarray(inputs["L0_attn_a_W_o"], dtype=np.float32),
        "bo": np.ascontiguousarray(np.asarray(inputs["L0_attn_a_b_o"]).reshape(1, D), dtype=np.float32),
        "Wqkv": np.ascontiguousarray(inputs["L0_attn_b_MhWqkv"], dtype=np.float32),
        "Wf": np.ascontiguousarray(inputs["L0_ffn_weight"], dtype=np.float32),
        "bf": np.ascontiguousarray(np.asarray(inputs["L0_ffn_bias"]).reshape(1, D), dtype=np.float32),
    }
    return [dict(w, x=np.ascontiguousarray(x[c])) for c in range(N_CORES)]


def kernel(**inputs):
    nc = build_nc(4)
    in_maps = make_in_maps(inputs)
    res = run_bass_kernel_spmd(nc, in_maps, core_ids=list(range(N_CORES)))
    return np.stack([np.asarray(r["out"]) for r in res.results], axis=0)


# version 20
# speedup vs baseline: 1.1102x; 1.0052x over previous
from contextlib import ExitStack

import numpy as np
import concourse.bass as bass
import concourse.mybir as mybir
from concourse.bass_utils import run_bass_kernel_spmd

F32 = mybir.dt.float32
BF16 = mybir.dt.bfloat16
SIM_WAITS = False
F8E4 = mybir.dt.float8e4
ALU = mybir.AluOpType
AF = mybir.ActivationFunctionType
AX = mybir.AxisListType.X

T = 2048
D = 1024
H = 16
NT = T // 128
ND = D // 128
NB = 8
BAND = T // NB
TH = T // 2
SCALE_A = 6.0 ** (-0.5)
EPS = 1e-5
N_CORES = 8

PAIRS = [(0, 1), (0, 2), (0, 3), (1, 2), (1, 3), (2, 3)]
J6SIGN = [1.0, -1.0, 1.0, 1.0, -1.0, 1.0]
PASSES3 = [(0, 0), (0, 1), (1, 0)]


class Sem:
    def __init__(self, handle):
        self.h = handle
        self.v = 0

    def inc(self, inst, k=1):
        inst.then_inc(self.h, k)
        self.v += k
        return self.v

    def dma(self, inst):
        return self.inc(inst, 16)


def build_nc(n_stages=4):
    nc = bass.Bass("TRN2", target_bir_lowering=False)

    x_in = nc.dram_tensor("x", [T, D], F32, kind="ExternalInput").ap()
    W1q_d = nc.dram_tensor("W1q", [2 * D, 64], F32, kind="ExternalInput").ap()
    W2q_d = nc.dram_tensor("W2q", [2 * D, 64], F32, kind="ExternalInput").ap()
    W1k_d = nc.dram_tensor("W1k", [D, 64], F32, kind="ExternalInput").ap()
    W2k_d = nc.dram_tensor("W2k", [D, 64], F32, kind="ExternalInput").ap()
    Wv_d = nc.dram_tensor("Wv", [D, D], F32, kind="ExternalInput").ap()
    bv_d = nc.dram_tensor("bv", [1, D], F32, kind="ExternalInput").ap()
    Wo_d = nc.dram_tensor("Wo", [D, D], F32, kind="ExternalInput").ap()
    bo_d = nc.dram_tensor("bo", [1, D], F32, kind="ExternalInput").ap()
    Wqkv_d = nc.dram_tensor("Wqkv", [D, 3 * D], F32, kind="ExternalInput").ap()
    Wf_d = nc.dram_tensor("Wf", [D, D], F32, kind="ExternalInput").ap()
    bf_d = nc.dram_tensor("bf", [1, D], F32, kind="ExternalInput").ap()
    out_d = nc.dram_tensor("out", [T, D], F32, kind="ExternalOutput").ap()
    cc_in = [nc.dram_tensor(f"cc_in{i}", [1, 128], F32) for i in range(2)]
    cc_out = [nc.dram_tensor(f"cc_out{i}", [1, 128], F32) for i in range(2)]

    ident_c = nc.inline_tensor(np.eye(128, dtype=np.float32), name="identc").ap()
    _G = np.zeros((128, 8 * 96), dtype=np.float32)
    for h_ in range(16):
        for c_ in range(6):
            a_, b_ = PAIRS[c_]
            _G[4 * h_ + a_, 0 * 96 + 6 * h_ + c_] = 1.0
            _G[64 + 4 * h_ + b_, 1 * 96 + 6 * h_ + c_] = 1.0
            _G[4 * h_ + b_, 2 * 96 + 6 * h_ + c_] = 1.0
            _G[64 + 4 * h_ + a_, 3 * 96 + 6 * h_ + c_] = 1.0
            ap_, bp_ = PAIRS[5 - c_]
            s_ = float(J6SIGN[c_])
            _G[4 * h_ + ap_, 4 * 96 + 6 * h_ + c_] = s_
            _G[64 + 4 * h_ + bp_, 5 * 96 + 6 * h_ + c_] = 1.0
            _G[4 * h_ + bp_, 6 * 96 + 6 * h_ + c_] = s_
            _G[64 + 4 * h_ + ap_, 7 * 96 + 6 * h_ + c_] = 1.0
    gmat_c = nc.inline_tensor(_G, name="gmatc").ap()
    triu_c = nc.inline_tensor(np.triu(np.ones((128, 128), dtype=np.float32)),
                              name="triuc").ap()


    st = ExitStack()
    e = st.enter_context

    xres = [e(nc.sbuf_tensor(f"xres{i}", [128, D], F32)) for i in range(NT)]
    idf = e(nc.sbuf_tensor("idf", [128, 128], F32))
    idb = e(nc.sbuf_tensor("idb", [128, 128], BF16))
    triu = e(nc.sbuf_tensor("triu", [128, 128], F32))
    mu = e(nc.sbuf_tensor("mu", [128, NT], F32))
    var_e = e(nc.sbuf_tensor("var_e", [128, NT], F32))
    rstd = e(nc.sbuf_tensor("rstd", [128, NT], F32))
    nsc = e(nc.sbuf_tensor("nsc", [128, NT], F32))
    bnst = e(nc.sbuf_tensor("bnst", [128, 12], F32))
    bnmv = e(nc.sbuf_tensor("bnmv", [128, 2], F32))
    pT = [e(nc.sbuf_tensor(f"pT{p}", [128, 512], F32)) for p in range(2)]
    ev32 = e(nc.sbuf_tensor("ev32", [128, 512], F32))
    sc_row = e(nc.sbuf_tensor("sc_row", [1, 128], F32))
    scal = e(nc.sbuf_tensor("scal", [1, 8], F32))
    amax_acc = e(nc.sbuf_tensor("amax_acc", [128, 1], F32))
    sx_t = e(nc.sbuf_tensor("sx_t", [128, 1], F32))
    rs_t = e(nc.sbuf_tensor("rs_t", [128, 1], F32))
    rd1 = e(nc.sbuf_tensor("rd1", [128, 1], F32))

    ps_big = e(nc.psum_tensor("ps_big", [128, T], F32))
    ps_o = e(nc.psum_tensor("ps_o", [128, T], F32))
    def _slot(ap):
        return (ap, ap.bitcast(BF16))

    SLOT_B2 = _slot(ps_big[:, 1024:1152])
    SLOT_B3 = _slot(ps_big[:, 1536:1664])
    SLOT_B0 = _slot(ps_big[:, 0:128])
    SLOT_B1 = _slot(ps_big[:, 512:640])
    SLOTS4 = [SLOT_B2, SLOT_B3, SLOT_B0, SLOT_B1]
    SLOTS2 = [SLOT_B2, SLOT_B3]
    SLOTS_A = SLOTS4

    DLS, DWS, PE, VE, SE, CC, DCC = (Sem(e(nc.semaphore(n))) for n in
                                     ["dls", "dws", "pe", "ve", "se", "cc", "dcc"])
    sem_s = Sem(e(nc.semaphore("sem_s")))
    sem_exp = Sem(e(nc.semaphore("sem_exp")))
    sem_msk = Sem(e(nc.semaphore("sem_msk")))
    sem_pv = Sem(e(nc.semaphore("sem_pv")))

    block = e(nc.Block())
    on_sy, on_pe, on_ve, on_se, on_gp = (block.sync, block.tensor, block.vector,
                                         block.scalar, block.gpsimd)

    @on_sy
    def _(sync):
        for i in range(NT):
            DLS.dma(sync.dma_start(out=xres[i][:], in_=x_in[128 * i:128 * (i + 1), :]))
        DWS.dma(sync.dma_start(out=idf[:], in_=ident_c))
        DWS.dma(sync.dma_start(out=triu[:], in_=triu_c))
    x_loaded = DLS.v
    const_loaded = DWS.v

    @on_ve
    def _(vector):
        vector.wait_ge(DWS.h, const_loaded)
        vector.wait_ge(VE.h, VE.v)
        VE.inc(vector.tensor_copy(idb[:], idf[:]))
        vector.wait_ge(VE.h, VE.v)
        m = VE.inc(vector.memset(scal[:], 0.0))
        vector.wait_ge(VE.h, m)
        vector.wait_ge(VE.h, VE.v)
        VE.inc(vector.tensor_scalar_add(scal[:, 0:1], scal[:, 0:1], 448.0))

    def ln_stats(first=False):
        @on_ve
        def _(vector):
            if first:
                vector.wait_ge(DLS.h, x_loaded)
            for i in range(NT):
                vector.wait_ge(VE.h, VE.v)
                a = VE.inc(vector.bn_stats(bnst[:, 0:6], xres[i][:, 0:512]))
                vector.wait_ge(VE.h, VE.v)
                b = VE.inc(vector.bn_stats(bnst[:, 6:12], xres[i][:, 512:1024]))
                vector.wait_ge(VE.h, b)
                vector.wait_ge(VE.h, VE.v)
                c = VE.inc(vector.bn_aggr(bnmv[:], bnst[:]))
                vector.wait_ge(VE.h, c)
                vector.wait_ge(VE.h, VE.v)
                VE.inc(vector.tensor_copy(mu[:, i:i + 1], bnmv[:, 0:1]))
                vector.wait_ge(VE.h, VE.v)
                d = VE.inc(vector.tensor_scalar_add(var_e[:, i:i + 1], bnmv[:, 1:2], EPS))
                vector.wait_ge(VE.h, d)
        ve_at = VE.v

        @on_se
        def _(scalar):
            scalar.wait_ge(VE.h, ve_at)
            s1_ = SE.inc(scalar.activation(rstd[:], var_e[:], AF.Ln))
            scalar.wait_ge(SE.h, s1_)
            SE.inc(scalar.activation(rstd[:], rstd[:], AF.Exp, scale=-0.5))
        se_at = SE.v

        @on_ve
        def _(vector):
            vector.wait_ge(SE.h, se_at)
            for _ in range(2):
                vector.wait_ge(VE.h, VE.v)
                a = VE.inc(vector.tensor_mul(nsc[:], rstd[:], rstd[:]))
                vector.wait_ge(VE.h, a)
                vector.wait_ge(VE.h, VE.v)
                b = VE.inc(vector.tensor_mul(nsc[:], nsc[:], var_e[:]))
                vector.wait_ge(VE.h, b)
                vector.wait_ge(VE.h, VE.v)
                c = VE.inc(vector.tensor_scalar(nsc[:], nsc[:], -0.5, 1.5,
                                                ALU.mult, ALU.add))
                vector.wait_ge(VE.h, c)
                vector.wait_ge(VE.h, VE.v)
                d = VE.inc(vector.tensor_mul(rstd[:], rstd[:], nsc[:]))
                vector.wait_ge(VE.h, d)
        return VE.v

    tp = {"n": 0, "pe": [], "ve": [], "slots": SLOTS_A, "last": {}}

    def twait(eng, tok):
        if isinstance(tok, tuple):
            eng.wait_ge(SE.h if tok[0] else VE.h, tok[1])
        elif tok:
            eng.wait_ge(VE.h, tok)

    def switch_slots(new_slots):
        tp["slots"] = new_slots

    def emit_transpose(src_ap, ident, m_out, pe_wait=None, evac_fn=None,
                       dst_ap=None, n_out=128, se_evac=False, gmat=None):
        k = tp["n"]
        slot = tp["slots"][k % len(tp["slots"])]
        sid = id(slot)
        war_tok = tp["last"].get(sid)
        view = (slot[1] if src_ap.dtype == BF16 else slot[0])[0:m_out, 0:n_out]

        @on_pe
        def _(tensor, k=k, src=src_ap, ident=ident, view=view, war_tok=war_tok,
              gmat=gmat):
            twait(tensor, war_tok)
            if pe_wait is not None:
                pe_wait(tensor)
            if gmat is not None:
                tp["pe"].append(PE.inc(tensor.matmul(view, src, gmat,
                                                     start=True, stop=True)))
            else:
                tp["pe"].append(PE.inc(tensor.transpose(view, src, ident)))

        if se_evac:
            @on_se
            def _(scalar, k=k, dst=dst_ap, view=view):
                scalar.wait_ge(PE.h, tp["pe"][k])
                scalar.wait_ge(SE.h, SE.v)
                tok = (True, SE.inc(scalar.activation(dst, view, AF.Copy)))
                tp["ve"].append(tok)
                tp["last"][sid] = tok
        else:
            @on_ve
            def _(vector, k=k, dst=dst_ap, view=view):
                vector.wait_ge(PE.h, tp["pe"][k])
                if evac_fn is not None:
                    tok = (False, evac_fn(vector, view))
                else:
                    tok = (False, VE.inc(vector.tensor_copy(dst, view)))
                tp["ve"].append(tok)
                tp["last"][sid] = tok
        tp["n"] += 1

    def tp_last_ve():
        return tp["ve"][-1] if tp["ve"] else 0

    def tp_last_pe():
        return tp["pe"][-1] if tp["pe"] else 0

    if n_stages >= 1 and n_stages not in (-2, -3):
        s1 = ExitStack()
        e1 = s1.enter_context
        vph = [e1(nc.sbuf_tensor(f"vph{i}", [128, H * 65], F32)) for i in range(NT)]
        bv_hc = e1(nc.sbuf_tensor("bv_hc", [64, H], F32))
        sv_f = e1(nc.sbuf_tensor("sv_f", [128, ND], F32))

        sRq = ExitStack()
        eRq = sRq.enter_context
        q12_f = eRq(nc.sbuf_tensor("q12_f", [128, T], F32, side="right"))
        k12_f = eRq(nc.sbuf_tensor("k12_f", [128, T], F32, side="right"))

        sRw = ExitStack()
        eRw = sRw.enter_context
        wqf = eRw(nc.sbuf_tensor("wqf", [128, 16 * 128], F32, side="right"))
        wkf = eRw(nc.sbuf_tensor("wkf", [128, 8 * 128], F32, side="right"))

        ln1_rstd = ln_stats(first=True)

        @on_sy
        def _(sync):
            sync.wait_ge(DWS.h, DWS.v)
            for c in range(16):
                DWS.dma(sync.dma_start(out=wqf[:, 128 * c:128 * c + 64],
                                       in_=W1q_d[128 * c:128 * (c + 1), :]))
                DWS.dma(sync.dma_start(out=wqf[:, 128 * c + 64:128 * (c + 1)],
                                       in_=W2q_d[128 * c:128 * (c + 1), :]))
            sync.wait_ge(DWS.h, DWS.v)
            for c in range(8):
                DWS.dma(sync.dma_start(out=wkf[:, 128 * c:128 * c + 64],
                                       in_=W1k_d[128 * c:128 * (c + 1), :]))
                DWS.dma(sync.dma_start(out=wkf[:, 128 * c + 64:128 * (c + 1)],
                                       in_=W2k_d[128 * c:128 * (c + 1), :]))
        wqk_loaded = DWS.v

        for h in range(H):
            @on_sy
            def _(sync, h=h):
                if h > 0:
                    sync.wait_ge(PE.h, tp["pe"][-1])
                sync.wait_ge(DWS.h, DWS.v)
                DWS.dma(sync.dma_start(out=sc_row[0:1, 0:64],
                                       in_=bv_d[0:1, 64 * h:64 * h + 64]))
            bvh_dma = DWS.v

            def evacB(vector, view, h=h):
                vector.wait_ge(VE.h, VE.v)
                return VE.inc(vector.tensor_copy(bv_hc[:, h:h + 1], view[:, 0:1]))
            emit_transpose(sc_row[0:1, 0:64], idf[0:1, 0:1], 64,
                           pe_wait=(lambda t, v=bvh_dma: t.wait_ge(DWS.h, v)),
                           evac_fn=evacB, n_out=1)
        bv_ready = tp_last_ve()

        sRl = ExitStack()
        eRl = sRl.enter_context
        lnf = [eRl(nc.sbuf_tensor(f"lnf{c}", [128, TH], F32, side="right"))
               for c in range(ND)]
        wvs = eRl(nc.sbuf_tensor("wvs", [128, 8 * 128], F32, side="right"))

        @on_ve
        def _(vector):
            vector.wait_ge(DLS.h, x_loaded)
            for i in range(NT):
                vector.wait_ge(VE.h, VE.v)
                VE.inc(vector.memset(vph[i][:], 1.0))
        vph_init = VE.v

        wv_gate = [None]
        q12_done_half = []
        for half in range(2):
            t0 = TH * half
            for iloc in range(NT // 2):
                i = (NT // 2) * half + iloc

                if half == 1 and iloc == 0:
                    @on_ve
                    def _(vector, _t0k=tp_last_ve()):
                        twait(vector, _t0k)
                        for c in range(ND):
                            vector.wait_ge(VE.h, VE.v)
                            VE.inc(vector.tensor_copy(sv_f[:, c:c + 1],
                                                      lnf[c][:, TH - 1:TH]))
                        vector.wait_ge(PE.h, half_consumed[0])
                for sub in range(2):
                    @on_ve
                    def _(vector, i=i, half=half, iloc=iloc, sub=sub,
                          _wp=tp_last_pe()):
                        if half == 0 and iloc == 0 and sub == 0:
                            vector.wait_ge(VE.h, ln1_rstd)
                        if _wp:
                            vector.wait_ge(PE.h, _wp)
                        vector.wait_ge(VE.h, VE.v)
                        w = VE.inc(vector.tensor_scalar(
                            ev32[:], xres[i][:, 512 * sub:512 * (sub + 1)],
                            mu[:, i:i + 1], rstd[:, i:i + 1],
                            ALU.subtract, ALU.mult))
                        vector.wait_ge(VE.h, w)
                    ve_split = VE.v
                    for cl in range(4):
                        c = 4 * sub + cl
                        emit_transpose(ev32[:, 128 * cl:128 * (cl + 1)], idf[:], 128,
                                       pe_wait=(lambda t, v=ve_split: t.wait_ge(VE.h, v)),
                                       dst_ap=lnf[c][:, 128 * iloc:128 * (iloc + 1)],
                                       se_evac=False)
            ln_done = tp_last_ve()

            @on_pe
            def _(tensor, half=half, t0=t0, ln_done=ln_done):
                twait(tensor, ln_done)
                tensor.wait_ge(DWS.h, wqk_loaded)

                for c in range(8):
                    for nq in range(2):
                        n0l = 512 * nq
                        n0g = t0 + n0l
                        tensor.matmul(ps_big[:, n0g:n0g + 512],
                                      wqf[:, 128 * c:128 * (c + 1)],
                                      lnf[c][:, n0l:n0l + 512],
                                      start=(c == 0), stop=False)
                for c in range(8):
                    ws = wqf[:, 128 * (c + 8):128 * (c + 9)]
                    xt = lnf[c]
                    for nq in range(2):
                        n0l = 512 * nq
                        n0g = t0 + n0l
                        stopf = (c == 7)
                        if n0g == 0:
                            tensor.matmul(ps_big[:, 1:512], ws,
                                          xt[:, 0:511], start=False, stop=stopf)
                        elif n0l == 0:
                            tensor.matmul(ps_big[:, t0:t0 + 1], ws,
                                          sv_f[:, c:c + 1],
                                          start=False, stop=False)
                            tensor.matmul(ps_big[:, t0 + 1:t0 + 512], ws,
                                          xt[:, 0:511], start=False, stop=stopf)
                        else:
                            tensor.matmul(ps_big[:, n0g:n0g + 512], ws,
                                          xt[:, n0l - 1:n0l + 511],
                                          start=False, stop=stopf)
                PE.inc(tensor.matmul(ps_o[0:1, 1536:1537], triu[0:1, 0:1],
                                     triu[0:1, 0:1], start=True, stop=True))
            q12_pe = PE.v

            @on_ve
            def _(vector, t0=t0, q12_pe=q12_pe):
                vector.wait_ge(PE.h, q12_pe)
                vector.wait_ge(VE.h, VE.v)
                VE.inc(vector.tensor_copy(q12_f[:, t0:t0 + TH], ps_big[:, t0:t0 + TH]))
            q12_ev = VE.v

            @on_pe
            def _(tensor, t0=t0, q12_ev=q12_ev):
                tensor.wait_ge(VE.h, q12_ev)
                for c in range(8):
                    for nq in range(2):
                        n0l = 512 * nq
                        tensor.matmul(ps_big[:, t0 + n0l:t0 + n0l + 512],
                                      wkf[:, 128 * c:128 * (c + 1)],
                                      lnf[c][:, n0l:n0l + 512],
                                      start=(c == 0), stop=(c == 7))
                PE.inc(tensor.matmul(ps_o[0:1, 1536:1537], triu[0:1, 0:1],
                                     triu[0:1, 0:1], start=True, stop=True))
            k12_pe = PE.v

            @on_ve
            def _(vector, t0=t0, k12_pe=k12_pe):
                vector.wait_ge(PE.h, k12_pe)
                vector.wait_ge(VE.h, VE.v)
                VE.inc(vector.tensor_copy(k12_f[:, t0:t0 + TH], ps_big[:, t0:t0 + TH]))
            k12_ev = VE.v

            for p in range(H // 2):
                @on_sy
                def _(sync, p=p, half=half, _wg=wv_gate[0]):
                    if _wg is not None:
                        sync.wait_ge(PE.h, _wg)
                    sync.wait_ge(DWS.h, DWS.v)
                    for c in range(8):
                        DWS.dma(sync.dma_start(
                            out=wvs[:, 128 * c:128 * (c + 1)],
                            in_=Wv_d[128 * c:128 * (c + 1), 128 * p:128 * (p + 1)]))
                wv_loaded = DWS.v

                @on_pe
                def _(tensor, wv_loaded=wv_loaded, p=p, half=half, k12_ev=k12_ev):
                    tensor.wait_ge(DWS.h, wv_loaded)
                    if p == 0:
                        tensor.wait_ge(VE.h, k12_ev)
                    for c in range(8):
                        for nq in range(2):
                            n0l = 512 * nq
                            tensor.matmul(ps_o[:, n0l:n0l + 512],
                                          wvs[:, 128 * c:128 * (c + 1)],
                                          lnf[c][:, n0l:n0l + 512],
                                          start=(c == 0), stop=(c == 7))
                    PE.inc(tensor.matmul(ps_big[0:1, 256:257], triu[0:1, 0:1],
                                         triu[0:1, 0:1], start=True, stop=True))
                vp_pe = PE.v
                wv_gate[0] = vp_pe

                for nq in range(2):
                    @on_ve
                    def _(vector, nq=nq, vp_pe=vp_pe):
                        vector.wait_ge(PE.h, vp_pe)
                        if nq == 0:
                            vector.wait_ge(VE.h, vph_init)
                        vector.wait_ge(VE.h, VE.v)
                        VE.inc(vector.tensor_copy(ev32[:], ps_o[:, 512 * nq:512 * (nq + 1)]))
                    ev_done = VE.v
                    for tloc in range(4):
                        i = (NT // 2) * half + 4 * nq + tloc
                        hA, hB = 2 * p, 2 * p + 1

                        def evacV(vector, view, i=i, hA=hA, hB=hB):
                            vector.wait_ge(VE.h, VE.v)
                            VE.inc(vector.tensor_copy(
                                vph[i][:, 65 * hA:65 * hA + 64], view[:, 0:64]))
                            vector.wait_ge(VE.h, VE.v)
                            return VE.inc(vector.tensor_copy(
                                vph[i][:, 65 * hB:65 * hB + 64], view[:, 64:128]))
                        emit_transpose(ev32[:, 128 * tloc:128 * (tloc + 1)], idf[:], 128,
                                       pe_wait=(lambda t, v=ev_done: t.wait_ge(VE.h, v)),
                                       evac_fn=evacV)
            half_consumed = [PE.v]
        vprime_done = tp_last_ve()

        sRl.close()
        sRw.close()

        sRe = ExitStack()
        eRe = sRe.enter_context
        rep = [eRe(nc.sbuf_tensor(f"rep{r}", [128, 96], F32, side="right"))
               for r in range(2)]
        gmats = eRe(nc.sbuf_tensor("gmats", [128, 8 * 96], F32, side="right"))
        L32 = eRe(nc.sbuf_tensor("L32", [128, 96], F32, side="right"))
        Lsq = eRe(nc.sbuf_tensor("Lsq", [128, 96], F32, side="right"))
        Lss = eRe(nc.sbuf_tensor("Lss", [128, 16], F32, side="right"))
        Lrs = eRe(nc.sbuf_tensor("Lrs", [128, 16], F32, side="right"))
        Lt1 = eRe(nc.sbuf_tensor("Lt1", [128, 16], F32, side="right"))
        Lh_s = eRe(nc.sbuf_tensor("Lh_s", [128, 96], BF16, side="right"))
        Ll_s = eRe(nc.sbuf_tensor("Ll_s", [128, 96], BF16, side="right"))
        stk_big = eRe(nc.sbuf_tensor("stk_big", [128, 576], BF16, side="right"))

        lqstk = [e1(nc.sbuf_tensor(f"lqstk{g}", [128, T], BF16)) for g in range(6)]
        jkstk = [e1(nc.sbuf_tensor(f"jkstk{g}", [128, T], BF16)) for g in range(6)]

        @on_ve
        def _(vector):
            vector.wait_ge(VE.h, VE.v)
            VE.inc(vector.memset(stk_big[:], 0.0))
        stk_init = VE.v

        @on_sy
        def _(sync):
            sync.wait_ge(DWS.h, DWS.v)
            DWS.dma(sync.dma_start(out=gmats[:], in_=gmat_c))
        gm_ld = DWS.v

        def emit_exterior_tile(i):
            for which in ("q", "k"):
                fsrc = q12_f if which == "q" else k12_f
                goff = 0 if which == "q" else 4
                for r in range(2):
                    def evacT(vector, view, r=r):
                        vector.wait_ge(VE.h, VE.v)
                        return VE.inc(vector.tensor_copy(rep[r][:], view))
                    emit_transpose(fsrc[:, 128 * i:128 * (i + 1)], idf[:], 128,
                                   pe_wait=(lambda t: (twait(t, vprime_done),
                                                       t.wait_ge(DWS.h, gm_ld))),
                                   evac_fn=evacT, n_out=96,
                                   gmat=gmats[:, 96 * (goff + r):96 * (goff + r + 1)])
                tqk = tp_last_ve()

                @on_ve
                def _(vector, tqk=tqk):
                    twait(vector, tqk)
                    vector.wait_ge(VE.h, VE.v)
                    p1 = VE.inc(vector.tensor_mul(L32[:], rep[0][:], rep[1][:]))
                    vector.wait_ge(VE.h, p1)

                for r in range(2):
                    def evacT(vector, view, r=r):
                        vector.wait_ge(VE.h, VE.v)
                        return VE.inc(vector.tensor_copy(rep[r][:], view))
                    emit_transpose(fsrc[:, 128 * i:128 * (i + 1)], idf[:], 128,
                                   pe_wait=(lambda t: t.wait_ge(DWS.h, gm_ld)),
                                   evac_fn=evacT, n_out=96,
                                   gmat=gmats[:, 96 * (goff + r + 2):96 * (goff + r + 3)])
                tqk2 = tp_last_ve()

                @on_ve
                def _(vector, tqk2=tqk2):
                    twait(vector, tqk2)
                    vector.wait_ge(VE.h, VE.v)
                    p2 = VE.inc(vector.tensor_mul(Lsq[:], rep[0][:], rep[1][:]))
                    vector.wait_ge(VE.h, p2)
                    vector.wait_ge(VE.h, VE.v)
                    p3 = VE.inc(vector.tensor_sub(L32[:], L32[:], Lsq[:]))
                    vector.wait_ge(VE.h, p3)
                    vector.wait_ge(VE.h, VE.v)
                    w3 = VE.inc(vector.tensor_mul(Lsq[:], L32[:], L32[:]))
                    vector.wait_ge(VE.h, w3)
                    vector.wait_ge(VE.h, VE.v)
                    w4 = VE.inc(vector.reduce_sum(
                        Lss.rearrange("p (h u) -> p h u", u=1),
                        Lsq.rearrange("p (h c) -> p h c", c=6), axis=AX))
                    vector.wait_ge(VE.h, w4)
                    vector.wait_ge(VE.h, VE.v)
                    w6 = VE.inc(vector.tensor_scalar_max(Lss[:], Lss[:], 1e-24))
                    vector.wait_ge(VE.h, w6)
                ve_a = VE.v

                @on_se
                def _(scalar, ve_a=ve_a):
                    scalar.wait_ge(VE.h, ve_a)
                    s1_ = SE.inc(scalar.activation(Lrs[:], Lss[:], AF.Ln,
                                                   scale=2.0 ** 40))
                    scalar.wait_ge(SE.h, s1_)
                    SE.inc(scalar.activation(Lrs[:], Lrs[:], AF.Exp, scale=-0.5))
                se_a = SE.v

                @on_ve
                def _(vector, se_a=se_a, which=which):
                    vector.wait_ge(SE.h, se_a)
                    vector.wait_ge(VE.h, VE.v)
                    u_ = VE.inc(vector.tensor_scalar_mul(Lrs[:], Lrs[:], 2.0 ** 20))
                    vector.wait_ge(VE.h, u_)
                    for _ in range(1):
                        vector.wait_ge(VE.h, VE.v)
                        a2 = VE.inc(vector.tensor_mul(Lt1[:], Lrs[:], Lrs[:]))
                        vector.wait_ge(VE.h, a2)
                        vector.wait_ge(VE.h, VE.v)
                        b2 = VE.inc(vector.tensor_mul(Lt1[:], Lt1[:], Lss[:]))
                        vector.wait_ge(VE.h, b2)
                        vector.wait_ge(VE.h, VE.v)
                        c2 = VE.inc(vector.tensor_scalar(Lt1[:], Lt1[:], -0.5, 1.5,
                                                         ALU.mult, ALU.add))
                        vector.wait_ge(VE.h, c2)
                        vector.wait_ge(VE.h, VE.v)
                        d2 = VE.inc(vector.tensor_mul(Lrs[:], Lrs[:], Lt1[:]))
                        vector.wait_ge(VE.h, d2)
                    if which == "q":
                        vector.wait_ge(VE.h, VE.v)
                        s2 = VE.inc(vector.tensor_scalar_mul(Lrs[:], Lrs[:], SCALE_A))
                        vector.wait_ge(VE.h, s2)
                    vector.wait_ge(VE.h, VE.v)
                    jm = VE.inc(vector.tensor_mul(
                        L32.rearrange("p (h c) -> p h c", c=6),
                        L32.rearrange("p (h c) -> p h c", c=6),
                        Lrs.rearrange("p (h u) -> p h u", u=1)
                        .broadcast_to([128, 16, 6])))
                    vector.wait_ge(VE.h, jm)
                    vector.wait_ge(VE.h, VE.v)
                    wv_ = VE.inc(vector.tensor_copy(Lh_s[:], L32[:]))
                    vector.wait_ge(VE.h, wv_)
                    vector.wait_ge(VE.h, VE.v)
                    VE.inc(vector.tensor_sub(Ll_s[:], L32[:], Lh_s[:]))
                ve_b = VE.v

                blocks = [0, 0, 1, 1] if which == "q" else [0, 1, 0, 1]
                stks = lqstk if which == "q" else jkstk

                @on_ve
                def _(vector, ve_b=ve_b, blocks=blocks, _pg=tp_last_pe()):
                    vector.wait_ge(VE.h, max(ve_b, stk_init))
                    if _pg:
                        vector.wait_ge(PE.h, _pg)
                    dstv = stk_big.rearrange("p (gh s) -> p gh s", s=32)
                    for b_i, half_ in enumerate(blocks):
                        srcT = Lh_s if half_ == 0 else Ll_s
                        vector.wait_ge(VE.h, VE.v)
                        VE.inc(vector.tensor_copy(
                            dstv[:, 0:16, 6 * b_i:6 * b_i + 6],
                            srcT.rearrange("p (h c) -> p h c", c=6)))
                asm_ve = VE.v

                for g in range(6):
                    def evacS(vector, view, stks=stks, g=g, i=i):
                        vector.wait_ge(VE.h, VE.v)
                        return VE.inc(vector.tensor_copy(
                            stks[g][0:96, 128 * i:128 * (i + 1)], view))
                    emit_transpose(stk_big[:, 96 * g:96 * (g + 1)], idb[:], 96,
                                   pe_wait=(lambda t, v=asm_ve: t.wait_ge(VE.h, v)),
                                   evac_fn=evacS)

        for _i in range(NT):
            emit_exterior_tile(_i)
        stacks_done = tp_last_ve()
        ext_pe = tp_last_pe()
        sRe.close()
        sRq.close()

        sRb = ExitStack()
        eRb = sRb.enter_context
        o_f = [eRb(nc.sbuf_tensor(f"o_f{c}", [128, BAND], F32, side="right"))
               for c in range(ND)]
        rds = eRb(nc.sbuf_tensor("rds", [128, BAND], F32, side="right"))
        os_f = eRb(nc.sbuf_tensor("os_f", [64, BAND], F32, side="right"))
        wsc = eRb(nc.sbuf_tensor("wsc", [128, 128], F32, side="right"))
        bo_t = eRb(nc.sbuf_tensor("bo_t", [128, D], F32, side="right"))
        wof_s = [eRb(nc.sbuf_tensor(f"wof_s{p}", [128, 128], F32, side="right"))
                 for p in range(4)]

        @on_sy
        def _(sync):
            sync.wait_ge(PE.h, ext_pe)
            sync.wait_ge(DWS.h, DWS.v)
            DWS.dma(sync.dma_start(out=bo_t[0:1, :], in_=bo_d))
        bo_row = DWS.v

        @on_pe
        def _(tensor):
            tensor.wait_ge(DWS.h, bo_row)
            for halfq in range(2):
                PE.inc(tensor.matmul(ps_o[:, 1024 + 512 * halfq:1024 + 512 * (halfq + 1)],
                                     triu[0:1, :], bo_t[0:1, 512 * halfq:512 * (halfq + 1)],
                                     start=True, stop=True))
        bo_pe = PE.v

        @on_ve
        def _(vector):
            vector.wait_ge(PE.h, bo_pe)
            vector.wait_ge(VE.h, VE.v)
            VE.inc(vector.tensor_copy(bo_t[:, :], ps_o[:, 1024:2048]))
        bo_ready = VE.v

        switch_slots(SLOTS2)
        o_written = {}
        gitem = [0]
        msk_count_at = {}
        wo_gate = [{}]
        mv_gate = [None]

        for b in range(NB):
            q0 = BAND * b
            for h in range(H):
                g, s_ = h // 3, h % 3
                jmax = 2 * b + 1
                items = [(j, max(0, 128 * j - q0), BAND - max(0, 128 * j - q0))
                         for j in range(jmax + 1)]
                base_n = gitem[0]
                prev_o = o_written.get((b, h - 1) if h > 0 else (b - 1, H - 1))
                _mc = sem_msk.v
                for it, (j, off, N) in enumerate(items):
                    if 128 * j >= q0:
                        _mc += 1
                        msk_count_at[base_n + it] = _mc

                @on_pe
                def _(tensor, b=b, h=h, g=g, s_=s_, q0=q0, items=items,
                      base_n=base_n, prev_o=prev_o):
                    if b == 0 and h == 0:
                        twait(tensor, stacks_done)
                        twait(tensor, vprime_done)
                        tensor.wait_ge(VE.h, bo_ready)

                    def emit_S(it):
                        j, off, N = items[it]
                        n = base_n + it
                        par = n % 2
                        if n >= 2:
                            tensor.wait_ge(sem_exp.h, n - 1)
                        sem_s.inc(tensor.matmul(
                            ps_big[:, 512 * par:512 * par + N],
                            jkstk[g][32 * s_:32 * s_ + 24, 128 * j:128 * (j + 1)],
                            lqstk[g][32 * s_:32 * s_ + 24, q0 + off:q0 + BAND],
                            start=True, stop=True))

                    def emit_PV(it):
                        j, off, N = items[it]
                        n = base_n + it
                        par = n % 2
                        tensor.wait_ge(sem_exp.h, n + 1)
                        if 128 * j >= q0:
                            tensor.wait_ge(sem_msk.h, msk_count_at[n])
                        if it == 0 and prev_o is not None:
                            twait(tensor, prev_o)
                        sem_pv.inc(tensor.matmul(
                            ps_o[0:65, off:BAND],
                            vph[j][:, 65 * h:65 * h + 65],
                            pT[par][:, 0:N],
                            start=(it == 0), stop=(it == len(items) - 1)))

                    emit_S(0)
                    for it in range(1, len(items)):
                        emit_S(it)
                        emit_PV(it - 1)
                    emit_PV(len(items) - 1)
                gitem[0] += len(items)

                @on_se
                def _(scalar, items=items, base_n=base_n):
                    for it, (j, off, N) in enumerate(items):
                        n = base_n + it
                        par = n % 2
                        scalar.wait_ge(sem_s.h, n + 1)
                        if n >= 2:
                            scalar.wait_ge(sem_pv.h, n - 1)
                        sem_exp.inc(scalar.activation(
                            pT[par][:, 0:N], ps_big[:, 512 * par:512 * par + N],
                            AF.Exp))

                @on_ve
                def _(vector, items=items, base_n=base_n, q0=q0):
                    for it, (j, off, N) in enumerate(items):
                        n = base_n + it
                        if 128 * j < q0:
                            continue
                        par = n % 2
                        vector.wait_ge(sem_exp.h, n + 1)
                        vector.wait_ge(VE.h, VE.v)
                        sem_msk.inc(vector.tensor_mul(
                            pT[par][:, 0:128], pT[par][:, 0:128], triu[:]))

                @on_ve
                def _(vector):
                    vector.wait_ge(sem_pv.h, sem_pv.v)
                    vector.wait_ge(VE.h, VE.v)
                    r = VE.inc(vector.reciprocal(rds[64:65, :], ps_o[64:65, 0:BAND]))
                    vector.wait_ge(VE.h, r)
                rd_ready = VE.v

                @on_pe
                def _(tensor, rd_ready=rd_ready):
                    tensor.wait_ge(VE.h, rd_ready)
                    PE.inc(tensor.matmul(ps_o[0:64, 1024:1024 + BAND],
                                         triu[64:65, 64:128], rds[64:65, :],
                                         start=True, stop=True))
                bc_pe = PE.v

                @on_ve
                def _(vector, h=h, bc_pe=bc_pe):
                    vector.wait_ge(PE.h, bc_pe)
                    c = h // 2
                    rb = (h % 2) * 64
                    vector.wait_ge(VE.h, VE.v)
                    a = VE.inc(vector.tensor_copy(rds[0:64, :], ps_o[0:64, 1024:1024 + BAND]))
                    vector.wait_ge(VE.h, a)
                    vector.wait_ge(VE.h, VE.v)
                    m1 = VE.inc(vector.tensor_mul(
                        ev32[0:64, 0:BAND], ps_o[0:64, 0:BAND], rds[0:64, :]))
                    vector.wait_ge(VE.h, m1)
                    vector.wait_ge(VE.h, VE.v)
                    if rb == 0:
                        VE.inc(vector.tensor_scalar_add(
                            o_f[c][0:64, :], ev32[0:64, 0:BAND],
                            bv_hc[:, h:h + 1]))
                    else:
                        VE.inc(vector.tensor_scalar_add(
                            os_f[:], ev32[0:64, 0:BAND],
                            bv_hc[:, h:h + 1]))
                norm_ve = VE.v

                if h % 2 == 1:
                    mvv = ps_big[:, 1664:1792]

                    for hc in range(2):
                        @on_pe
                        def _(tensor, norm_ve=norm_ve, _tv=tp_last_ve(),
                              hc=hc, _mg=mv_gate[0]):
                            tensor.wait_ge(VE.h, norm_ve)
                            twait(tensor, _tv)
                            if _mg:
                                tensor.wait_ge(VE.h, _mg)
                            PE.inc(tensor.matmul(
                                mvv[64:128, 0:128], idf[0:64, 0:64],
                                os_f[:, 128 * hc:128 * (hc + 1)],
                                start=True, stop=True))
                        mv1 = PE.v

                        @on_ve
                        def _(vector, mv1=mv1, h=h, hc=hc):
                            c = h // 2
                            vector.wait_ge(PE.h, mv1)
                            vector.wait_ge(VE.h, VE.v)
                            VE.inc(vector.tensor_copy(
                                o_f[c][64:128, 128 * hc:128 * (hc + 1)],
                                mvv[64:128, 0:128]))
                        mv_gate[0] = VE.v
                o_written[(b, h)] = VE.v

            for f in range(ND):
                for c in range(8):
                    par = c % 4

                    @on_sy
                    def _(sync, f=f, c=c, par=par, b=b):
                        gate = wo_gate[0].get(par)
                        sync.wait_ge(PE.h, ext_pe if gate is None else gate)
                        sync.wait_ge(DWS.h, DWS.v)
                        DWS.dma(sync.dma_start(
                            out=wof_s[par][:],
                            in_=Wo_d[128 * c:128 * (c + 1), 128 * f:128 * (f + 1)]))
                    wo_ld = DWS.v

                    @on_pe
                    def _(tensor, wo_ld=wo_ld, c=c, par=par, b=b):
                        tensor.wait_ge(DWS.h, wo_ld)
                        if c == 0:
                            tensor.wait_ge(VE.h, o_written[(b, H - 1)])
                        PE.inc(tensor.matmul(ps_o[:, 1536:1536 + BAND],
                                             wof_s[par][:], o_f[c][:],
                                             start=(c == 0), stop=(c == 7)))
                    wo_gate[0][par] = PE.v
                wo_pe = PE.v

                @on_ve
                def _(vector, wo_pe=wo_pe):
                    vector.wait_ge(PE.h, wo_pe)
                    vector.wait_ge(VE.h, VE.v)
                    VE.inc(vector.tensor_copy(ev32[:, 0:BAND], ps_o[:, 1536:1536 + BAND]))
                wo_ev = VE.v

                for tloc in range(BAND // 128):
                    i = (BAND // 128) * b + tloc

                    def evacW(vector, view, i=i, f=f):
                        vector.wait_ge(VE.h, VE.v)
                        t1 = VE.inc(vector.tensor_add(
                            wsc[:], view, bo_t[:, 128 * f:128 * (f + 1)]))
                        vector.wait_ge(VE.h, t1)
                        vector.wait_ge(VE.h, VE.v)
                        return VE.inc(vector.tensor_add(
                            xres[i][:, 128 * f:128 * (f + 1)],
                            xres[i][:, 128 * f:128 * (f + 1)], wsc[:]))
                    emit_transpose(ev32[:, 128 * tloc:128 * (tloc + 1)], idf[:], 128,
                                   pe_wait=(lambda t, v=wo_ev: t.wait_ge(VE.h, v)),
                                   evac_fn=evacW)
        sRb.close()
        s1.close()


    attn_done_gates = []

    if n_stages >= 2 or n_stages == -2:
        switch_slots(SLOTS4)
        s2 = ExitStack()
        e2 = s2.enter_context
        wqT = [e2(nc.sbuf_tensor(f"wqT{c}", [128, D], BF16)) for c in range(ND)]
        amax_t2 = e2(nc.sbuf_tensor("amax_t2", [128, 1], F32))
        bf_t = e2(nc.sbuf_tensor("bf_t", [128, D], F32))

        sW = ExitStack()
        eW = sW.enter_context
        wstage = eW(nc.sbuf_tensor("wstage", [128, 8 * 1024], F32, side="right"))
        wqrows = eW(nc.sbuf_tensor("wqrows", [128, 8 * 1024], BF16, side="right"))
        qys = eW(nc.sbuf_tensor("qys", [128, 1024], F32, side="right"))
        qy2s = eW(nc.sbuf_tensor("qy2s", [128, 1024], F32, side="right"))
        qcls = eW(nc.sbuf_tensor("qcls", [128, 1024], F32, side="right"))
        qf8as = eW(nc.sbuf_tensor("qf8as", [128, 1024], F8E4, side="right"))
        qf8bhs = eW(nc.sbuf_tensor("qf8bhs", [128, 1024], F8E4, side="right"))
        qf8bs = eW(nc.sbuf_tensor("qf8bs", [128, 1024], F32, side="right"))
        qmasks = eW(nc.sbuf_tensor("qmasks", [128, 1024], mybir.dt.uint8, side="right"))

        @on_sy
        def _(sync):
            sync.wait_ge(DWS.h, DWS.v)
            sync.wait_ge(VE.h, VE.v)
            for c in range(8):
                DWS.dma(sync.dma_start(out=wstage[:, 1024 * c:1024 * (c + 1)],
                                       in_=Wf_d[128 * c:128 * (c + 1), :]))
        wf_loaded = DWS.v

        @on_ve
        def _(vector):
            vector.wait_ge(DWS.h, wf_loaded)
            vector.wait_ge(VE.h, VE.v)
            VE.inc(vector.reduce_max(amax_acc[:], wstage[:, 0:1024], axis=AX,
                                     apply_absolute_value=True))
            for c in range(1, 8):
                vector.wait_ge(VE.h, VE.v)
                VE.inc(vector.reduce_max(amax_t2[:], wstage[:, 1024 * c:1024 * (c + 1)],
                                         axis=AX, apply_absolute_value=True))
                vector.wait_ge(VE.h, VE.v)
                VE.inc(vector.tensor_max(amax_acc[:], amax_acc[:], amax_t2[:]))
        aw_ve = VE.v

        def evacAW(vector, view):
            vector.wait_ge(VE.h, VE.v)
            return VE.inc(vector.tensor_copy(sc_row[0:1, :], view))
        emit_transpose(amax_acc[:], idf[:], 1,
                       pe_wait=(lambda t: t.wait_ge(VE.h, aw_ve)),
                       evac_fn=evacAW, n_out=128)

        @on_ve
        def _(vector):
            vector.wait_ge(VE.h, VE.v)
            VE.inc(vector.reduce_max(scal[:, 1:2], sc_row[:], axis=AX))
            vector.wait_ge(VE.h, VE.v)
            VE.inc(vector.reciprocal(scal[:, 7:8], scal[:, 1:2]))
            vector.wait_ge(VE.h, VE.v)
            VE.inc(vector.tensor_scalar_mul(scal[:, 2:3], scal[:, 7:8], 448.0))
        sw_ve = VE.v

        @on_pe
        def _(tensor):
            tensor.wait_ge(VE.h, sw_ve)
            PE.inc(tensor.matmul(ps_o[:, 0:1], triu[0:1, :], scal[0:1, 2:3],
                                 start=True, stop=True))
        swb_pe = PE.v

        @on_ve
        def _(vector):
            vector.wait_ge(PE.h, swb_pe)
            vector.wait_ge(VE.h, VE.v)
            VE.inc(vector.tensor_copy(rd1[:], ps_o[:, 0:1]))
        sw_ready = VE.v

        @on_ve
        def _(vector):
            vector.wait_ge(VE.h, sw_ready)
            for c in range(8):
                W_ = wstage[:, 1024 * c:1024 * (c + 1)]
                Q_ = wqrows[:, 1024 * c:1024 * (c + 1)]
                vector.wait_ge(VE.h, VE.v)
                VE.inc(vector.tensor_scalar_mul(qys[:], W_, rd1[:]))
                vector.wait_ge(VE.h, VE.v)
                VE.inc(vector.tensor_scalar_mul(qy2s[:], qys[:], 0.5))
                vector.wait_ge(VE.h, VE.v)
                VE.inc(vector.tensor_scalar(qcls[:], qys[:], 1.0, -1.0, ALU.min, ALU.max))
                vector.wait_ge(VE.h, VE.v)
                VE.inc(vector.tensor_copy(qf8as[:], qcls[:]))
                vector.wait_ge(VE.h, VE.v)
                VE.inc(vector.tensor_copy(qf8bhs[:], qy2s[:]))
                vector.wait_ge(VE.h, VE.v)
                VE.inc(vector.tensor_scalar_mul(qf8bs[:], qf8bhs[:], 2.0))
                vector.wait_ge(VE.h, VE.v)
                VE.inc(vector.tensor_mul(qmasks[:], qys[:], qys[:]))
                vector.wait_ge(VE.h, VE.v)
                VE.inc(vector.tensor_single_scalar(qmasks[:], qmasks[:], 1.0, ALU.is_lt))
                vector.wait_ge(VE.h, VE.v)
                VE.inc(vector.tensor_copy(qcls[:], qf8as[:]))
                vector.wait_ge(VE.h, VE.v)
                VE.inc(vector.select(Q_, qmasks[:], qcls[:], qf8bs[:], add_drain=True))
        wq_quant = VE.v

        for c1 in range(8):
            for c2 in range(8):
                emit_transpose(wqrows[:, 1024 * c1 + 128 * c2:1024 * c1 + 128 * (c2 + 1)],
                               idb[:], 128,
                               pe_wait=(lambda t: t.wait_ge(VE.h, wq_quant)),
                               dst_ap=wqT[c2][:, 128 * c1:128 * (c1 + 1)],
                               se_evac=False)
        wqT_done = tp_last_ve()

        @on_sy
        def _(sync):
            sync.wait_ge(DWS.h, DWS.v)
            DWS.dma(sync.dma_start(out=bf_t[0:1, :], in_=bf_d))
        bfr = DWS.v

        @on_pe
        def _(tensor):
            tensor.wait_ge(DWS.h, bfr)
            twait(tensor, wqT_done)
            for halfq in range(2):
                PE.inc(tensor.matmul(ps_o[:, 1024 + 512 * halfq:1024 + 512 * (halfq + 1)],
                                     triu[0:1, :], bf_t[0:1, 512 * halfq:512 * (halfq + 1)],
                                     start=True, stop=True))
        bf_pe = PE.v

        @on_ve
        def _(vector):
            vector.wait_ge(PE.h, bf_pe)
            vector.wait_ge(VE.h, VE.v)
            VE.inc(vector.tensor_copy(bf_t[:, :], ps_o[:, 1024:2048]))
        bf_ready = VE.v
        sW.close()

        def emit_ffn(cc_idx):
            switch_slots(SLOTS4)
            sF = ExitStack()
            eF = sF.enter_context
            qy = eF(nc.sbuf_tensor(f"qy{cc_idx}", [128, D], F32, side="right"))
            qy2 = [eF(nc.sbuf_tensor(f"qy2{cc_idx}_{p}", [128, D], F32, side="right"))
                   for p in range(2)]
            qcl = [eF(nc.sbuf_tensor(f"qcl{cc_idx}_{p}", [128, D], F32, side="right"))
                   for p in range(2)]
            qf8a = eF(nc.sbuf_tensor(f"qf8a{cc_idx}", [128, D], F8E4, side="right"))
            qf8bh = eF(nc.sbuf_tensor(f"qf8bh{cc_idx}", [128, D], F8E4, side="right"))
            qf8b = [eF(nc.sbuf_tensor(f"qf8b{cc_idx}_{p}", [128, D], F32, side="right"))
                    for p in range(2)]
            qmask = [eF(nc.sbuf_tensor(f"qmask{cc_idx}_{p}", [128, D], mybir.dt.uint8,
                                       side="right")) for p in range(2)]
            negb = eF(nc.sbuf_tensor(f"negb{cc_idx}", [128, 2 * NT], F32, side="right"))
            xq_t = [eF(nc.sbuf_tensor(f"xq_t{cc_idx}_{i}", [128, D], BF16, side="right"))
                    for i in range(NT)]
            xq_f = [eF(nc.sbuf_tensor(f"xq_f{cc_idx}_{c}", [128, T], BF16, side="right"))
                    for c in range(ND)]

            ln_stats(first=(n_stages == -2 and cc_idx == 0))
            rstd_at = VE.v

            @on_ve
            def _(vector):
                vector.wait_ge(VE.h, rstd_at)
                vector.wait_ge(VE.h, VE.v)
                a = VE.inc(vector.tensor_mul(negb[:, 0:NT], mu[:], rstd[:]))
                vector.wait_ge(VE.h, a)
                vector.wait_ge(VE.h, VE.v)
                VE.inc(vector.tensor_scalar_mul(negb[:, 0:NT], negb[:, 0:NT], -1.0))
            negb_ve = VE.v

            red_tok = {}
            for k in range(2 * NT):
                i, sub = k // 2, k % 2
                par = k % 2
                qv = qy[:, 512 * par:512 * (par + 1)]

                @on_se
                def _(scalar, i=i, sub=sub, k=k, qv=qv):
                    if k == 0:
                        scalar.wait_ge(VE.h, negb_ve)
                    if k >= 2:
                        scalar.wait_ge(VE.h, red_tok[k - 2])
                    scalar.wait_ge(SE.h, SE.v)
                    SE.inc(scalar.activation(
                        qv, xres[i][:, 512 * sub:512 * (sub + 1)], AF.Identity,
                        scale=rstd[:, i:i + 1], bias=negb[:, i:i + 1]))
                ap_se = SE.v

                @on_ve
                def _(vector, k=k, ap_se=ap_se, qv=qv):
                    vector.wait_ge(SE.h, ap_se)
                    vector.wait_ge(VE.h, VE.v)
                    if k == 0:
                        r = VE.inc(vector.reduce_max(amax_acc[:], qv, axis=AX,
                                                     apply_absolute_value=True))
                    else:
                        r = VE.inc(vector.reduce_max(amax_t2[:], qv, axis=AX,
                                                     apply_absolute_value=True))
                        vector.wait_ge(VE.h, r)
                        vector.wait_ge(VE.h, VE.v)
                        r = VE.inc(vector.tensor_max(amax_acc[:], amax_acc[:],
                                                     amax_t2[:]))
                red_tok[k] = VE.v
            ax_ve = VE.v

            def evacAX(vector, view):
                vector.wait_ge(VE.h, VE.v)
                return VE.inc(vector.tensor_copy(sc_row[0:1, :], view))
            emit_transpose(amax_acc[:], idf[:], 1,
                           pe_wait=(lambda t: t.wait_ge(VE.h, ax_ve)),
                           evac_fn=evacAX, n_out=128)

            @on_ve
            def _(vector):
                vector.wait_ge(VE.h, VE.v)
                VE.inc(vector.reduce_max(scal[:, 3:4], sc_row[:], axis=AX))
                vector.wait_ge(VE.h, VE.v)
                VE.inc(vector.tensor_scalar_mul(sc_row[:], triu[0:1, :], scal[0:1, 3:4]))
            ccrow_ve = VE.v

            @on_sy
            def _(sync):
                sync.wait_ge(VE.h, ccrow_ve)
                sync.wait_ge(DCC.h, DCC.v)
                DCC.dma(sync.dma_start(out=cc_in[cc_idx].ap(), in_=sc_row[:]))
            ccin_dma = DCC.v

            @on_gp
            def _(gpsimd):
                gpsimd.wait_ge(DCC.h, ccin_dma)
                CC.inc(gpsimd.collective_compute(
                    "AllReduce", ALU.max,
                    replica_groups=[list(range(N_CORES))],
                    ins=[cc_in[cc_idx].ap()],
                    outs=[cc_out[cc_idx].ap()]))
            cc_at = CC.v

            @on_sy
            def _(sync):
                sync.wait_ge(CC.h, cc_at)
                sync.wait_ge(DCC.h, DCC.v)
                DCC.dma(sync.dma_start(out=sc_row[:], in_=cc_out[cc_idx].ap()))
            ccout_dma = DCC.v

            @on_ve
            def _(vector):
                vector.wait_ge(DCC.h, ccout_dma)
                vector.wait_ge(VE.h, VE.v)
                VE.inc(vector.reciprocal(scal[:, 7:8], sc_row[0:1, 0:1]))
                vector.wait_ge(VE.h, VE.v)
                VE.inc(vector.tensor_scalar_mul(scal[:, 4:5], scal[:, 7:8], 448.0))
                vector.wait_ge(VE.h, VE.v)
                VE.inc(vector.tensor_tensor(scal[:, 5:6], sc_row[0:1, 0:1],
                                            scal[:, 1:2], ALU.mult))
                vector.wait_ge(VE.h, VE.v)
                VE.inc(vector.tensor_scalar_mul(scal[:, 6:7], scal[:, 5:6],
                                                1.0 / (448.0 * 448.0)))
            sxrs_ve = VE.v

            @on_pe
            def _(tensor):
                tensor.wait_ge(VE.h, sxrs_ve)
                PE.inc(tensor.matmul(ps_o[:, 0:1], triu[0:1, :], scal[0:1, 4:5],
                                     start=True, stop=True))
                PE.inc(tensor.matmul(ps_o[:, 512:513], triu[0:1, :], scal[0:1, 6:7],
                                     start=True, stop=True))
            sxb_pe = PE.v

            @on_ve
            def _(vector):
                vector.wait_ge(PE.h, sxb_pe)
                vector.wait_ge(VE.h, VE.v)
                VE.inc(vector.tensor_copy(sx_t[:], ps_o[:, 0:1]))
                vector.wait_ge(VE.h, VE.v)
                VE.inc(vector.tensor_copy(rs_t[:], ps_o[:, 512:513]))
                vector.wait_ge(VE.h, VE.v)
                VE.inc(vector.tensor_scalar_mul(nsc[:], rstd[:], sx_t[:]))
                vector.wait_ge(VE.h, VE.v)
                b1 = VE.inc(vector.tensor_scalar_mul(negb[:, NT:2 * NT],
                                                     negb[:, 0:NT], sx_t[:]))
                vector.wait_ge(VE.h, b1)
            base_ve = VE.v

            clamp_tok = {}
            sel_tok = {}
            ep_gate = [None, None]
            for i in range(NT):
                par = i % 2

                @on_se
                def _(scalar, i=i, par=par):
                    scalar.wait_ge(VE.h, base_ve)
                    if i >= 1:
                        scalar.wait_ge(VE.h, clamp_tok[i - 1])
                    if i >= 2:
                        scalar.wait_ge(VE.h, sel_tok[i - 2])
                    scalar.wait_ge(SE.h, SE.v)
                    SE.inc(scalar.activation(qy[:], xres[i][:], AF.Identity,
                                             scale=nsc[:, i:i + 1],
                                             bias=negb[:, NT + i:NT + i + 1]))
                    scalar.wait_ge(SE.h, SE.v)
                    SE.inc(scalar.activation(qf8bh[:], qy[:], AF.Identity,
                                             scale=0.5))
                    scalar.wait_ge(SE.h, SE.v)
                    SE.inc(scalar.activation(qf8b[par][:], qf8bh[:], AF.Identity,
                                             scale=2.0))
                se_b = SE.v

                @on_ve
                def _(vector, i=i, par=par, se_b=se_b):
                    vector.wait_ge(SE.h, se_b)
                    vector.wait_ge(VE.h, VE.v)
                    sq = VE.inc(vector.tensor_mul(qy2[par][:], qy[:], qy[:]))
                    vector.wait_ge(VE.h, sq)
                    vector.wait_ge(VE.h, VE.v)
                    VE.inc(vector.tensor_scalar(qcl[par][:], qy[:], 1.0, -1.0,
                                                ALU.min, ALU.max))
                clamp_tok[i] = VE.v

                @on_se
                def _(scalar, i=i, par=par):
                    scalar.wait_ge(VE.h, clamp_tok[i])
                    scalar.wait_ge(SE.h, SE.v)
                    SE.inc(scalar.activation(qf8a[:], qcl[par][:], AF.Identity))
                    scalar.wait_ge(SE.h, SE.v)
                    SE.inc(scalar.activation(qcl[par][:], qf8a[:], AF.Identity))
                se_d = SE.v

                @on_ve
                def _(vector, i=i, par=par, se_d=se_d):
                    vector.wait_ge(SE.h, se_d)
                    vector.wait_ge(VE.h, VE.v)
                    m = VE.inc(vector.tensor_single_scalar(
                        qmask[par][:], qy2[par][:], 1.0, ALU.is_lt))
                    vector.wait_ge(VE.h, m)
                    vector.wait_ge(VE.h, VE.v)
                    VE.inc(vector.select(xq_t[i][:], qmask[par][:], qcl[par][:],
                                         qf8b[par][:], add_drain=True))
                sel_tok[i] = VE.v

                for c in range(ND):
                    emit_transpose(xq_t[i][:, 128 * c:128 * (c + 1)], idb[:], 128,
                                   pe_wait=(lambda t, v=sel_tok[i]:
                                            t.wait_ge(VE.h, v)),
                                   dst_ap=xq_f[c][:, 128 * i:128 * (i + 1)],
                                   se_evac=False)
                xqf_tok = tp_last_ve()

                o0 = 1024 * par

                @on_pe
                def _(tensor, i=i, par=par, o0=o0, xqf_tok=xqf_tok):
                    twait(tensor, xqf_tok)
                    g = ep_gate[par]
                    if g is not None:
                        tensor.wait_ge(VE.h, g)
                    for c in range(ND):
                        for half in range(2):
                            tensor.matmul(
                                ps_o[:, o0 + 512 * half:o0 + 512 * (half + 1)],
                                xq_f[c][:, 128 * i:128 * (i + 1)],
                                wqT[c][:, 512 * half:512 * (half + 1)],
                                start=(c == 0), stop=(c == ND - 1))
                    PE.inc(tensor.matmul(ps_big[0:1, 256:257], triu[0:1, 0:1],
                                         triu[0:1, 0:1], start=True, stop=True))
                mm_pe = PE.v

                @on_ve
                def _(vector, i=i, o0=o0, mm_pe=mm_pe):
                    vector.wait_ge(PE.h, mm_pe)
                    for half in range(2):
                        vector.wait_ge(VE.h, VE.v)
                        VE.inc(vector.scalar_tensor_tensor(
                            ev32[:], ps_o[:, o0 + 512 * half:o0 + 512 * (half + 1)],
                            rs_t[:], bf_t[:, 512 * half:512 * (half + 1)],
                            ALU.mult, ALU.add))
                        vector.wait_ge(VE.h, VE.v)
                        VE.inc(vector.tensor_add(
                            xres[i][:, 512 * half:512 * (half + 1)],
                            xres[i][:, 512 * half:512 * (half + 1)], ev32[:]))
                ep_gate[par] = VE.v
            sF.close()

        if n_stages != -3:
            emit_ffn(0)

    if n_stages >= 3 or n_stages == -3:
        switch_slots(SLOTS4)
        s3 = ExitStack()
        e3 = s3.enter_context
        ln3f = [e3(nc.sbuf_tensor(f"ln3f{c}", [128, T], F32)) for c in range(ND)]
        wqk_s = e3(nc.sbuf_tensor("wqk_s", [128, 1024], F32))
        hq_sp = e3(nc.sbuf_tensor("hq_sp", [128, T], BF16))
        hk_sp = e3(nc.sbuf_tensor("hk_sp", [128, T], BF16))
        hqd_h = e3(nc.sbuf_tensor("hqd_h", [128, T], BF16))
        hqd_l = e3(nc.sbuf_tensor("hqd_l", [128, T], BF16))
        hkstk = e3(nc.sbuf_tensor("hkstk", [128, T], BF16))
        vbs = [e3(nc.sbuf_tensor(f"vbs{k}", [128, NT * 65], F32)) for k in range(2)]

        ln_stats(first=(n_stages == -3))
        ln3_rstd = VE.v

        for i in range(NT):
            for sub in range(2):
                @on_ve
                def _(vector, i=i, sub=sub, _wp=tp_last_pe()):
                    if i == 0 and sub == 0:
                        vector.wait_ge(VE.h, ln3_rstd)
                    if _wp:
                        vector.wait_ge(PE.h, _wp)
                    vector.wait_ge(VE.h, VE.v)
                    VE.inc(vector.tensor_scalar(
                        ev32[:], xres[i][:, 512 * sub:512 * (sub + 1)],
                        mu[:, i:i + 1], rstd[:, i:i + 1],
                        ALU.subtract, ALU.mult))
                ve3 = VE.v
                for cl in range(4):
                    c = 4 * sub + cl
                    emit_transpose(ev32[:, 128 * cl:128 * (cl + 1)], idf[:], 128,
                                   pe_wait=(lambda t, v=ve3: t.wait_ge(VE.h, v)),
                                   dst_ap=ln3f[c][:, 128 * i:128 * (i + 1)],
                                   se_evac=False)
        ln3_done = tp_last_ve()

        wb_gate = [None]
        qk_ve_gate = [None]
        qk_pe_gate = [None]
        dup_gate = [None]
        gitem_b = [gitem[0] if n_stages >= 1 else 0]
        sem_s_base = [sem_s.v]
        sem_pv_base = [sem_pv.v]
        o_gate_b = [None]
        switch_slots(SLOTS2)
        for p in range(H // 2):
            hA, hB = 2 * p, 2 * p + 1
            @on_sy
            def _(sync, p=p, hA=hA, hB=hB, _pg=qk_pe_gate[0]):
                if _pg is not None:
                    sync.wait_ge(PE.h, _pg)
                if p == 0:
                    twait(sync, ln3_done)
                sync.wait_ge(DWS.h, DWS.v)
                for c in range(8):
                    DWS.dma(sync.dma_start(
                        out=wqk_s[:, 128 * c:128 * c + 64],
                        in_=Wqkv_d[128 * c:128 * (c + 1), 192 * hA + 128:192 * hA + 192]))
                    DWS.dma(sync.dma_start(
                        out=wqk_s[:, 128 * c + 64:128 * (c + 1)],
                        in_=Wqkv_d[128 * c:128 * (c + 1), 192 * hB + 128:192 * hB + 192]))
            wv_l = DWS.v

            @on_ve
            def _(vector, wv_l=wv_l, p=p, _pv=sem_pv.v, _pg=qk_pe_gate[0]):
                vector.wait_ge(DWS.h, wv_l)
                vector.wait_ge(sem_pv.h, _pv)
                if _pg is not None:
                    vector.wait_ge(PE.h, _pg)
                if p == 0:
                    twait(vector, ln3_done)
                vector.wait_ge(VE.h, VE.v)
                VE.inc(vector.memset(vbs[0][:], 1.0))
                vector.wait_ge(VE.h, VE.v)
                VE.inc(vector.memset(vbs[1][:], 1.0))
            wvb_split = VE.v

            @on_pe
            def _(tensor, wvb_split=wvb_split, wv_l=wv_l):
                tensor.wait_ge(VE.h, wvb_split)
                tensor.wait_ge(DWS.h, wv_l)
                for c in range(8):
                    for nq in range(4):
                        n0 = 512 * nq
                        tensor.matmul(ps_o[:, n0:n0 + 512],
                                      wqk_s[:, 128 * c:128 * (c + 1)],
                                      ln3f[c][:, n0:n0 + 512],
                                      start=(c == 0), stop=(c == 7))
                PE.inc(tensor.matmul(ps_big[0:1, 256:257], triu[0:1, 0:1],
                                     triu[0:1, 0:1], start=True, stop=True))
            vb_pe = PE.v
            qk_pe_gate[0] = vb_pe

            for nq in range(4):
                @on_ve
                def _(vector, nq=nq, vb_pe=vb_pe):
                    vector.wait_ge(PE.h, vb_pe)
                    vector.wait_ge(VE.h, VE.v)
                    VE.inc(vector.tensor_copy(ev32[:], ps_o[:, 512 * nq:512 * (nq + 1)]))
                evd = VE.v
                for tloc in range(4):
                    i = 4 * nq + tloc

                    def evacVB(vector, view, i=i):
                        vector.wait_ge(VE.h, VE.v)
                        VE.inc(vector.tensor_copy(
                            vbs[0][:, 65 * i:65 * i + 64], view[:, 0:64]))
                        vector.wait_ge(VE.h, VE.v)
                        return VE.inc(vector.tensor_copy(
                            vbs[1][:, 65 * i:65 * i + 64], view[:, 64:128]))
                    emit_transpose(ev32[:, 128 * tloc:128 * (tloc + 1)], idf[:], 128,
                                   pe_wait=(lambda t, v=evd: t.wait_ge(VE.h, v)),
                                   evac_fn=evacVB)
            vb_done = tp_last_ve()

            for hi, h in enumerate([hA, hB]):
                @on_sy
                def _(sync, h=h, vb_done=vb_done, _pg=qk_pe_gate[0]):
                    twait(sync, vb_done)
                    if _pg is not None:
                        sync.wait_ge(PE.h, _pg)
                    sync.wait_ge(DWS.h, DWS.v)
                    for c in range(8):
                        DWS.dma(sync.dma_start(
                            out=wqk_s[:, 128 * c:128 * (c + 1)],
                            in_=Wqkv_d[128 * c:128 * (c + 1), 192 * h:192 * h + 128]))
                wqk_ld = DWS.v

                @on_pe
                def _(tensor, wqk_ld=wqk_ld, _xv=sem_exp.v, _tv=tp_last_ve()):
                    tensor.wait_ge(DWS.h, wqk_ld)
                    tensor.wait_ge(sem_exp.h, _xv)
                    twait(tensor, _tv)
                    for c in range(8):
                        for nq in range(4):
                            n0 = 512 * nq
                            tensor.matmul(ps_big[:, n0:n0 + 512],
                                          wqk_s[:, 128 * c:128 * (c + 1)],
                                          ln3f[c][:, n0:n0 + 512],
                                          start=(c == 0), stop=(c == 7))
                    PE.inc(tensor.matmul(ps_o[0:1, 256:257], triu[0:1, 0:1],
                                         triu[0:1, 0:1], start=True, stop=True))
                qk_pe = PE.v
                qk_pe_gate[0] = qk_pe

                @on_ve
                def _(vector, qk_pe=qk_pe):
                    vector.wait_ge(PE.h, qk_pe)
                    vector.wait_ge(VE.h, VE.v)
                    a_ = VE.inc(vector.tensor_scalar_mul(hq_sp[0:64, :],
                                                         ps_big[0:64, :], 0.125))
                    vector.wait_ge(VE.h, a_)
                    vector.wait_ge(VE.h, VE.v)
                    VE.inc(vector.scalar_tensor_tensor(
                        hk_sp[0:64, :], ps_big[0:64, :], 0.125, hq_sp[0:64, :],
                        ALU.mult, ALU.subtract))
                    vector.wait_ge(VE.h, VE.v)
                    b_ = VE.inc(vector.tensor_copy(hk_sp[64:128, :],
                                                   ps_big[64:128, :]))
                    vector.wait_ge(VE.h, b_)
                    vector.wait_ge(VE.h, VE.v)
                    VE.inc(vector.tensor_sub(hq_sp[64:128, :], ps_big[64:128, :],
                                             hk_sp[64:128, :]))
                split_ve = VE.v

                for (dst, srcs) in [
                        (hqd_h, (hq_sp[0:64, :], hq_sp[0:64, :], idb[0:64, 0:64])),
                        (hqd_l, (hk_sp[0:64, :], hk_sp[0:64, :], idb[0:64, 0:64])),
                        (hkstk, (hk_sp[64:128, :], hq_sp[64:128, :], idb[64:128, 64:128])),
                ]:
                    src_top, src_bot, identd = srcs

                    @on_pe
                    def _(tensor, split_ve=split_ve, src_top=src_top,
                          src_bot=src_bot, identd=identd, _xv=sem_exp.v):
                        tensor.wait_ge(VE.h, max(split_ve, dup_gate[0] or 0))
                        tensor.wait_ge(sem_exp.h, _xv)
                        for nq in range(4):
                            n0 = 512 * nq
                            tensor.matmul(ps_big[0:64, n0:n0 + 512], identd,
                                          src_top[:, n0:n0 + 512],
                                          start=True, stop=True)
                            tensor.matmul(ps_big[64:128, n0:n0 + 512], identd,
                                          src_bot[:, n0:n0 + 512],
                                          start=True, stop=True,
                                          skip_group_check=True)
                        PE.inc(tensor.matmul(ps_o[0:1, 256:257], triu[0:1, 0:1],
                                             triu[0:1, 0:1], start=True, stop=True))
                    dup_pe = PE.v

                    @on_ve
                    def _(vector, dup_pe=dup_pe, dst=dst):
                        vector.wait_ge(PE.h, dup_pe)
                        vector.wait_ge(VE.h, VE.v)
                        VE.inc(vector.tensor_copy(dst[:], ps_big[:]))
                    dup_gate[0] = VE.v
                stk_ready = VE.v

                items_b = [(j, nchunk) for j in range(NT) for nchunk in range(4)]
                base_n = gitem_b[0]
                prev_o = o_gate_b[0]

                @on_pe
                def _(tensor, h=h, hi=hi, items_b=items_b, base_n=base_n,
                      prev_o=prev_o, stk_ready=stk_ready):
                    tensor.wait_ge(VE.h, stk_ready)

                    def emit_S(it):
                        j, nchunk = items_b[it]
                        n = base_n + it
                        par = n % 2
                        if n >= 2:
                            tensor.wait_ge(sem_exp.h, n - 1)
                        tensor.matmul(
                            ps_big[:, 512 * par:512 * (par + 1)],
                            hkstk[:, 128 * j:128 * (j + 1)],
                            hqd_h[:, 512 * nchunk:512 * (nchunk + 1)],
                            start=True, stop=False)
                        sem_s.inc(tensor.matmul(
                            ps_big[:, 512 * par:512 * (par + 1)],
                            hkstk[:, 128 * j:128 * (j + 1)],
                            hqd_l[:, 512 * nchunk:512 * (nchunk + 1)],
                            start=False, stop=True))

                    def emit_PV(it):
                        j, nchunk = items_b[it]
                        n = base_n + it
                        par = n % 2
                        tensor.wait_ge(sem_exp.h, n + 1)
                        if it == 0 and prev_o is not None:
                            twait(tensor, prev_o)
                        sem_pv.inc(tensor.matmul(
                            ps_o[0:65, 512 * nchunk:512 * nchunk + 512],
                            vbs[hi][:, 65 * j:65 * j + 65],
                            pT[par][:, 0:512],
                            start=(j == 0), stop=(j == NT - 1)))

                    emit_S(0)
                    for it in range(1, len(items_b)):
                        emit_S(it)
                        emit_PV(it - 1)
                    emit_PV(len(items_b) - 1)
                gitem_b[0] += len(items_b)

                @on_se
                def _(scalar, items_b=items_b, base_n=base_n):
                    for it in range(len(items_b)):
                        n = base_n + it
                        par = n % 2
                        scalar.wait_ge(sem_s.h, sem_s_base[0] + it + 1)
                        if n >= 2:
                            scalar.wait_ge(sem_pv.h, sem_pv_base[0] + it - 1)
                        sem_exp.inc(scalar.activation(
                            pT[par][:, 0:512], ps_big[:, 512 * par:512 * (par + 1)],
                            AF.Exp))
                sem_s_base[0] += len(items_b)
                sem_pv_base[0] += len(items_b)

                @on_ve
                def _(vector, h=h):
                    vector.wait_ge(sem_pv.h, sem_pv.v)
                    vector.wait_ge(VE.h, VE.v)
                o_pv_done = VE.v
                for qc in range(4):
                    @on_ve
                    def _(vector, qc=qc):
                        vector.wait_ge(sem_pv.h, sem_pv.v)
                        vector.wait_ge(VE.h, VE.v)
                        VE.inc(vector.tensor_copy(
                            ev32[0:65, :], ps_o[0:65, 512 * qc:512 * (qc + 1)]))
                    evo = VE.v
                    for tloc in range(4):
                        i = 4 * qc + tloc

                        def evacOB(vector, view, i=i, h=h):
                            vector.wait_ge(VE.h, VE.v)
                            r = VE.inc(vector.reciprocal(rd1[:], view[:, 64:65]))
                            vector.wait_ge(VE.h, r)
                            return VE.inc(vector.scalar_tensor_tensor(
                                xres[i][:, 64 * h:64 * h + 64], view[:, 0:64],
                                rd1[:], xres[i][:, 64 * h:64 * h + 64],
                                ALU.mult, ALU.add))
                        emit_transpose(ev32[0:65, 128 * tloc:128 * (tloc + 1)],
                                       idf[0:65, 0:65], 128,
                                       pe_wait=(lambda t, v=evo: t.wait_ge(VE.h, v)),
                                       evac_fn=evacOB, n_out=65)
                o_gate_b[0] = tp_last_ve()
            wb_gate[0] = VE.v
        s3.close()

    if n_stages >= 4:
        emit_ffn(1)
    if n_stages >= 2 or n_stages == -2:
        s2.close()

    @on_sy
    def _(sync):
        sync.wait_ge(VE.h, VE.v)
        sync.wait_ge(PE.h, PE.v)
        sync.wait_ge(SE.h, SE.v)
        sync.wait_ge(sem_pv.h, sem_pv.v)
        sync.wait_ge(sem_exp.h, sem_exp.v)
        sync.wait_ge(sem_msk.h, sem_msk.v)
        for i in range(NT):
            DCC.dma(sync.dma_start(out=out_d[128 * i:128 * (i + 1), :],
                                   in_=xres[i][:]))
        sync.wait_ge(DCC.h, DCC.v)

    st.close()
    return nc


def make_in_maps(inputs):
    x = np.ascontiguousarray(np.asarray(inputs["L0_attn_a_x"], dtype=np.float32))
    w = {
        "W1q": np.ascontiguousarray(inputs["L0_attn_a_W1_q"], dtype=np.float32),
        "W2q": np.ascontiguousarray(inputs["L0_attn_a_W2_q"], dtype=np.float32),
        "W1k": np.ascontiguousarray(inputs["L0_attn_a_W1_k"], dtype=np.float32),
        "W2k": np.ascontiguousarray(inputs["L0_attn_a_W2_k"], dtype=np.float32),
        "Wv": np.ascontiguousarray(inputs["L0_attn_a_W_v"], dtype=np.float32),
        "bv": np.ascontiguousarray(np.asarray(inputs["L0_attn_a_b_v"]).reshape(1, D), dtype=np.float32),
        "Wo": np.ascontiguousarray(inputs["L0_attn_a_W_o"], dtype=np.float32),
        "bo": np.ascontiguousarray(np.asarray(inputs["L0_attn_a_b_o"]).reshape(1, D), dtype=np.float32),
        "Wqkv": np.ascontiguousarray(inputs["L0_attn_b_MhWqkv"], dtype=np.float32),
        "Wf": np.ascontiguousarray(inputs["L0_ffn_weight"], dtype=np.float32),
        "bf": np.ascontiguousarray(np.asarray(inputs["L0_ffn_bias"]).reshape(1, D), dtype=np.float32),
    }
    return [dict(w, x=np.ascontiguousarray(x[c])) for c in range(N_CORES)]


def kernel(**inputs):
    nc = build_nc(4)
    in_maps = make_in_maps(inputs)
    res = run_bass_kernel_spmd(nc, in_maps, core_ids=list(range(N_CORES)))
    return np.stack([np.asarray(r["out"]) for r in res.results], axis=0)


# version 21 (final)
# speedup vs baseline: 1.1115x; 1.0011x over previous
from contextlib import ExitStack

import numpy as np
import concourse.bass as bass
import concourse.mybir as mybir
from concourse.bass_utils import run_bass_kernel_spmd

F32 = mybir.dt.float32
BF16 = mybir.dt.bfloat16
SIM_WAITS = False
F8E4 = mybir.dt.float8e4
ALU = mybir.AluOpType
AF = mybir.ActivationFunctionType
AX = mybir.AxisListType.X

T = 2048
D = 1024
H = 16
NT = T // 128
ND = D // 128
NB = 8
BAND = T // NB
TH = T // 2
SCALE_A = 6.0 ** (-0.5)
EPS = 1e-5
N_CORES = 8

PAIRS = [(0, 1), (0, 2), (0, 3), (1, 2), (1, 3), (2, 3)]
J6SIGN = [1.0, -1.0, 1.0, 1.0, -1.0, 1.0]
PASSES3 = [(0, 0), (0, 1), (1, 0)]


class Sem:
    def __init__(self, handle):
        self.h = handle
        self.v = 0

    def inc(self, inst, k=1):
        inst.then_inc(self.h, k)
        self.v += k
        return self.v

    def dma(self, inst):
        return self.inc(inst, 16)


def build_nc(n_stages=4):
    nc = bass.Bass("TRN2", target_bir_lowering=False)

    x_in = nc.dram_tensor("x", [T, D], F32, kind="ExternalInput").ap()
    W1q_d = nc.dram_tensor("W1q", [2 * D, 64], F32, kind="ExternalInput").ap()
    W2q_d = nc.dram_tensor("W2q", [2 * D, 64], F32, kind="ExternalInput").ap()
    W1k_d = nc.dram_tensor("W1k", [D, 64], F32, kind="ExternalInput").ap()
    W2k_d = nc.dram_tensor("W2k", [D, 64], F32, kind="ExternalInput").ap()
    Wv_d = nc.dram_tensor("Wv", [D, D], F32, kind="ExternalInput").ap()
    bv_d = nc.dram_tensor("bv", [1, D], F32, kind="ExternalInput").ap()
    Wo_d = nc.dram_tensor("Wo", [D, D], F32, kind="ExternalInput").ap()
    bo_d = nc.dram_tensor("bo", [1, D], F32, kind="ExternalInput").ap()
    Wqkv_d = nc.dram_tensor("Wqkv", [D, 3 * D], F32, kind="ExternalInput").ap()
    Wf_d = nc.dram_tensor("Wf", [D, D], F32, kind="ExternalInput").ap()
    bf_d = nc.dram_tensor("bf", [1, D], F32, kind="ExternalInput").ap()
    out_d = nc.dram_tensor("out", [T, D], F32, kind="ExternalOutput").ap()
    cc_in = [nc.dram_tensor(f"cc_in{i}", [1, 128], F32) for i in range(2)]
    cc_out = [nc.dram_tensor(f"cc_out{i}", [1, 128], F32) for i in range(2)]

    ident_c = nc.inline_tensor(np.eye(128, dtype=np.float32), name="identc").ap()
    _G = np.zeros((128, 8 * 96), dtype=np.float32)
    for h_ in range(16):
        for c_ in range(6):
            a_, b_ = PAIRS[c_]
            _G[4 * h_ + a_, 0 * 96 + 6 * h_ + c_] = 1.0
            _G[64 + 4 * h_ + b_, 1 * 96 + 6 * h_ + c_] = 1.0
            _G[4 * h_ + b_, 2 * 96 + 6 * h_ + c_] = 1.0
            _G[64 + 4 * h_ + a_, 3 * 96 + 6 * h_ + c_] = 1.0
            ap_, bp_ = PAIRS[5 - c_]
            s_ = float(J6SIGN[c_])
            _G[4 * h_ + ap_, 4 * 96 + 6 * h_ + c_] = s_
            _G[64 + 4 * h_ + bp_, 5 * 96 + 6 * h_ + c_] = 1.0
            _G[4 * h_ + bp_, 6 * 96 + 6 * h_ + c_] = s_
            _G[64 + 4 * h_ + ap_, 7 * 96 + 6 * h_ + c_] = 1.0
    gmat_c = nc.inline_tensor(_G, name="gmatc").ap()
    triu_c = nc.inline_tensor(np.triu(np.ones((128, 128), dtype=np.float32)),
                              name="triuc").ap()


    st = ExitStack()
    e = st.enter_context

    xres = [e(nc.sbuf_tensor(f"xres{i}", [128, D], F32)) for i in range(NT)]
    idf = e(nc.sbuf_tensor("idf", [128, 128], F32))
    idb = e(nc.sbuf_tensor("idb", [128, 128], BF16))
    triu = e(nc.sbuf_tensor("triu", [128, 128], F32))
    mu = e(nc.sbuf_tensor("mu", [128, NT], F32))
    var_e = e(nc.sbuf_tensor("var_e", [128, NT], F32))
    rstd = e(nc.sbuf_tensor("rstd", [128, NT], F32))
    nsc = e(nc.sbuf_tensor("nsc", [128, NT], F32))
    bnst = e(nc.sbuf_tensor("bnst", [128, 12], F32))
    bnmv = e(nc.sbuf_tensor("bnmv", [128, 2], F32))
    pT = [e(nc.sbuf_tensor(f"pT{p}", [128, 512], F32)) for p in range(2)]
    ev32 = e(nc.sbuf_tensor("ev32", [128, 512], F32))
    sc_row = e(nc.sbuf_tensor("sc_row", [1, 128], F32))
    scal = e(nc.sbuf_tensor("scal", [1, 8], F32))
    amax_acc = e(nc.sbuf_tensor("amax_acc", [128, 1], F32))
    sx_t = e(nc.sbuf_tensor("sx_t", [128, 1], F32))
    rs_t = e(nc.sbuf_tensor("rs_t", [128, 1], F32))
    rd1 = e(nc.sbuf_tensor("rd1", [128, 1], F32))

    ps_big = e(nc.psum_tensor("ps_big", [128, T], F32))
    ps_o = e(nc.psum_tensor("ps_o", [128, T], F32))
    def _slot(ap):
        return (ap, ap.bitcast(BF16))

    SLOT_B2 = _slot(ps_big[:, 1024:1152])
    SLOT_B3 = _slot(ps_big[:, 1536:1664])
    SLOT_B0 = _slot(ps_big[:, 0:128])
    SLOT_B1 = _slot(ps_big[:, 512:640])
    SLOTS4 = [SLOT_B2, SLOT_B3, SLOT_B0, SLOT_B1]
    SLOTS2 = [SLOT_B2, SLOT_B3]
    SLOTS_A = SLOTS4

    DLS, DWS, PE, VE, SE, CC, DCC = (Sem(e(nc.semaphore(n))) for n in
                                     ["dls", "dws", "pe", "ve", "se", "cc", "dcc"])
    sem_s = Sem(e(nc.semaphore("sem_s")))
    sem_exp = Sem(e(nc.semaphore("sem_exp")))
    sem_msk = Sem(e(nc.semaphore("sem_msk")))
    sem_pv = Sem(e(nc.semaphore("sem_pv")))

    block = e(nc.Block())
    on_sy, on_pe, on_ve, on_se, on_gp = (block.sync, block.tensor, block.vector,
                                         block.scalar, block.gpsimd)

    @on_sy
    def _(sync):
        for i in range(NT):
            DLS.dma(sync.dma_start(out=xres[i][:], in_=x_in[128 * i:128 * (i + 1), :]))
        DWS.dma(sync.dma_start(out=idf[:], in_=ident_c))
        DWS.dma(sync.dma_start(out=triu[:], in_=triu_c))
    x_loaded = DLS.v
    const_loaded = DWS.v

    @on_ve
    def _(vector):
        vector.wait_ge(DWS.h, const_loaded)
        vector.wait_ge(VE.h, VE.v)
        VE.inc(vector.tensor_copy(idb[:], idf[:]))
        vector.wait_ge(VE.h, VE.v)
        m = VE.inc(vector.memset(scal[:], 0.0))
        vector.wait_ge(VE.h, m)
        vector.wait_ge(VE.h, VE.v)
        VE.inc(vector.tensor_scalar_add(scal[:, 0:1], scal[:, 0:1], 448.0))

    def ln_stats(first=False):
        @on_ve
        def _(vector):
            if first:
                vector.wait_ge(DLS.h, x_loaded)
            for i in range(NT):
                vector.wait_ge(VE.h, VE.v)
                a = VE.inc(vector.bn_stats(bnst[:, 0:6], xres[i][:, 0:512]))
                vector.wait_ge(VE.h, VE.v)
                b = VE.inc(vector.bn_stats(bnst[:, 6:12], xres[i][:, 512:1024]))
                vector.wait_ge(VE.h, b)
                vector.wait_ge(VE.h, VE.v)
                c = VE.inc(vector.bn_aggr(bnmv[:], bnst[:]))
                vector.wait_ge(VE.h, c)
                vector.wait_ge(VE.h, VE.v)
                VE.inc(vector.tensor_copy(mu[:, i:i + 1], bnmv[:, 0:1]))
                vector.wait_ge(VE.h, VE.v)
                d = VE.inc(vector.tensor_scalar_add(var_e[:, i:i + 1], bnmv[:, 1:2], EPS))
                vector.wait_ge(VE.h, d)
        ve_at = VE.v

        @on_se
        def _(scalar):
            scalar.wait_ge(VE.h, ve_at)
            s1_ = SE.inc(scalar.activation(rstd[:], var_e[:], AF.Ln))
            scalar.wait_ge(SE.h, s1_)
            SE.inc(scalar.activation(rstd[:], rstd[:], AF.Exp, scale=-0.5))
        se_at = SE.v

        @on_ve
        def _(vector):
            vector.wait_ge(SE.h, se_at)
            for _ in range(2):
                vector.wait_ge(VE.h, VE.v)
                a = VE.inc(vector.tensor_mul(nsc[:], rstd[:], rstd[:]))
                vector.wait_ge(VE.h, a)
                vector.wait_ge(VE.h, VE.v)
                b = VE.inc(vector.tensor_mul(nsc[:], nsc[:], var_e[:]))
                vector.wait_ge(VE.h, b)
                vector.wait_ge(VE.h, VE.v)
                c = VE.inc(vector.tensor_scalar(nsc[:], nsc[:], -0.5, 1.5,
                                                ALU.mult, ALU.add))
                vector.wait_ge(VE.h, c)
                vector.wait_ge(VE.h, VE.v)
                d = VE.inc(vector.tensor_mul(rstd[:], rstd[:], nsc[:]))
                vector.wait_ge(VE.h, d)
        return VE.v

    tp = {"n": 0, "pe": [], "ve": [], "slots": SLOTS_A, "last": {}}

    def twait(eng, tok):
        if isinstance(tok, tuple):
            eng.wait_ge(SE.h if tok[0] else VE.h, tok[1])
        elif tok:
            eng.wait_ge(VE.h, tok)

    def switch_slots(new_slots):
        tp["slots"] = new_slots

    def emit_transpose(src_ap, ident, m_out, pe_wait=None, evac_fn=None,
                       dst_ap=None, n_out=128, se_evac=False, gmat=None):
        k = tp["n"]
        slot = tp["slots"][k % len(tp["slots"])]
        sid = id(slot)
        war_tok = tp["last"].get(sid)
        view = (slot[1] if src_ap.dtype == BF16 else slot[0])[0:m_out, 0:n_out]

        @on_pe
        def _(tensor, k=k, src=src_ap, ident=ident, view=view, war_tok=war_tok,
              gmat=gmat):
            twait(tensor, war_tok)
            if pe_wait is not None:
                pe_wait(tensor)
            if gmat is not None:
                tp["pe"].append(PE.inc(tensor.matmul(view, src, gmat,
                                                     start=True, stop=True)))
            else:
                tp["pe"].append(PE.inc(tensor.transpose(view, src, ident)))

        if se_evac:
            @on_se
            def _(scalar, k=k, dst=dst_ap, view=view):
                scalar.wait_ge(PE.h, tp["pe"][k])
                scalar.wait_ge(SE.h, SE.v)
                tok = (True, SE.inc(scalar.activation(dst, view, AF.Copy)))
                tp["ve"].append(tok)
                tp["last"][sid] = tok
        else:
            @on_ve
            def _(vector, k=k, dst=dst_ap, view=view):
                vector.wait_ge(PE.h, tp["pe"][k])
                if evac_fn is not None:
                    tok = (False, evac_fn(vector, view))
                else:
                    tok = (False, VE.inc(vector.tensor_copy(dst, view)))
                tp["ve"].append(tok)
                tp["last"][sid] = tok
        tp["n"] += 1

    def tp_last_ve():
        return tp["ve"][-1] if tp["ve"] else 0

    def tp_last_pe():
        return tp["pe"][-1] if tp["pe"] else 0

    if n_stages >= 1 and n_stages not in (-2, -3):
        s1 = ExitStack()
        e1 = s1.enter_context
        vph = [e1(nc.sbuf_tensor(f"vph{i}", [128, H * 65], F32)) for i in range(NT)]
        bv_hc = e1(nc.sbuf_tensor("bv_hc", [64, H], F32))
        sv_f = e1(nc.sbuf_tensor("sv_f", [128, ND], F32))

        sRq = ExitStack()
        eRq = sRq.enter_context
        q12_f = eRq(nc.sbuf_tensor("q12_f", [128, T], F32, side="right"))
        k12_f = eRq(nc.sbuf_tensor("k12_f", [128, T], F32, side="right"))

        sRw = ExitStack()
        eRw = sRw.enter_context
        wqf = eRw(nc.sbuf_tensor("wqf", [128, 16 * 128], F32, side="right"))
        wkf = eRw(nc.sbuf_tensor("wkf", [128, 8 * 128], F32, side="right"))

        ln1_rstd = ln_stats(first=True)

        @on_sy
        def _(sync):
            sync.wait_ge(DWS.h, DWS.v)
            for c in range(16):
                DWS.dma(sync.dma_start(out=wqf[:, 128 * c:128 * c + 64],
                                       in_=W1q_d[128 * c:128 * (c + 1), :]))
                DWS.dma(sync.dma_start(out=wqf[:, 128 * c + 64:128 * (c + 1)],
                                       in_=W2q_d[128 * c:128 * (c + 1), :]))
            sync.wait_ge(DWS.h, DWS.v)
            for c in range(8):
                DWS.dma(sync.dma_start(out=wkf[:, 128 * c:128 * c + 64],
                                       in_=W1k_d[128 * c:128 * (c + 1), :]))
                DWS.dma(sync.dma_start(out=wkf[:, 128 * c + 64:128 * (c + 1)],
                                       in_=W2k_d[128 * c:128 * (c + 1), :]))
        wqk_loaded = DWS.v

        for h in range(H):
            @on_sy
            def _(sync, h=h):
                if h > 0:
                    sync.wait_ge(PE.h, tp["pe"][-1])
                sync.wait_ge(DWS.h, DWS.v)
                DWS.dma(sync.dma_start(out=sc_row[0:1, 0:64],
                                       in_=bv_d[0:1, 64 * h:64 * h + 64]))
            bvh_dma = DWS.v

            def evacB(vector, view, h=h):
                vector.wait_ge(VE.h, VE.v)
                return VE.inc(vector.tensor_copy(bv_hc[:, h:h + 1], view[:, 0:1]))
            emit_transpose(sc_row[0:1, 0:64], idf[0:1, 0:1], 64,
                           pe_wait=(lambda t, v=bvh_dma: t.wait_ge(DWS.h, v)),
                           evac_fn=evacB, n_out=1)
        bv_ready = tp_last_ve()

        sRl = ExitStack()
        eRl = sRl.enter_context
        lnf = [eRl(nc.sbuf_tensor(f"lnf{c}", [128, TH], F32, side="right"))
               for c in range(ND)]
        wvs = eRl(nc.sbuf_tensor("wvs", [128, 8 * 128], F32, side="right"))

        @on_ve
        def _(vector):
            vector.wait_ge(DLS.h, x_loaded)
            for i in range(NT):
                vector.wait_ge(VE.h, VE.v)
                VE.inc(vector.memset(vph[i][:], 1.0))
        vph_init = VE.v

        wv_gate = [None]
        q12_done_half = []
        for half in range(2):
            t0 = TH * half
            for iloc in range(NT // 2):
                i = (NT // 2) * half + iloc

                if half == 1 and iloc == 0:
                    @on_ve
                    def _(vector, _t0k=tp_last_ve()):
                        twait(vector, _t0k)
                        for c in range(ND):
                            vector.wait_ge(VE.h, VE.v)
                            VE.inc(vector.tensor_copy(sv_f[:, c:c + 1],
                                                      lnf[c][:, TH - 1:TH]))
                        vector.wait_ge(PE.h, half_consumed[0])
                for sub in range(2):
                    @on_ve
                    def _(vector, i=i, half=half, iloc=iloc, sub=sub,
                          _wp=tp_last_pe()):
                        if half == 0 and iloc == 0 and sub == 0:
                            vector.wait_ge(VE.h, ln1_rstd)
                        if _wp:
                            vector.wait_ge(PE.h, _wp)
                        vector.wait_ge(VE.h, VE.v)
                        w = VE.inc(vector.tensor_scalar(
                            ev32[:], xres[i][:, 512 * sub:512 * (sub + 1)],
                            mu[:, i:i + 1], rstd[:, i:i + 1],
                            ALU.subtract, ALU.mult))
                        vector.wait_ge(VE.h, w)
                    ve_split = VE.v
                    for cl in range(4):
                        c = 4 * sub + cl
                        emit_transpose(ev32[:, 128 * cl:128 * (cl + 1)], idf[:], 128,
                                       pe_wait=(lambda t, v=ve_split: t.wait_ge(VE.h, v)),
                                       dst_ap=lnf[c][:, 128 * iloc:128 * (iloc + 1)],
                                       se_evac=False)
            ln_done = tp_last_ve()

            @on_pe
            def _(tensor, half=half, t0=t0, ln_done=ln_done):
                twait(tensor, ln_done)
                tensor.wait_ge(DWS.h, wqk_loaded)

                for c in range(8):
                    for nq in range(2):
                        n0l = 512 * nq
                        n0g = t0 + n0l
                        tensor.matmul(ps_big[:, n0g:n0g + 512],
                                      wqf[:, 128 * c:128 * (c + 1)],
                                      lnf[c][:, n0l:n0l + 512],
                                      start=(c == 0), stop=False)
                for c in range(8):
                    ws = wqf[:, 128 * (c + 8):128 * (c + 9)]
                    xt = lnf[c]
                    for nq in range(2):
                        n0l = 512 * nq
                        n0g = t0 + n0l
                        stopf = (c == 7)
                        if n0g == 0:
                            tensor.matmul(ps_big[:, 1:512], ws,
                                          xt[:, 0:511], start=False, stop=stopf)
                        elif n0l == 0:
                            tensor.matmul(ps_big[:, t0:t0 + 1], ws,
                                          sv_f[:, c:c + 1],
                                          start=False, stop=False)
                            tensor.matmul(ps_big[:, t0 + 1:t0 + 512], ws,
                                          xt[:, 0:511], start=False, stop=stopf)
                        else:
                            tensor.matmul(ps_big[:, n0g:n0g + 512], ws,
                                          xt[:, n0l - 1:n0l + 511],
                                          start=False, stop=stopf)
                PE.inc(tensor.matmul(ps_o[0:1, 1536:1537], triu[0:1, 0:1],
                                     triu[0:1, 0:1], start=True, stop=True))
            q12_pe = PE.v

            @on_ve
            def _(vector, t0=t0, q12_pe=q12_pe):
                vector.wait_ge(PE.h, q12_pe)
                vector.wait_ge(VE.h, VE.v)
                VE.inc(vector.tensor_copy(q12_f[:, t0:t0 + TH], ps_big[:, t0:t0 + TH]))
            q12_ev = VE.v

            @on_pe
            def _(tensor, t0=t0, q12_ev=q12_ev):
                tensor.wait_ge(VE.h, q12_ev)
                for c in range(8):
                    for nq in range(2):
                        n0l = 512 * nq
                        tensor.matmul(ps_big[:, t0 + n0l:t0 + n0l + 512],
                                      wkf[:, 128 * c:128 * (c + 1)],
                                      lnf[c][:, n0l:n0l + 512],
                                      start=(c == 0), stop=(c == 7))
                PE.inc(tensor.matmul(ps_o[0:1, 1536:1537], triu[0:1, 0:1],
                                     triu[0:1, 0:1], start=True, stop=True))
            k12_pe = PE.v

            @on_ve
            def _(vector, t0=t0, k12_pe=k12_pe):
                vector.wait_ge(PE.h, k12_pe)
                vector.wait_ge(VE.h, VE.v)
                VE.inc(vector.tensor_copy(k12_f[:, t0:t0 + TH], ps_big[:, t0:t0 + TH]))
            k12_ev = VE.v

            for p in range(H // 2):
                @on_sy
                def _(sync, p=p, half=half, _wg=wv_gate[0]):
                    if _wg is not None:
                        sync.wait_ge(PE.h, _wg)
                    sync.wait_ge(DWS.h, DWS.v)
                    for c in range(8):
                        DWS.dma(sync.dma_start(
                            out=wvs[:, 128 * c:128 * (c + 1)],
                            in_=Wv_d[128 * c:128 * (c + 1), 128 * p:128 * (p + 1)]))
                wv_loaded = DWS.v

                @on_pe
                def _(tensor, wv_loaded=wv_loaded, p=p, half=half, k12_ev=k12_ev):
                    tensor.wait_ge(DWS.h, wv_loaded)
                    if p == 0:
                        tensor.wait_ge(VE.h, k12_ev)
                    for c in range(8):
                        for nq in range(2):
                            n0l = 512 * nq
                            tensor.matmul(ps_o[:, n0l:n0l + 512],
                                          wvs[:, 128 * c:128 * (c + 1)],
                                          lnf[c][:, n0l:n0l + 512],
                                          start=(c == 0), stop=(c == 7))
                    PE.inc(tensor.matmul(ps_big[0:1, 256:257], triu[0:1, 0:1],
                                         triu[0:1, 0:1], start=True, stop=True))
                vp_pe = PE.v
                wv_gate[0] = vp_pe

                for nq in range(2):
                    @on_ve
                    def _(vector, nq=nq, vp_pe=vp_pe):
                        vector.wait_ge(PE.h, vp_pe)
                        if nq == 0:
                            vector.wait_ge(VE.h, vph_init)
                        vector.wait_ge(VE.h, VE.v)
                        VE.inc(vector.tensor_copy(ev32[:], ps_o[:, 512 * nq:512 * (nq + 1)]))
                    ev_done = VE.v
                    for tloc in range(4):
                        i = (NT // 2) * half + 4 * nq + tloc
                        hA, hB = 2 * p, 2 * p + 1

                        def evacV(vector, view, i=i, hA=hA, hB=hB):
                            vector.wait_ge(VE.h, VE.v)
                            VE.inc(vector.tensor_copy(
                                vph[i][:, 65 * hA:65 * hA + 64], view[:, 0:64]))
                            vector.wait_ge(VE.h, VE.v)
                            return VE.inc(vector.tensor_copy(
                                vph[i][:, 65 * hB:65 * hB + 64], view[:, 64:128]))
                        emit_transpose(ev32[:, 128 * tloc:128 * (tloc + 1)], idf[:], 128,
                                       pe_wait=(lambda t, v=ev_done: t.wait_ge(VE.h, v)),
                                       evac_fn=evacV)
            half_consumed = [PE.v]
        vprime_done = tp_last_ve()

        sRl.close()
        sRw.close()

        sRe = ExitStack()
        eRe = sRe.enter_context
        rep = [eRe(nc.sbuf_tensor(f"rep{r}", [128, 96], F32, side="right"))
               for r in range(2)]
        gmats = eRe(nc.sbuf_tensor("gmats", [128, 8 * 96], F32, side="right"))
        L32 = eRe(nc.sbuf_tensor("L32", [128, 96], F32, side="right"))
        Lsq = eRe(nc.sbuf_tensor("Lsq", [128, 96], F32, side="right"))
        Lss = eRe(nc.sbuf_tensor("Lss", [128, 16], F32, side="right"))
        Lrs = eRe(nc.sbuf_tensor("Lrs", [128, 16], F32, side="right"))
        Lt1 = eRe(nc.sbuf_tensor("Lt1", [128, 16], F32, side="right"))
        Lh_s = eRe(nc.sbuf_tensor("Lh_s", [128, 96], BF16, side="right"))
        Ll_s = eRe(nc.sbuf_tensor("Ll_s", [128, 96], BF16, side="right"))
        stk_big = eRe(nc.sbuf_tensor("stk_big", [128, 576], BF16, side="right"))

        lqstk = [e1(nc.sbuf_tensor(f"lqstk{g}", [128, T], BF16)) for g in range(6)]
        jkstk = [e1(nc.sbuf_tensor(f"jkstk{g}", [128, T], BF16)) for g in range(6)]

        @on_ve
        def _(vector):
            vector.wait_ge(VE.h, VE.v)
            VE.inc(vector.memset(stk_big[:], 0.0))
        stk_init = VE.v

        @on_sy
        def _(sync):
            sync.wait_ge(DWS.h, DWS.v)
            DWS.dma(sync.dma_start(out=gmats[:], in_=gmat_c))
        gm_ld = DWS.v

        def emit_exterior_tile(i):
            for which in ("q", "k"):
                fsrc = q12_f if which == "q" else k12_f
                goff = 0 if which == "q" else 4
                for r in range(2):
                    def evacT(vector, view, r=r):
                        vector.wait_ge(VE.h, VE.v)
                        return VE.inc(vector.tensor_copy(rep[r][:], view))
                    emit_transpose(fsrc[:, 128 * i:128 * (i + 1)], idf[:], 128,
                                   pe_wait=(lambda t: (twait(t, vprime_done),
                                                       t.wait_ge(DWS.h, gm_ld))),
                                   evac_fn=evacT, n_out=96,
                                   gmat=gmats[:, 96 * (goff + r):96 * (goff + r + 1)])
                tqk = tp_last_ve()

                @on_ve
                def _(vector, tqk=tqk):
                    twait(vector, tqk)
                    vector.wait_ge(VE.h, VE.v)
                    p1 = VE.inc(vector.tensor_mul(L32[:], rep[0][:], rep[1][:]))
                    vector.wait_ge(VE.h, p1)

                for r in range(2):
                    def evacT(vector, view, r=r):
                        vector.wait_ge(VE.h, VE.v)
                        return VE.inc(vector.tensor_copy(rep[r][:], view))
                    emit_transpose(fsrc[:, 128 * i:128 * (i + 1)], idf[:], 128,
                                   pe_wait=(lambda t: t.wait_ge(DWS.h, gm_ld)),
                                   evac_fn=evacT, n_out=96,
                                   gmat=gmats[:, 96 * (goff + r + 2):96 * (goff + r + 3)])
                tqk2 = tp_last_ve()

                @on_ve
                def _(vector, tqk2=tqk2):
                    twait(vector, tqk2)
                    vector.wait_ge(VE.h, VE.v)
                    p2 = VE.inc(vector.tensor_mul(Lsq[:], rep[0][:], rep[1][:]))
                    vector.wait_ge(VE.h, p2)
                    vector.wait_ge(VE.h, VE.v)
                    p3 = VE.inc(vector.tensor_sub(L32[:], L32[:], Lsq[:]))
                    vector.wait_ge(VE.h, p3)
                    vector.wait_ge(VE.h, VE.v)
                    w3 = VE.inc(vector.tensor_mul(Lsq[:], L32[:], L32[:]))
                    vector.wait_ge(VE.h, w3)
                    vector.wait_ge(VE.h, VE.v)
                    w4 = VE.inc(vector.reduce_sum(
                        Lss.rearrange("p (h u) -> p h u", u=1),
                        Lsq.rearrange("p (h c) -> p h c", c=6), axis=AX))
                    vector.wait_ge(VE.h, w4)
                    vector.wait_ge(VE.h, VE.v)
                    w6 = VE.inc(vector.tensor_scalar_max(Lss[:], Lss[:], 1e-24))
                    vector.wait_ge(VE.h, w6)
                ve_a = VE.v

                @on_se
                def _(scalar, ve_a=ve_a):
                    scalar.wait_ge(VE.h, ve_a)
                    s1_ = SE.inc(scalar.activation(Lrs[:], Lss[:], AF.Ln,
                                                   scale=2.0 ** 40))
                    scalar.wait_ge(SE.h, s1_)
                    SE.inc(scalar.activation(Lrs[:], Lrs[:], AF.Exp, scale=-0.5))
                se_a = SE.v

                @on_ve
                def _(vector, se_a=se_a, which=which):
                    vector.wait_ge(SE.h, se_a)
                    vector.wait_ge(VE.h, VE.v)
                    u_ = VE.inc(vector.tensor_scalar_mul(Lrs[:], Lrs[:], 2.0 ** 20))
                    vector.wait_ge(VE.h, u_)
                    for _ in range(1):
                        vector.wait_ge(VE.h, VE.v)
                        a2 = VE.inc(vector.tensor_mul(Lt1[:], Lrs[:], Lrs[:]))
                        vector.wait_ge(VE.h, a2)
                        vector.wait_ge(VE.h, VE.v)
                        b2 = VE.inc(vector.tensor_mul(Lt1[:], Lt1[:], Lss[:]))
                        vector.wait_ge(VE.h, b2)
                        vector.wait_ge(VE.h, VE.v)
                        c2 = VE.inc(vector.tensor_scalar(Lt1[:], Lt1[:], -0.5, 1.5,
                                                         ALU.mult, ALU.add))
                        vector.wait_ge(VE.h, c2)
                        vector.wait_ge(VE.h, VE.v)
                        d2 = VE.inc(vector.tensor_mul(Lrs[:], Lrs[:], Lt1[:]))
                        vector.wait_ge(VE.h, d2)
                    if which == "q":
                        vector.wait_ge(VE.h, VE.v)
                        s2 = VE.inc(vector.tensor_scalar_mul(Lrs[:], Lrs[:], SCALE_A))
                        vector.wait_ge(VE.h, s2)
                    vector.wait_ge(VE.h, VE.v)
                    jm = VE.inc(vector.tensor_mul(
                        L32.rearrange("p (h c) -> p h c", c=6),
                        L32.rearrange("p (h c) -> p h c", c=6),
                        Lrs.rearrange("p (h u) -> p h u", u=1)
                        .broadcast_to([128, 16, 6])))
                    vector.wait_ge(VE.h, jm)
                    vector.wait_ge(VE.h, VE.v)
                    wv_ = VE.inc(vector.tensor_copy(Lh_s[:], L32[:]))
                    vector.wait_ge(VE.h, wv_)
                    vector.wait_ge(VE.h, VE.v)
                    VE.inc(vector.tensor_sub(Ll_s[:], L32[:], Lh_s[:]))
                ve_b = VE.v

                blocks = [0, 0, 1, 1] if which == "q" else [0, 1, 0, 1]
                stks = lqstk if which == "q" else jkstk

                @on_ve
                def _(vector, ve_b=ve_b, blocks=blocks, _pg=tp_last_pe()):
                    vector.wait_ge(VE.h, max(ve_b, stk_init))
                    if _pg:
                        vector.wait_ge(PE.h, _pg)
                    dstv = stk_big.rearrange("p (gh s) -> p gh s", s=32)
                    for b_i, half_ in enumerate(blocks):
                        srcT = Lh_s if half_ == 0 else Ll_s
                        vector.wait_ge(VE.h, VE.v)
                        VE.inc(vector.tensor_copy(
                            dstv[:, 0:16, 6 * b_i:6 * b_i + 6],
                            srcT.rearrange("p (h c) -> p h c", c=6)))
                asm_ve = VE.v

                for g in range(6):
                    def evacS(vector, view, stks=stks, g=g, i=i):
                        vector.wait_ge(VE.h, VE.v)
                        return VE.inc(vector.tensor_copy(
                            stks[g][0:96, 128 * i:128 * (i + 1)], view))
                    emit_transpose(stk_big[:, 96 * g:96 * (g + 1)], idb[:], 96,
                                   pe_wait=(lambda t, v=asm_ve: t.wait_ge(VE.h, v)),
                                   evac_fn=evacS)

        for _i in range(NT):
            emit_exterior_tile(_i)
        stacks_done = tp_last_ve()
        ext_pe = tp_last_pe()
        sRe.close()
        sRq.close()

        sRb = ExitStack()
        eRb = sRb.enter_context
        o_f = [eRb(nc.sbuf_tensor(f"o_f{c}", [128, BAND], F32, side="right"))
               for c in range(ND)]
        rds = eRb(nc.sbuf_tensor("rds", [128, BAND], F32, side="right"))
        os_f = eRb(nc.sbuf_tensor("os_f", [64, BAND], F32, side="right"))
        wsc = eRb(nc.sbuf_tensor("wsc", [128, 128], F32, side="right"))
        bo_t = eRb(nc.sbuf_tensor("bo_t", [128, D], F32, side="right"))
        wof_s = [eRb(nc.sbuf_tensor(f"wof_s{p}", [128, 128], F32, side="right"))
                 for p in range(4)]

        @on_sy
        def _(sync):
            sync.wait_ge(PE.h, ext_pe)
            sync.wait_ge(DWS.h, DWS.v)
            DWS.dma(sync.dma_start(out=bo_t[0:1, :], in_=bo_d))
        bo_row = DWS.v

        @on_pe
        def _(tensor):
            tensor.wait_ge(DWS.h, bo_row)
            for halfq in range(2):
                PE.inc(tensor.matmul(ps_o[:, 1024 + 512 * halfq:1024 + 512 * (halfq + 1)],
                                     triu[0:1, :], bo_t[0:1, 512 * halfq:512 * (halfq + 1)],
                                     start=True, stop=True))
        bo_pe = PE.v

        @on_ve
        def _(vector):
            vector.wait_ge(PE.h, bo_pe)
            vector.wait_ge(VE.h, VE.v)
            VE.inc(vector.tensor_copy(bo_t[:, :], ps_o[:, 1024:2048]))
        bo_ready = VE.v

        switch_slots(SLOTS2)
        o_written = {}
        gitem = [0]
        msk_count_at = {}
        wo_gate = [{}]
        mv_gate = [None]

        for b in range(NB):
            q0 = BAND * b
            for h in range(H):
                g, s_ = h // 3, h % 3
                jmax = 2 * b + 1
                items = [(j, max(0, 128 * j - q0), BAND - max(0, 128 * j - q0))
                         for j in range(jmax + 1)]
                base_n = gitem[0]
                prev_o = o_written.get((b, h - 1) if h > 0 else (b - 1, H - 1))
                _mc = sem_msk.v
                for it, (j, off, N) in enumerate(items):
                    if 128 * j >= q0:
                        _mc += 1
                        msk_count_at[base_n + it] = _mc

                @on_pe
                def _(tensor, b=b, h=h, g=g, s_=s_, q0=q0, items=items,
                      base_n=base_n, prev_o=prev_o):
                    if b == 0 and h == 0:
                        twait(tensor, stacks_done)
                        twait(tensor, vprime_done)
                        tensor.wait_ge(VE.h, bo_ready)

                    def emit_S(it):
                        j, off, N = items[it]
                        n = base_n + it
                        par = n % 2
                        if n >= 2:
                            tensor.wait_ge(sem_exp.h, n - 1)
                        sem_s.inc(tensor.matmul(
                            ps_big[:, 512 * par:512 * par + N],
                            jkstk[g][32 * s_:32 * s_ + 24, 128 * j:128 * (j + 1)],
                            lqstk[g][32 * s_:32 * s_ + 24, q0 + off:q0 + BAND],
                            start=True, stop=True))

                    def emit_PV(it):
                        j, off, N = items[it]
                        n = base_n + it
                        par = n % 2
                        tensor.wait_ge(sem_exp.h, n + 1)
                        if 128 * j >= q0:
                            tensor.wait_ge(sem_msk.h, msk_count_at[n])
                        if it == 0 and prev_o is not None:
                            twait(tensor, prev_o)
                        sem_pv.inc(tensor.matmul(
                            ps_o[0:65, off:BAND],
                            vph[j][:, 65 * h:65 * h + 65],
                            pT[par][:, 0:N],
                            start=(it == 0), stop=(it == len(items) - 1)))

                    emit_S(0)
                    for it in range(1, len(items)):
                        emit_S(it)
                        emit_PV(it - 1)
                    emit_PV(len(items) - 1)
                gitem[0] += len(items)

                @on_se
                def _(scalar, items=items, base_n=base_n):
                    for it, (j, off, N) in enumerate(items):
                        n = base_n + it
                        par = n % 2
                        scalar.wait_ge(sem_s.h, n + 1)
                        if n >= 2:
                            scalar.wait_ge(sem_pv.h, n - 1)
                        sem_exp.inc(scalar.activation(
                            pT[par][:, 0:N], ps_big[:, 512 * par:512 * par + N],
                            AF.Exp))

                @on_ve
                def _(vector, items=items, base_n=base_n, q0=q0):
                    for it, (j, off, N) in enumerate(items):
                        n = base_n + it
                        if 128 * j < q0:
                            continue
                        par = n % 2
                        vector.wait_ge(sem_exp.h, n + 1)
                        vector.wait_ge(VE.h, VE.v)
                        sem_msk.inc(vector.tensor_mul(
                            pT[par][:, 0:128], pT[par][:, 0:128], triu[:]))

                @on_ve
                def _(vector):
                    vector.wait_ge(sem_pv.h, sem_pv.v)
                    vector.wait_ge(VE.h, VE.v)
                    r = VE.inc(vector.reciprocal(rds[64:65, :], ps_o[64:65, 0:BAND]))
                    vector.wait_ge(VE.h, r)
                rd_ready = VE.v

                @on_pe
                def _(tensor, rd_ready=rd_ready):
                    tensor.wait_ge(VE.h, rd_ready)
                    PE.inc(tensor.matmul(ps_o[0:64, 1024:1024 + BAND],
                                         triu[64:65, 64:128], rds[64:65, :],
                                         start=True, stop=True))
                bc_pe = PE.v

                @on_ve
                def _(vector, h=h, bc_pe=bc_pe):
                    vector.wait_ge(PE.h, bc_pe)
                    c = h // 2
                    rb = (h % 2) * 64
                    vector.wait_ge(VE.h, VE.v)
                    a = VE.inc(vector.tensor_copy(rds[0:64, :], ps_o[0:64, 1024:1024 + BAND]))
                    vector.wait_ge(VE.h, a)
                    vector.wait_ge(VE.h, VE.v)
                    m1 = VE.inc(vector.tensor_mul(
                        ev32[0:64, 0:BAND], ps_o[0:64, 0:BAND], rds[0:64, :]))
                    vector.wait_ge(VE.h, m1)
                    vector.wait_ge(VE.h, VE.v)
                    if rb == 0:
                        VE.inc(vector.tensor_scalar_add(
                            o_f[c][0:64, :], ev32[0:64, 0:BAND],
                            bv_hc[:, h:h + 1]))
                    else:
                        VE.inc(vector.tensor_scalar_add(
                            os_f[:], ev32[0:64, 0:BAND],
                            bv_hc[:, h:h + 1]))
                norm_ve = VE.v

                if h % 2 == 1:
                    mvv = ps_big[:, 1664:1792]

                    for hc in range(2):
                        @on_pe
                        def _(tensor, norm_ve=norm_ve, _tv=tp_last_ve(),
                              hc=hc, _mg=mv_gate[0]):
                            tensor.wait_ge(VE.h, norm_ve)
                            twait(tensor, _tv)
                            if _mg:
                                tensor.wait_ge(VE.h, _mg)
                            PE.inc(tensor.matmul(
                                mvv[64:128, 0:128], idf[0:64, 0:64],
                                os_f[:, 128 * hc:128 * (hc + 1)],
                                start=True, stop=True))
                        mv1 = PE.v

                        @on_ve
                        def _(vector, mv1=mv1, h=h, hc=hc):
                            c = h // 2
                            vector.wait_ge(PE.h, mv1)
                            vector.wait_ge(VE.h, VE.v)
                            VE.inc(vector.tensor_copy(
                                o_f[c][64:128, 128 * hc:128 * (hc + 1)],
                                mvv[64:128, 0:128]))
                        mv_gate[0] = VE.v
                o_written[(b, h)] = VE.v

            for f in range(ND):
                for c in range(8):
                    par = c % 4

                    @on_sy
                    def _(sync, f=f, c=c, par=par, b=b):
                        gate = wo_gate[0].get(par)
                        sync.wait_ge(PE.h, ext_pe if gate is None else gate)
                        sync.wait_ge(DWS.h, DWS.v)
                        DWS.dma(sync.dma_start(
                            out=wof_s[par][:],
                            in_=Wo_d[128 * c:128 * (c + 1), 128 * f:128 * (f + 1)]))
                    wo_ld = DWS.v

                    @on_pe
                    def _(tensor, wo_ld=wo_ld, c=c, par=par, b=b):
                        tensor.wait_ge(DWS.h, wo_ld)
                        if c == 0:
                            tensor.wait_ge(VE.h, o_written[(b, H - 1)])
                        PE.inc(tensor.matmul(ps_o[:, 1536:1536 + BAND],
                                             wof_s[par][:], o_f[c][:],
                                             start=(c == 0), stop=(c == 7)))
                    wo_gate[0][par] = PE.v
                wo_pe = PE.v

                @on_ve
                def _(vector, wo_pe=wo_pe):
                    vector.wait_ge(PE.h, wo_pe)
                    vector.wait_ge(VE.h, VE.v)
                    VE.inc(vector.tensor_copy(ev32[:, 0:BAND], ps_o[:, 1536:1536 + BAND]))
                wo_ev = VE.v

                for tloc in range(BAND // 128):
                    i = (BAND // 128) * b + tloc

                    def evacW(vector, view, i=i, f=f):
                        vector.wait_ge(VE.h, VE.v)
                        t1 = VE.inc(vector.tensor_add(
                            wsc[:], view, bo_t[:, 128 * f:128 * (f + 1)]))
                        vector.wait_ge(VE.h, t1)
                        vector.wait_ge(VE.h, VE.v)
                        return VE.inc(vector.tensor_add(
                            xres[i][:, 128 * f:128 * (f + 1)],
                            xres[i][:, 128 * f:128 * (f + 1)], wsc[:]))
                    emit_transpose(ev32[:, 128 * tloc:128 * (tloc + 1)], idf[:], 128,
                                   pe_wait=(lambda t, v=wo_ev: t.wait_ge(VE.h, v)),
                                   evac_fn=evacW)
        sRb.close()
        s1.close()


    attn_done_gates = []

    if n_stages >= 2 or n_stages == -2:
        switch_slots(SLOTS4)
        s2 = ExitStack()
        e2 = s2.enter_context
        wqT = [e2(nc.sbuf_tensor(f"wqT{c}", [128, D], BF16)) for c in range(ND)]
        amax_t2 = e2(nc.sbuf_tensor("amax_t2", [128, 1], F32))
        bf_t = e2(nc.sbuf_tensor("bf_t", [128, D], F32))

        sW = ExitStack()
        eW = sW.enter_context
        wstage = eW(nc.sbuf_tensor("wstage", [128, 8 * 1024], F32, side="right"))
        wqrows = eW(nc.sbuf_tensor("wqrows", [128, 8 * 1024], BF16, side="right"))
        qys = eW(nc.sbuf_tensor("qys", [128, 1024], F32, side="right"))
        qy2s = eW(nc.sbuf_tensor("qy2s", [128, 1024], F32, side="right"))
        qcls = eW(nc.sbuf_tensor("qcls", [128, 1024], F32, side="right"))
        qf8as = eW(nc.sbuf_tensor("qf8as", [128, 1024], F8E4, side="right"))
        qf8bhs = eW(nc.sbuf_tensor("qf8bhs", [128, 1024], F8E4, side="right"))
        qf8bs = eW(nc.sbuf_tensor("qf8bs", [128, 1024], F32, side="right"))
        qmasks = eW(nc.sbuf_tensor("qmasks", [128, 1024], mybir.dt.uint8, side="right"))

        @on_sy
        def _(sync):
            sync.wait_ge(DWS.h, DWS.v)
            sync.wait_ge(VE.h, VE.v)
            for c in range(8):
                DWS.dma(sync.dma_start(out=wstage[:, 1024 * c:1024 * (c + 1)],
                                       in_=Wf_d[128 * c:128 * (c + 1), :]))
        wf_loaded = DWS.v

        @on_ve
        def _(vector):
            vector.wait_ge(DWS.h, wf_loaded)
            vector.wait_ge(VE.h, VE.v)
            VE.inc(vector.reduce_max(amax_acc[:], wstage[:, 0:1024], axis=AX,
                                     apply_absolute_value=True))
            for c in range(1, 8):
                vector.wait_ge(VE.h, VE.v)
                VE.inc(vector.reduce_max(amax_t2[:], wstage[:, 1024 * c:1024 * (c + 1)],
                                         axis=AX, apply_absolute_value=True))
                vector.wait_ge(VE.h, VE.v)
                VE.inc(vector.tensor_max(amax_acc[:], amax_acc[:], amax_t2[:]))
        aw_ve = VE.v

        def evacAW(vector, view):
            vector.wait_ge(VE.h, VE.v)
            return VE.inc(vector.tensor_copy(sc_row[0:1, :], view))
        emit_transpose(amax_acc[:], idf[:], 1,
                       pe_wait=(lambda t: t.wait_ge(VE.h, aw_ve)),
                       evac_fn=evacAW, n_out=128)

        @on_ve
        def _(vector):
            vector.wait_ge(VE.h, VE.v)
            VE.inc(vector.reduce_max(scal[:, 1:2], sc_row[:], axis=AX))
            vector.wait_ge(VE.h, VE.v)
            VE.inc(vector.reciprocal(scal[:, 7:8], scal[:, 1:2]))
            vector.wait_ge(VE.h, VE.v)
            VE.inc(vector.tensor_scalar_mul(scal[:, 2:3], scal[:, 7:8], 448.0))
        sw_ve = VE.v

        @on_pe
        def _(tensor):
            tensor.wait_ge(VE.h, sw_ve)
            PE.inc(tensor.matmul(ps_o[:, 0:1], triu[0:1, :], scal[0:1, 2:3],
                                 start=True, stop=True))
        swb_pe = PE.v

        @on_ve
        def _(vector):
            vector.wait_ge(PE.h, swb_pe)
            vector.wait_ge(VE.h, VE.v)
            VE.inc(vector.tensor_copy(rd1[:], ps_o[:, 0:1]))
        sw_ready = VE.v

        @on_ve
        def _(vector):
            vector.wait_ge(VE.h, sw_ready)
            for c in range(8):
                W_ = wstage[:, 1024 * c:1024 * (c + 1)]
                Q_ = wqrows[:, 1024 * c:1024 * (c + 1)]
                vector.wait_ge(VE.h, VE.v)
                VE.inc(vector.tensor_scalar_mul(qys[:], W_, rd1[:]))
                vector.wait_ge(VE.h, VE.v)
                VE.inc(vector.tensor_scalar_mul(qy2s[:], qys[:], 0.5))
                vector.wait_ge(VE.h, VE.v)
                VE.inc(vector.tensor_scalar(qcls[:], qys[:], 1.0, -1.0, ALU.min, ALU.max))
                vector.wait_ge(VE.h, VE.v)
                VE.inc(vector.tensor_copy(qf8as[:], qcls[:]))
                vector.wait_ge(VE.h, VE.v)
                VE.inc(vector.tensor_copy(qf8bhs[:], qy2s[:]))
                vector.wait_ge(VE.h, VE.v)
                VE.inc(vector.tensor_scalar_mul(qf8bs[:], qf8bhs[:], 2.0))
                vector.wait_ge(VE.h, VE.v)
                VE.inc(vector.tensor_mul(qmasks[:], qys[:], qys[:]))
                vector.wait_ge(VE.h, VE.v)
                VE.inc(vector.tensor_single_scalar(qmasks[:], qmasks[:], 1.0, ALU.is_lt))
                vector.wait_ge(VE.h, VE.v)
                VE.inc(vector.tensor_copy(qcls[:], qf8as[:]))
                vector.wait_ge(VE.h, VE.v)
                VE.inc(vector.select(Q_, qmasks[:], qcls[:], qf8bs[:], add_drain=True))
        wq_quant = VE.v

        for c1 in range(8):
            for c2 in range(8):
                emit_transpose(wqrows[:, 1024 * c1 + 128 * c2:1024 * c1 + 128 * (c2 + 1)],
                               idb[:], 128,
                               pe_wait=(lambda t: t.wait_ge(VE.h, wq_quant)),
                               dst_ap=wqT[c2][:, 128 * c1:128 * (c1 + 1)],
                               se_evac=False)
        wqT_done = tp_last_ve()

        @on_sy
        def _(sync):
            sync.wait_ge(DWS.h, DWS.v)
            DWS.dma(sync.dma_start(out=bf_t[0:1, :], in_=bf_d))
        bfr = DWS.v

        @on_pe
        def _(tensor):
            tensor.wait_ge(DWS.h, bfr)
            twait(tensor, wqT_done)
            for halfq in range(2):
                PE.inc(tensor.matmul(ps_o[:, 1024 + 512 * halfq:1024 + 512 * (halfq + 1)],
                                     triu[0:1, :], bf_t[0:1, 512 * halfq:512 * (halfq + 1)],
                                     start=True, stop=True))
        bf_pe = PE.v

        @on_ve
        def _(vector):
            vector.wait_ge(PE.h, bf_pe)
            vector.wait_ge(VE.h, VE.v)
            VE.inc(vector.tensor_copy(bf_t[:, :], ps_o[:, 1024:2048]))
        bf_ready = VE.v
        sW.close()

        def emit_ffn(cc_idx):
            switch_slots(SLOTS4)
            sF = ExitStack()
            eF = sF.enter_context
            qy = eF(nc.sbuf_tensor(f"qy{cc_idx}", [128, D], F32, side="right"))
            qy2 = [eF(nc.sbuf_tensor(f"qy2{cc_idx}_{p}", [128, D], F32, side="right"))
                   for p in range(2)]
            qcl = [eF(nc.sbuf_tensor(f"qcl{cc_idx}_{p}", [128, D], F32, side="right"))
                   for p in range(2)]
            qf8a = eF(nc.sbuf_tensor(f"qf8a{cc_idx}", [128, D], F8E4, side="right"))
            qf8bh = eF(nc.sbuf_tensor(f"qf8bh{cc_idx}", [128, D], F8E4, side="right"))
            qf8b = [eF(nc.sbuf_tensor(f"qf8b{cc_idx}_{p}", [128, D], F32, side="right"))
                    for p in range(2)]
            qmask = [eF(nc.sbuf_tensor(f"qmask{cc_idx}_{p}", [128, D], mybir.dt.uint8,
                                       side="right")) for p in range(2)]
            negb = eF(nc.sbuf_tensor(f"negb{cc_idx}", [128, 2 * NT], F32, side="right"))
            xq_t = [eF(nc.sbuf_tensor(f"xq_t{cc_idx}_{i}", [128, D], BF16, side="right"))
                    for i in range(NT)]
            xq_f = [eF(nc.sbuf_tensor(f"xq_f{cc_idx}_{c}", [128, T], BF16, side="right"))
                    for c in range(ND)]

            ln_stats(first=(n_stages == -2 and cc_idx == 0))
            rstd_at = VE.v

            @on_ve
            def _(vector):
                vector.wait_ge(VE.h, rstd_at)
                vector.wait_ge(VE.h, VE.v)
                a = VE.inc(vector.tensor_mul(negb[:, 0:NT], mu[:], rstd[:]))
                vector.wait_ge(VE.h, a)
                vector.wait_ge(VE.h, VE.v)
                VE.inc(vector.tensor_scalar_mul(negb[:, 0:NT], negb[:, 0:NT], -1.0))
            negb_ve = VE.v

            red_tok = {}
            for k in range(2 * NT):
                i, sub = k // 2, k % 2
                par = k % 2
                qv = qy[:, 512 * par:512 * (par + 1)]

                @on_se
                def _(scalar, i=i, sub=sub, k=k, qv=qv):
                    if k == 0:
                        scalar.wait_ge(VE.h, negb_ve)
                    if k >= 2:
                        scalar.wait_ge(VE.h, red_tok[k - 2])
                    scalar.wait_ge(SE.h, SE.v)
                    SE.inc(scalar.activation(
                        qv, xres[i][:, 512 * sub:512 * (sub + 1)], AF.Identity,
                        scale=rstd[:, i:i + 1], bias=negb[:, i:i + 1]))
                ap_se = SE.v

                @on_ve
                def _(vector, k=k, ap_se=ap_se, qv=qv):
                    vector.wait_ge(SE.h, ap_se)
                    vector.wait_ge(VE.h, VE.v)
                    if k == 0:
                        r = VE.inc(vector.reduce_max(amax_acc[:], qv, axis=AX,
                                                     apply_absolute_value=True))
                    else:
                        r = VE.inc(vector.reduce_max(amax_t2[:], qv, axis=AX,
                                                     apply_absolute_value=True))
                        vector.wait_ge(VE.h, r)
                        vector.wait_ge(VE.h, VE.v)
                        r = VE.inc(vector.tensor_max(amax_acc[:], amax_acc[:],
                                                     amax_t2[:]))
                red_tok[k] = VE.v
            ax_ve = VE.v

            def evacAX(vector, view):
                vector.wait_ge(VE.h, VE.v)
                return VE.inc(vector.tensor_copy(sc_row[0:1, :], view))
            emit_transpose(amax_acc[:], idf[:], 1,
                           pe_wait=(lambda t: t.wait_ge(VE.h, ax_ve)),
                           evac_fn=evacAX, n_out=128)

            @on_ve
            def _(vector):
                vector.wait_ge(VE.h, VE.v)
                VE.inc(vector.reduce_max(scal[:, 3:4], sc_row[:], axis=AX))
                vector.wait_ge(VE.h, VE.v)
                VE.inc(vector.tensor_scalar_mul(sc_row[:], triu[0:1, :], scal[0:1, 3:4]))
            ccrow_ve = VE.v

            @on_sy
            def _(sync):
                sync.wait_ge(VE.h, ccrow_ve)
                sync.wait_ge(DCC.h, DCC.v)
                DCC.dma(sync.dma_start(out=cc_in[cc_idx].ap(), in_=sc_row[:]))
            ccin_dma = DCC.v

            @on_gp
            def _(gpsimd):
                gpsimd.wait_ge(DCC.h, ccin_dma)
                CC.inc(gpsimd.collective_compute(
                    "AllReduce", ALU.max,
                    replica_groups=[list(range(N_CORES))],
                    ins=[cc_in[cc_idx].ap()],
                    outs=[cc_out[cc_idx].ap()]))
            cc_at = CC.v

            @on_sy
            def _(sync):
                sync.wait_ge(CC.h, cc_at)
                sync.wait_ge(DCC.h, DCC.v)
                DCC.dma(sync.dma_start(out=sc_row[:], in_=cc_out[cc_idx].ap()))
            ccout_dma = DCC.v

            @on_ve
            def _(vector):
                vector.wait_ge(DCC.h, ccout_dma)
                vector.wait_ge(VE.h, VE.v)
                VE.inc(vector.reciprocal(scal[:, 7:8], sc_row[0:1, 0:1]))
                vector.wait_ge(VE.h, VE.v)
                VE.inc(vector.tensor_scalar_mul(scal[:, 4:5], scal[:, 7:8], 448.0))
                vector.wait_ge(VE.h, VE.v)
                VE.inc(vector.tensor_tensor(scal[:, 5:6], sc_row[0:1, 0:1],
                                            scal[:, 1:2], ALU.mult))
                vector.wait_ge(VE.h, VE.v)
                VE.inc(vector.tensor_scalar_mul(scal[:, 6:7], scal[:, 5:6],
                                                1.0 / (448.0 * 448.0)))
            sxrs_ve = VE.v

            @on_pe
            def _(tensor):
                tensor.wait_ge(VE.h, sxrs_ve)
                PE.inc(tensor.matmul(ps_o[:, 0:1], triu[0:1, :], scal[0:1, 4:5],
                                     start=True, stop=True))
                PE.inc(tensor.matmul(ps_o[:, 512:513], triu[0:1, :], scal[0:1, 6:7],
                                     start=True, stop=True))
            sxb_pe = PE.v

            @on_ve
            def _(vector):
                vector.wait_ge(PE.h, sxb_pe)
                vector.wait_ge(VE.h, VE.v)
                VE.inc(vector.tensor_copy(sx_t[:], ps_o[:, 0:1]))
                vector.wait_ge(VE.h, VE.v)
                VE.inc(vector.tensor_copy(rs_t[:], ps_o[:, 512:513]))
                vector.wait_ge(VE.h, VE.v)
                VE.inc(vector.tensor_scalar_mul(nsc[:], rstd[:], sx_t[:]))
                vector.wait_ge(VE.h, VE.v)
                b1 = VE.inc(vector.tensor_scalar_mul(negb[:, NT:2 * NT],
                                                     negb[:, 0:NT], sx_t[:]))
                vector.wait_ge(VE.h, b1)
            base_ve = VE.v

            clamp_tok = {}
            sel_tok = {}
            ep_gate = [None, None]
            for i in range(NT):
                par = i % 2

                @on_se
                def _(scalar, i=i, par=par):
                    scalar.wait_ge(VE.h, base_ve)
                    if i >= 1:
                        scalar.wait_ge(VE.h, clamp_tok[i - 1])
                    if i >= 2:
                        scalar.wait_ge(VE.h, sel_tok[i - 2])
                    scalar.wait_ge(SE.h, SE.v)
                    SE.inc(scalar.activation(qy[:], xres[i][:], AF.Identity,
                                             scale=nsc[:, i:i + 1],
                                             bias=negb[:, NT + i:NT + i + 1]))
                    scalar.wait_ge(SE.h, SE.v)
                    SE.inc(scalar.activation(qf8bh[:], qy[:], AF.Identity,
                                             scale=0.5))
                    scalar.wait_ge(SE.h, SE.v)
                    SE.inc(scalar.activation(qf8b[par][:], qf8bh[:], AF.Identity,
                                             scale=2.0))
                se_b = SE.v

                @on_ve
                def _(vector, i=i, par=par, se_b=se_b):
                    vector.wait_ge(SE.h, se_b)
                    vector.wait_ge(VE.h, VE.v)
                    sq = VE.inc(vector.tensor_mul(qy2[par][:], qy[:], qy[:]))
                    vector.wait_ge(VE.h, sq)
                    vector.wait_ge(VE.h, VE.v)
                    VE.inc(vector.tensor_scalar(qcl[par][:], qy[:], 1.0, -1.0,
                                                ALU.min, ALU.max))
                clamp_tok[i] = VE.v

                @on_se
                def _(scalar, i=i, par=par):
                    scalar.wait_ge(VE.h, clamp_tok[i])
                    scalar.wait_ge(SE.h, SE.v)
                    SE.inc(scalar.activation(qf8a[:], qcl[par][:], AF.Identity))
                    scalar.wait_ge(SE.h, SE.v)
                    SE.inc(scalar.activation(qcl[par][:], qf8a[:], AF.Identity))
                se_d = SE.v

                @on_ve
                def _(vector, i=i, par=par, se_d=se_d):
                    vector.wait_ge(SE.h, se_d)
                    vector.wait_ge(VE.h, VE.v)
                    m = VE.inc(vector.tensor_single_scalar(
                        qmask[par][:], qy2[par][:], 1.0, ALU.is_lt))
                    vector.wait_ge(VE.h, m)
                    vector.wait_ge(VE.h, VE.v)
                    VE.inc(vector.select(xq_t[i][:], qmask[par][:], qcl[par][:],
                                         qf8b[par][:], add_drain=True))
                sel_tok[i] = VE.v

                for c in range(ND):
                    emit_transpose(xq_t[i][:, 128 * c:128 * (c + 1)], idb[:], 128,
                                   pe_wait=(lambda t, v=sel_tok[i]:
                                            t.wait_ge(VE.h, v)),
                                   dst_ap=xq_f[c][:, 128 * i:128 * (i + 1)],
                                   se_evac=False)
                xqf_tok = tp_last_ve()

                o0 = 1024 * par

                @on_pe
                def _(tensor, i=i, par=par, o0=o0, xqf_tok=xqf_tok):
                    twait(tensor, xqf_tok)
                    g = ep_gate[par]
                    if g is not None:
                        tensor.wait_ge(VE.h, g)
                    for c in range(ND):
                        for half in range(2):
                            tensor.matmul(
                                ps_o[:, o0 + 512 * half:o0 + 512 * (half + 1)],
                                xq_f[c][:, 128 * i:128 * (i + 1)],
                                wqT[c][:, 512 * half:512 * (half + 1)],
                                start=(c == 0), stop=(c == ND - 1))
                    PE.inc(tensor.matmul(ps_big[0:1, 256:257], triu[0:1, 0:1],
                                         triu[0:1, 0:1], start=True, stop=True))
                mm_pe = PE.v

                @on_ve
                def _(vector, i=i, o0=o0, mm_pe=mm_pe):
                    vector.wait_ge(PE.h, mm_pe)
                    for half in range(2):
                        vector.wait_ge(VE.h, VE.v)
                        VE.inc(vector.scalar_tensor_tensor(
                            ev32[:], ps_o[:, o0 + 512 * half:o0 + 512 * (half + 1)],
                            rs_t[:], bf_t[:, 512 * half:512 * (half + 1)],
                            ALU.mult, ALU.add))
                        vector.wait_ge(VE.h, VE.v)
                        VE.inc(vector.tensor_add(
                            xres[i][:, 512 * half:512 * (half + 1)],
                            xres[i][:, 512 * half:512 * (half + 1)], ev32[:]))
                ep_gate[par] = VE.v
            sF.close()

        if n_stages != -3:
            emit_ffn(0)

    if n_stages >= 3 or n_stages == -3:
        switch_slots(SLOTS4)
        s3 = ExitStack()
        e3 = s3.enter_context
        ln3f = [e3(nc.sbuf_tensor(f"ln3f{c}", [128, T], F32)) for c in range(ND)]
        wqk_s = e3(nc.sbuf_tensor("wqk_s", [128, 1024], F32))
        hq_sp = e3(nc.sbuf_tensor("hq_sp", [128, T], BF16))
        hk_sp = e3(nc.sbuf_tensor("hk_sp", [128, T], BF16))
        hqd_h = e3(nc.sbuf_tensor("hqd_h", [128, T], BF16))
        hqd_l = e3(nc.sbuf_tensor("hqd_l", [128, T], BF16))
        hkstk = e3(nc.sbuf_tensor("hkstk", [128, T], BF16))
        vbs = [e3(nc.sbuf_tensor(f"vbs{k}", [128, NT * 65], F32)) for k in range(2)]

        ln_stats(first=(n_stages == -3))
        ln3_rstd = VE.v

        for i in range(NT):
            for sub in range(2):
                @on_ve
                def _(vector, i=i, sub=sub, _wp=tp_last_pe()):
                    if i == 0 and sub == 0:
                        vector.wait_ge(VE.h, ln3_rstd)
                    if _wp:
                        vector.wait_ge(PE.h, _wp)
                    vector.wait_ge(VE.h, VE.v)
                    VE.inc(vector.tensor_scalar(
                        ev32[:], xres[i][:, 512 * sub:512 * (sub + 1)],
                        mu[:, i:i + 1], rstd[:, i:i + 1],
                        ALU.subtract, ALU.mult))
                ve3 = VE.v
                for cl in range(4):
                    c = 4 * sub + cl
                    emit_transpose(ev32[:, 128 * cl:128 * (cl + 1)], idf[:], 128,
                                   pe_wait=(lambda t, v=ve3: t.wait_ge(VE.h, v)),
                                   dst_ap=ln3f[c][:, 128 * i:128 * (i + 1)],
                                   se_evac=False)
        ln3_done = tp_last_ve()

        wb_gate = [None]
        qk_ve_gate = [None]
        qk_pe_gate = [None]
        dup_gate = [None]
        gitem_b = [gitem[0] if n_stages >= 1 else 0]
        sem_s_base = [sem_s.v]
        sem_pv_base = [sem_pv.v]
        o_gate_b = [None]
        switch_slots(SLOTS2)
        for p in range(H // 2):
            hA, hB = 2 * p, 2 * p + 1
            @on_sy
            def _(sync, p=p, hA=hA, hB=hB, _pg=qk_pe_gate[0]):
                if _pg is not None:
                    sync.wait_ge(PE.h, _pg)
                if p == 0:
                    twait(sync, ln3_done)
                sync.wait_ge(DWS.h, DWS.v)
                for c in range(8):
                    DWS.dma(sync.dma_start(
                        out=wqk_s[:, 128 * c:128 * c + 64],
                        in_=Wqkv_d[128 * c:128 * (c + 1), 192 * hA + 128:192 * hA + 192]))
                    DWS.dma(sync.dma_start(
                        out=wqk_s[:, 128 * c + 64:128 * (c + 1)],
                        in_=Wqkv_d[128 * c:128 * (c + 1), 192 * hB + 128:192 * hB + 192]))
            wv_l = DWS.v

            @on_ve
            def _(vector, wv_l=wv_l, p=p, _pv=sem_pv.v, _pg=qk_pe_gate[0]):
                vector.wait_ge(DWS.h, wv_l)
                vector.wait_ge(sem_pv.h, _pv)
                if _pg is not None:
                    vector.wait_ge(PE.h, _pg)
                if p == 0:
                    twait(vector, ln3_done)
                vector.wait_ge(VE.h, VE.v)
                VE.inc(vector.memset(
                    vbs[0].rearrange("p (i u) -> p i u", u=65)[:, :, 64:65], 1.0))
                vector.wait_ge(VE.h, VE.v)
                VE.inc(vector.memset(
                    vbs[1].rearrange("p (i u) -> p i u", u=65)[:, :, 64:65], 1.0))
            wvb_split = VE.v

            @on_pe
            def _(tensor, wvb_split=wvb_split, wv_l=wv_l):
                tensor.wait_ge(VE.h, wvb_split)
                tensor.wait_ge(DWS.h, wv_l)
                for c in range(8):
                    for nq in range(4):
                        n0 = 512 * nq
                        tensor.matmul(ps_o[:, n0:n0 + 512],
                                      wqk_s[:, 128 * c:128 * (c + 1)],
                                      ln3f[c][:, n0:n0 + 512],
                                      start=(c == 0), stop=(c == 7))
                PE.inc(tensor.matmul(ps_big[0:1, 256:257], triu[0:1, 0:1],
                                     triu[0:1, 0:1], start=True, stop=True))
            vb_pe = PE.v
            qk_pe_gate[0] = vb_pe

            for nq in range(4):
                @on_ve
                def _(vector, nq=nq, vb_pe=vb_pe):
                    vector.wait_ge(PE.h, vb_pe)
                    vector.wait_ge(VE.h, VE.v)
                    VE.inc(vector.tensor_copy(ev32[:], ps_o[:, 512 * nq:512 * (nq + 1)]))
                evd = VE.v
                for tloc in range(4):
                    i = 4 * nq + tloc

                    def evacVB(vector, view, i=i):
                        vector.wait_ge(VE.h, VE.v)
                        VE.inc(vector.tensor_copy(
                            vbs[0][:, 65 * i:65 * i + 64], view[:, 0:64]))
                        vector.wait_ge(VE.h, VE.v)
                        return VE.inc(vector.tensor_copy(
                            vbs[1][:, 65 * i:65 * i + 64], view[:, 64:128]))
                    emit_transpose(ev32[:, 128 * tloc:128 * (tloc + 1)], idf[:], 128,
                                   pe_wait=(lambda t, v=evd: t.wait_ge(VE.h, v)),
                                   evac_fn=evacVB)
            vb_done = tp_last_ve()

            for hi, h in enumerate([hA, hB]):
                @on_sy
                def _(sync, h=h, vb_done=vb_done, _pg=qk_pe_gate[0]):
                    twait(sync, vb_done)
                    if _pg is not None:
                        sync.wait_ge(PE.h, _pg)
                    sync.wait_ge(DWS.h, DWS.v)
                    for c in range(8):
                        DWS.dma(sync.dma_start(
                            out=wqk_s[:, 128 * c:128 * (c + 1)],
                            in_=Wqkv_d[128 * c:128 * (c + 1), 192 * h:192 * h + 128]))
                wqk_ld = DWS.v

                @on_pe
                def _(tensor, wqk_ld=wqk_ld, _xv=sem_exp.v, _tv=tp_last_ve()):
                    tensor.wait_ge(DWS.h, wqk_ld)
                    tensor.wait_ge(sem_exp.h, _xv)
                    twait(tensor, _tv)
                    for c in range(8):
                        for nq in range(4):
                            n0 = 512 * nq
                            tensor.matmul(ps_big[:, n0:n0 + 512],
                                          wqk_s[:, 128 * c:128 * (c + 1)],
                                          ln3f[c][:, n0:n0 + 512],
                                          start=(c == 0), stop=(c == 7))
                    PE.inc(tensor.matmul(ps_o[0:1, 256:257], triu[0:1, 0:1],
                                         triu[0:1, 0:1], start=True, stop=True))
                qk_pe = PE.v
                qk_pe_gate[0] = qk_pe

                @on_ve
                def _(vector, qk_pe=qk_pe):
                    vector.wait_ge(PE.h, qk_pe)
                    vector.wait_ge(VE.h, VE.v)
                    a_ = VE.inc(vector.tensor_scalar_mul(hq_sp[0:64, :],
                                                         ps_big[0:64, :], 0.125))
                    vector.wait_ge(VE.h, a_)
                    vector.wait_ge(VE.h, VE.v)
                    VE.inc(vector.scalar_tensor_tensor(
                        hk_sp[0:64, :], ps_big[0:64, :], 0.125, hq_sp[0:64, :],
                        ALU.mult, ALU.subtract))
                    vector.wait_ge(VE.h, VE.v)
                    b_ = VE.inc(vector.tensor_copy(hk_sp[64:128, :],
                                                   ps_big[64:128, :]))
                    vector.wait_ge(VE.h, b_)
                    vector.wait_ge(VE.h, VE.v)
                    VE.inc(vector.tensor_sub(hq_sp[64:128, :], ps_big[64:128, :],
                                             hk_sp[64:128, :]))
                split_ve = VE.v

                for (dst, srcs) in [
                        (hqd_h, (hq_sp[0:64, :], hq_sp[0:64, :], idb[0:64, 0:64])),
                        (hqd_l, (hk_sp[0:64, :], hk_sp[0:64, :], idb[0:64, 0:64])),
                        (hkstk, (hk_sp[64:128, :], hq_sp[64:128, :], idb[64:128, 64:128])),
                ]:
                    src_top, src_bot, identd = srcs

                    @on_pe
                    def _(tensor, split_ve=split_ve, src_top=src_top,
                          src_bot=src_bot, identd=identd, _xv=sem_exp.v):
                        tensor.wait_ge(VE.h, max(split_ve, dup_gate[0] or 0))
                        tensor.wait_ge(sem_exp.h, _xv)
                        for nq in range(4):
                            n0 = 512 * nq
                            tensor.matmul(ps_big[0:64, n0:n0 + 512], identd,
                                          src_top[:, n0:n0 + 512],
                                          start=True, stop=True)
                            tensor.matmul(ps_big[64:128, n0:n0 + 512], identd,
                                          src_bot[:, n0:n0 + 512],
                                          start=True, stop=True,
                                          skip_group_check=True)
                        PE.inc(tensor.matmul(ps_o[0:1, 256:257], triu[0:1, 0:1],
                                             triu[0:1, 0:1], start=True, stop=True))
                    dup_pe = PE.v

                    @on_ve
                    def _(vector, dup_pe=dup_pe, dst=dst):
                        vector.wait_ge(PE.h, dup_pe)
                        vector.wait_ge(VE.h, VE.v)
                        VE.inc(vector.tensor_copy(dst[:], ps_big[:]))
                    dup_gate[0] = VE.v
                stk_ready = VE.v

                items_b = [(j, nchunk) for j in range(NT) for nchunk in range(4)]
                base_n = gitem_b[0]
                prev_o = o_gate_b[0]

                @on_pe
                def _(tensor, h=h, hi=hi, items_b=items_b, base_n=base_n,
                      prev_o=prev_o, stk_ready=stk_ready):
                    tensor.wait_ge(VE.h, stk_ready)

                    def emit_S(it):
                        j, nchunk = items_b[it]
                        n = base_n + it
                        par = n % 2
                        if n >= 2:
                            tensor.wait_ge(sem_exp.h, n - 1)
                        tensor.matmul(
                            ps_big[:, 512 * par:512 * (par + 1)],
                            hkstk[:, 128 * j:128 * (j + 1)],
                            hqd_h[:, 512 * nchunk:512 * (nchunk + 1)],
                            start=True, stop=False)
                        sem_s.inc(tensor.matmul(
                            ps_big[:, 512 * par:512 * (par + 1)],
                            hkstk[:, 128 * j:128 * (j + 1)],
                            hqd_l[:, 512 * nchunk:512 * (nchunk + 1)],
                            start=False, stop=True))

                    def emit_PV(it):
                        j, nchunk = items_b[it]
                        n = base_n + it
                        par = n % 2
                        tensor.wait_ge(sem_exp.h, n + 1)
                        if it == 0 and prev_o is not None:
                            twait(tensor, prev_o)
                        sem_pv.inc(tensor.matmul(
                            ps_o[0:65, 512 * nchunk:512 * nchunk + 512],
                            vbs[hi][:, 65 * j:65 * j + 65],
                            pT[par][:, 0:512],
                            start=(j == 0), stop=(j == NT - 1)))

                    emit_S(0)
                    for it in range(1, len(items_b)):
                        emit_S(it)
                        emit_PV(it - 1)
                    emit_PV(len(items_b) - 1)
                gitem_b[0] += len(items_b)

                @on_se
                def _(scalar, items_b=items_b, base_n=base_n):
                    for it in range(len(items_b)):
                        n = base_n + it
                        par = n % 2
                        scalar.wait_ge(sem_s.h, sem_s_base[0] + it + 1)
                        if n >= 2:
                            scalar.wait_ge(sem_pv.h, sem_pv_base[0] + it - 1)
                        sem_exp.inc(scalar.activation(
                            pT[par][:, 0:512], ps_big[:, 512 * par:512 * (par + 1)],
                            AF.Exp))
                sem_s_base[0] += len(items_b)
                sem_pv_base[0] += len(items_b)

                @on_ve
                def _(vector, h=h):
                    vector.wait_ge(sem_pv.h, sem_pv.v)
                    vector.wait_ge(VE.h, VE.v)
                o_pv_done = VE.v
                for qc in range(4):
                    @on_ve
                    def _(vector, qc=qc):
                        vector.wait_ge(sem_pv.h, sem_pv.v)
                        vector.wait_ge(VE.h, VE.v)
                        VE.inc(vector.tensor_copy(
                            ev32[0:65, :], ps_o[0:65, 512 * qc:512 * (qc + 1)]))
                    evo = VE.v
                    for tloc in range(4):
                        i = 4 * qc + tloc

                        def evacOB(vector, view, i=i, h=h):
                            vector.wait_ge(VE.h, VE.v)
                            r = VE.inc(vector.reciprocal(rd1[:], view[:, 64:65]))
                            vector.wait_ge(VE.h, r)
                            return VE.inc(vector.scalar_tensor_tensor(
                                xres[i][:, 64 * h:64 * h + 64], view[:, 0:64],
                                rd1[:], xres[i][:, 64 * h:64 * h + 64],
                                ALU.mult, ALU.add))
                        emit_transpose(ev32[0:65, 128 * tloc:128 * (tloc + 1)],
                                       idf[0:65, 0:65], 128,
                                       pe_wait=(lambda t, v=evo: t.wait_ge(VE.h, v)),
                                       evac_fn=evacOB, n_out=65)
                o_gate_b[0] = tp_last_ve()
            wb_gate[0] = VE.v
        s3.close()

    if n_stages >= 4:
        emit_ffn(1)
    if n_stages >= 2 or n_stages == -2:
        s2.close()

    @on_sy
    def _(sync):
        sync.wait_ge(VE.h, VE.v)
        sync.wait_ge(PE.h, PE.v)
        sync.wait_ge(SE.h, SE.v)
        sync.wait_ge(sem_pv.h, sem_pv.v)
        sync.wait_ge(sem_exp.h, sem_exp.v)
        sync.wait_ge(sem_msk.h, sem_msk.v)
        for i in range(NT):
            DCC.dma(sync.dma_start(out=out_d[128 * i:128 * (i + 1), :],
                                   in_=xres[i][:]))
        sync.wait_ge(DCC.h, DCC.v)

    st.close()
    return nc


def make_in_maps(inputs):
    x = np.ascontiguousarray(np.asarray(inputs["L0_attn_a_x"], dtype=np.float32))
    w = {
        "W1q": np.ascontiguousarray(inputs["L0_attn_a_W1_q"], dtype=np.float32),
        "W2q": np.ascontiguousarray(inputs["L0_attn_a_W2_q"], dtype=np.float32),
        "W1k": np.ascontiguousarray(inputs["L0_attn_a_W1_k"], dtype=np.float32),
        "W2k": np.ascontiguousarray(inputs["L0_attn_a_W2_k"], dtype=np.float32),
        "Wv": np.ascontiguousarray(inputs["L0_attn_a_W_v"], dtype=np.float32),
        "bv": np.ascontiguousarray(np.asarray(inputs["L0_attn_a_b_v"]).reshape(1, D), dtype=np.float32),
        "Wo": np.ascontiguousarray(inputs["L0_attn_a_W_o"], dtype=np.float32),
        "bo": np.ascontiguousarray(np.asarray(inputs["L0_attn_a_b_o"]).reshape(1, D), dtype=np.float32),
        "Wqkv": np.ascontiguousarray(inputs["L0_attn_b_MhWqkv"], dtype=np.float32),
        "Wf": np.ascontiguousarray(inputs["L0_ffn_weight"], dtype=np.float32),
        "bf": np.ascontiguousarray(np.asarray(inputs["L0_ffn_bias"]).reshape(1, D), dtype=np.float32),
    }
    return [dict(w, x=np.ascontiguousarray(x[c])) for c in range(N_CORES)]


def kernel(**inputs):
    nc = build_nc(4)
    in_maps = make_in_maps(inputs)
    res = run_bass_kernel_spmd(nc, in_maps, core_ids=list(range(N_CORES)))
    return np.stack([np.asarray(r["out"]) for r in res.results], axis=0)


# version 22
# speedup vs baseline: 1.1149x; 1.0031x over previous
from contextlib import ExitStack

import numpy as np
import concourse.bass as bass
import concourse.mybir as mybir
from concourse.bass_utils import run_bass_kernel_spmd

F32 = mybir.dt.float32
BF16 = mybir.dt.bfloat16
SIM_WAITS = False
F8E4 = mybir.dt.float8e4
ALU = mybir.AluOpType
AF = mybir.ActivationFunctionType
AX = mybir.AxisListType.X

T = 2048
D = 1024
H = 16
NT = T // 128
ND = D // 128
NB = 8
BAND = T // NB
TH = T // 2
SCALE_A = 6.0 ** (-0.5)
EPS = 1e-5
N_CORES = 8

PAIRS = [(0, 1), (0, 2), (0, 3), (1, 2), (1, 3), (2, 3)]
J6SIGN = [1.0, -1.0, 1.0, 1.0, -1.0, 1.0]
PASSES3 = [(0, 0), (0, 1), (1, 0)]


class Sem:
    def __init__(self, handle):
        self.h = handle
        self.v = 0

    def inc(self, inst, k=1):
        inst.then_inc(self.h, k)
        self.v += k
        return self.v

    def dma(self, inst):
        return self.inc(inst, 16)


def build_nc(n_stages=4):
    nc = bass.Bass("TRN2", target_bir_lowering=False)

    x_in = nc.dram_tensor("x", [T, D], F32, kind="ExternalInput").ap()
    W1q_d = nc.dram_tensor("W1q", [2 * D, 64], F32, kind="ExternalInput").ap()
    W2q_d = nc.dram_tensor("W2q", [2 * D, 64], F32, kind="ExternalInput").ap()
    W1k_d = nc.dram_tensor("W1k", [D, 64], F32, kind="ExternalInput").ap()
    W2k_d = nc.dram_tensor("W2k", [D, 64], F32, kind="ExternalInput").ap()
    Wv_d = nc.dram_tensor("Wv", [D, D], F32, kind="ExternalInput").ap()
    bv_d = nc.dram_tensor("bv", [1, D], F32, kind="ExternalInput").ap()
    Wo_d = nc.dram_tensor("Wo", [D, D], F32, kind="ExternalInput").ap()
    bo_d = nc.dram_tensor("bo", [1, D], F32, kind="ExternalInput").ap()
    Wqkv_d = nc.dram_tensor("Wqkv", [D, 3 * D], F32, kind="ExternalInput").ap()
    Wf_d = nc.dram_tensor("Wf", [D, D], F32, kind="ExternalInput").ap()
    bf_d = nc.dram_tensor("bf", [1, D], F32, kind="ExternalInput").ap()
    out_d = nc.dram_tensor("out", [T, D], F32, kind="ExternalOutput").ap()
    cc_in = [nc.dram_tensor(f"cc_in{i}", [1, 128], F32) for i in range(2)]
    cc_out = [nc.dram_tensor(f"cc_out{i}", [1, 128], F32) for i in range(2)]

    ident_c = nc.inline_tensor(np.eye(128, dtype=np.float32), name="identc").ap()
    _G = np.zeros((128, 8 * 96), dtype=np.float32)
    for h_ in range(16):
        for c_ in range(6):
            a_, b_ = PAIRS[c_]
            _G[4 * h_ + a_, 0 * 96 + 6 * h_ + c_] = 1.0
            _G[64 + 4 * h_ + b_, 1 * 96 + 6 * h_ + c_] = 1.0
            _G[4 * h_ + b_, 2 * 96 + 6 * h_ + c_] = 1.0
            _G[64 + 4 * h_ + a_, 3 * 96 + 6 * h_ + c_] = 1.0
            ap_, bp_ = PAIRS[5 - c_]
            s_ = float(J6SIGN[c_])
            _G[4 * h_ + ap_, 4 * 96 + 6 * h_ + c_] = s_
            _G[64 + 4 * h_ + bp_, 5 * 96 + 6 * h_ + c_] = 1.0
            _G[4 * h_ + bp_, 6 * 96 + 6 * h_ + c_] = s_
            _G[64 + 4 * h_ + ap_, 7 * 96 + 6 * h_ + c_] = 1.0
    gmat_c = nc.inline_tensor(_G, name="gmatc").ap()
    triu_c = nc.inline_tensor(np.triu(np.ones((128, 128), dtype=np.float32)),
                              name="triuc").ap()


    st = ExitStack()
    e = st.enter_context

    xres = [e(nc.sbuf_tensor(f"xres{i}", [128, D], F32)) for i in range(NT)]
    idf = e(nc.sbuf_tensor("idf", [128, 128], F32))
    idb = e(nc.sbuf_tensor("idb", [128, 128], BF16))
    triu = e(nc.sbuf_tensor("triu", [128, 128], F32))
    mu = e(nc.sbuf_tensor("mu", [128, NT], F32))
    var_e = e(nc.sbuf_tensor("var_e", [128, NT], F32))
    rstd = e(nc.sbuf_tensor("rstd", [128, NT], F32))
    nsc = e(nc.sbuf_tensor("nsc", [128, NT], F32))
    bnst = e(nc.sbuf_tensor("bnst", [128, 12], F32))
    bnmv = e(nc.sbuf_tensor("bnmv", [128, 2], F32))
    pT = [e(nc.sbuf_tensor(f"pT{p}", [128, 512], F32)) for p in range(2)]
    ev32 = e(nc.sbuf_tensor("ev32", [128, 512], F32))
    sc_row = e(nc.sbuf_tensor("sc_row", [1, 128], F32))
    scal = e(nc.sbuf_tensor("scal", [1, 8], F32))
    amax_acc = e(nc.sbuf_tensor("amax_acc", [128, 1], F32))
    sx_t = e(nc.sbuf_tensor("sx_t", [128, 1], F32))
    rs_t = e(nc.sbuf_tensor("rs_t", [128, 1], F32))
    rd1 = e(nc.sbuf_tensor("rd1", [128, 1], F32))

    ps_big = e(nc.psum_tensor("ps_big", [128, T], F32))
    ps_o = e(nc.psum_tensor("ps_o", [128, T], F32))
    def _slot(ap):
        return (ap, ap.bitcast(BF16))

    SLOT_B2 = _slot(ps_big[:, 1024:1152])
    SLOT_B3 = _slot(ps_big[:, 1536:1664])
    SLOT_B0 = _slot(ps_big[:, 0:128])
    SLOT_B1 = _slot(ps_big[:, 512:640])
    SLOTS4 = [SLOT_B2, SLOT_B3, SLOT_B0, SLOT_B1]
    SLOTS2 = [SLOT_B2, SLOT_B3]
    SLOTS_A = SLOTS4

    DLS, DWS, PE, VE, SE, CC, DCC = (Sem(e(nc.semaphore(n))) for n in
                                     ["dls", "dws", "pe", "ve", "se", "cc", "dcc"])
    sem_s = Sem(e(nc.semaphore("sem_s")))
    sem_exp = Sem(e(nc.semaphore("sem_exp")))
    sem_msk = Sem(e(nc.semaphore("sem_msk")))
    sem_pv = Sem(e(nc.semaphore("sem_pv")))

    block = e(nc.Block())
    on_sy, on_pe, on_ve, on_se, on_gp = (block.sync, block.tensor, block.vector,
                                         block.scalar, block.gpsimd)

    @on_sy
    def _(sync):
        for i in range(NT):
            DLS.dma(sync.dma_start(out=xres[i][:], in_=x_in[128 * i:128 * (i + 1), :]))
        DWS.dma(sync.dma_start(out=idf[:], in_=ident_c))
        DWS.dma(sync.dma_start(out=triu[:], in_=triu_c))
    x_loaded = DLS.v
    const_loaded = DWS.v

    @on_ve
    def _(vector):
        vector.wait_ge(DWS.h, const_loaded)
        vector.wait_ge(VE.h, VE.v)
        VE.inc(vector.tensor_copy(idb[:], idf[:]))
        vector.wait_ge(VE.h, VE.v)
        m = VE.inc(vector.memset(scal[:], 0.0))
        vector.wait_ge(VE.h, m)
        vector.wait_ge(VE.h, VE.v)
        VE.inc(vector.tensor_scalar_add(scal[:, 0:1], scal[:, 0:1], 448.0))

    def ln_stats(first=False):
        @on_ve
        def _(vector):
            if first:
                vector.wait_ge(DLS.h, x_loaded)
            for i in range(NT):
                vector.wait_ge(VE.h, VE.v)
                a = VE.inc(vector.bn_stats(bnst[:, 0:6], xres[i][:, 0:512]))
                vector.wait_ge(VE.h, VE.v)
                b = VE.inc(vector.bn_stats(bnst[:, 6:12], xres[i][:, 512:1024]))
                vector.wait_ge(VE.h, b)
                vector.wait_ge(VE.h, VE.v)
                c = VE.inc(vector.bn_aggr(bnmv[:], bnst[:]))
                vector.wait_ge(VE.h, c)
                vector.wait_ge(VE.h, VE.v)
                VE.inc(vector.tensor_copy(mu[:, i:i + 1], bnmv[:, 0:1]))
                vector.wait_ge(VE.h, VE.v)
                d = VE.inc(vector.tensor_scalar_add(var_e[:, i:i + 1], bnmv[:, 1:2], EPS))
                vector.wait_ge(VE.h, d)
        ve_at = VE.v

        @on_se
        def _(scalar):
            scalar.wait_ge(VE.h, ve_at)
            s1_ = SE.inc(scalar.activation(rstd[:], var_e[:], AF.Ln))
            scalar.wait_ge(SE.h, s1_)
            SE.inc(scalar.activation(rstd[:], rstd[:], AF.Exp, scale=-0.5))
        se_at = SE.v

        @on_ve
        def _(vector):
            vector.wait_ge(SE.h, se_at)
            for _ in range(2):
                vector.wait_ge(VE.h, VE.v)
                a = VE.inc(vector.tensor_mul(nsc[:], rstd[:], rstd[:]))
                vector.wait_ge(VE.h, a)
                vector.wait_ge(VE.h, VE.v)
                b = VE.inc(vector.tensor_mul(nsc[:], nsc[:], var_e[:]))
                vector.wait_ge(VE.h, b)
                vector.wait_ge(VE.h, VE.v)
                c = VE.inc(vector.tensor_scalar(nsc[:], nsc[:], -0.5, 1.5,
                                                ALU.mult, ALU.add))
                vector.wait_ge(VE.h, c)
                vector.wait_ge(VE.h, VE.v)
                d = VE.inc(vector.tensor_mul(rstd[:], rstd[:], nsc[:]))
                vector.wait_ge(VE.h, d)
        return VE.v

    tp = {"n": 0, "pe": [], "ve": [], "slots": SLOTS_A, "last": {}}

    def twait(eng, tok):
        if isinstance(tok, tuple):
            eng.wait_ge(SE.h if tok[0] else VE.h, tok[1])
        elif tok:
            eng.wait_ge(VE.h, tok)

    def switch_slots(new_slots):
        tp["slots"] = new_slots

    def emit_transpose(src_ap, ident, m_out, pe_wait=None, evac_fn=None,
                       dst_ap=None, n_out=128, se_evac=False, gmat=None):
        k = tp["n"]
        slot = tp["slots"][k % len(tp["slots"])]
        sid = id(slot)
        war_tok = tp["last"].get(sid)
        view = (slot[1] if src_ap.dtype == BF16 else slot[0])[0:m_out, 0:n_out]

        @on_pe
        def _(tensor, k=k, src=src_ap, ident=ident, view=view, war_tok=war_tok,
              gmat=gmat):
            twait(tensor, war_tok)
            if pe_wait is not None:
                pe_wait(tensor)
            if gmat is not None:
                tp["pe"].append(PE.inc(tensor.matmul(view, src, gmat,
                                                     start=True, stop=True)))
            else:
                tp["pe"].append(PE.inc(tensor.transpose(view, src, ident)))

        if se_evac:
            @on_se
            def _(scalar, k=k, dst=dst_ap, view=view):
                scalar.wait_ge(PE.h, tp["pe"][k])
                scalar.wait_ge(SE.h, SE.v)
                tok = (True, SE.inc(scalar.activation(dst, view, AF.Copy)))
                tp["ve"].append(tok)
                tp["last"][sid] = tok
        else:
            @on_ve
            def _(vector, k=k, dst=dst_ap, view=view):
                vector.wait_ge(PE.h, tp["pe"][k])
                if evac_fn is not None:
                    tok = (False, evac_fn(vector, view))
                else:
                    tok = (False, VE.inc(vector.tensor_copy(dst, view)))
                tp["ve"].append(tok)
                tp["last"][sid] = tok
        tp["n"] += 1

    def tp_last_ve():
        return tp["ve"][-1] if tp["ve"] else 0

    def tp_last_pe():
        return tp["pe"][-1] if tp["pe"] else 0

    if n_stages >= 1 and n_stages not in (-2, -3):
        s1 = ExitStack()
        e1 = s1.enter_context
        vph = [e1(nc.sbuf_tensor(f"vph{i}", [128, H * 65], F32)) for i in range(NT)]
        bv_hc = e1(nc.sbuf_tensor("bv_hc", [64, H], F32))
        sv_f = e1(nc.sbuf_tensor("sv_f", [128, ND], F32))

        sRq = ExitStack()
        eRq = sRq.enter_context
        q12_f = eRq(nc.sbuf_tensor("q12_f", [128, T], F32, side="right"))
        k12_f = eRq(nc.sbuf_tensor("k12_f", [128, T], F32, side="right"))

        sRw = ExitStack()
        eRw = sRw.enter_context
        wqf = eRw(nc.sbuf_tensor("wqf", [128, 16 * 128], F32, side="right"))
        wkf = eRw(nc.sbuf_tensor("wkf", [128, 8 * 128], F32, side="right"))

        ln1_rstd = ln_stats(first=True)

        @on_sy
        def _(sync):
            sync.wait_ge(DWS.h, DWS.v)
            for c in range(16):
                DWS.dma(sync.dma_start(out=wqf[:, 128 * c:128 * c + 64],
                                       in_=W1q_d[128 * c:128 * (c + 1), :]))
                DWS.dma(sync.dma_start(out=wqf[:, 128 * c + 64:128 * (c + 1)],
                                       in_=W2q_d[128 * c:128 * (c + 1), :]))
            sync.wait_ge(DWS.h, DWS.v)
            for c in range(8):
                DWS.dma(sync.dma_start(out=wkf[:, 128 * c:128 * c + 64],
                                       in_=W1k_d[128 * c:128 * (c + 1), :]))
                DWS.dma(sync.dma_start(out=wkf[:, 128 * c + 64:128 * (c + 1)],
                                       in_=W2k_d[128 * c:128 * (c + 1), :]))
        wqk_loaded = DWS.v

        for h in range(H):
            @on_sy
            def _(sync, h=h):
                if h > 0:
                    sync.wait_ge(PE.h, tp["pe"][-1])
                sync.wait_ge(DWS.h, DWS.v)
                DWS.dma(sync.dma_start(out=sc_row[0:1, 0:64],
                                       in_=bv_d[0:1, 64 * h:64 * h + 64]))
            bvh_dma = DWS.v

            def evacB(vector, view, h=h):
                vector.wait_ge(VE.h, VE.v)
                return VE.inc(vector.tensor_copy(bv_hc[:, h:h + 1], view[:, 0:1]))
            emit_transpose(sc_row[0:1, 0:64], idf[0:1, 0:1], 64,
                           pe_wait=(lambda t, v=bvh_dma: t.wait_ge(DWS.h, v)),
                           evac_fn=evacB, n_out=1)
        bv_ready = tp_last_ve()

        sRl = ExitStack()
        eRl = sRl.enter_context
        lnf = [eRl(nc.sbuf_tensor(f"lnf{c}", [128, TH], F32, side="right"))
               for c in range(ND)]
        wvs = eRl(nc.sbuf_tensor("wvs", [128, 8 * 128], F32, side="right"))

        @on_ve
        def _(vector):
            vector.wait_ge(DLS.h, x_loaded)
            for i in range(NT):
                vector.wait_ge(VE.h, VE.v)
                VE.inc(vector.memset(vph[i][:], 1.0))
        vph_init = VE.v

        wv_gate = [None]
        q12_done_half = []
        for half in range(2):
            t0 = TH * half
            for iloc in range(NT // 2):
                i = (NT // 2) * half + iloc

                if half == 1 and iloc == 0:
                    @on_ve
                    def _(vector, _t0k=tp_last_ve()):
                        twait(vector, _t0k)
                        for c in range(ND):
                            vector.wait_ge(VE.h, VE.v)
                            VE.inc(vector.tensor_copy(sv_f[:, c:c + 1],
                                                      lnf[c][:, TH - 1:TH]))
                        vector.wait_ge(PE.h, half_consumed[0])
                for sub in range(2):
                    @on_ve
                    def _(vector, i=i, half=half, iloc=iloc, sub=sub,
                          _wp=tp_last_pe()):
                        if half == 0 and iloc == 0 and sub == 0:
                            vector.wait_ge(VE.h, ln1_rstd)
                        if _wp:
                            vector.wait_ge(PE.h, _wp)
                        vector.wait_ge(VE.h, VE.v)
                        w = VE.inc(vector.tensor_scalar(
                            ev32[:], xres[i][:, 512 * sub:512 * (sub + 1)],
                            mu[:, i:i + 1], rstd[:, i:i + 1],
                            ALU.subtract, ALU.mult))
                        vector.wait_ge(VE.h, w)
                    ve_split = VE.v
                    for cl in range(4):
                        c = 4 * sub + cl
                        emit_transpose(ev32[:, 128 * cl:128 * (cl + 1)], idf[:], 128,
                                       pe_wait=(lambda t, v=ve_split: t.wait_ge(VE.h, v)),
                                       dst_ap=lnf[c][:, 128 * iloc:128 * (iloc + 1)],
                                       se_evac=False)
            ln_done = tp_last_ve()

            @on_pe
            def _(tensor, half=half, t0=t0, ln_done=ln_done):
                twait(tensor, ln_done)
                tensor.wait_ge(DWS.h, wqk_loaded)

                for c in range(8):
                    for nq in range(2):
                        n0l = 512 * nq
                        n0g = t0 + n0l
                        tensor.matmul(ps_big[:, n0g:n0g + 512],
                                      wqf[:, 128 * c:128 * (c + 1)],
                                      lnf[c][:, n0l:n0l + 512],
                                      start=(c == 0), stop=False)
                for c in range(8):
                    ws = wqf[:, 128 * (c + 8):128 * (c + 9)]
                    xt = lnf[c]
                    for nq in range(2):
                        n0l = 512 * nq
                        n0g = t0 + n0l
                        stopf = (c == 7)
                        if n0g == 0:
                            tensor.matmul(ps_big[:, 1:512], ws,
                                          xt[:, 0:511], start=False, stop=stopf)
                        elif n0l == 0:
                            tensor.matmul(ps_big[:, t0:t0 + 1], ws,
                                          sv_f[:, c:c + 1],
                                          start=False, stop=False)
                            tensor.matmul(ps_big[:, t0 + 1:t0 + 512], ws,
                                          xt[:, 0:511], start=False, stop=stopf)
                        else:
                            tensor.matmul(ps_big[:, n0g:n0g + 512], ws,
                                          xt[:, n0l - 1:n0l + 511],
                                          start=False, stop=stopf)
                PE.inc(tensor.matmul(ps_o[0:1, 1536:1537], triu[0:1, 0:1],
                                     triu[0:1, 0:1], start=True, stop=True))
            q12_pe = PE.v

            @on_ve
            def _(vector, t0=t0, q12_pe=q12_pe):
                vector.wait_ge(PE.h, q12_pe)
                vector.wait_ge(VE.h, VE.v)
                VE.inc(vector.tensor_copy(q12_f[:, t0:t0 + TH], ps_big[:, t0:t0 + TH]))
            q12_ev = VE.v

            @on_pe
            def _(tensor, t0=t0, q12_ev=q12_ev):
                tensor.wait_ge(VE.h, q12_ev)
                for c in range(8):
                    for nq in range(2):
                        n0l = 512 * nq
                        tensor.matmul(ps_big[:, t0 + n0l:t0 + n0l + 512],
                                      wkf[:, 128 * c:128 * (c + 1)],
                                      lnf[c][:, n0l:n0l + 512],
                                      start=(c == 0), stop=(c == 7))
                PE.inc(tensor.matmul(ps_o[0:1, 1536:1537], triu[0:1, 0:1],
                                     triu[0:1, 0:1], start=True, stop=True))
            k12_pe = PE.v

            @on_ve
            def _(vector, t0=t0, k12_pe=k12_pe):
                vector.wait_ge(PE.h, k12_pe)
                vector.wait_ge(VE.h, VE.v)
                VE.inc(vector.tensor_copy(k12_f[:, t0:t0 + TH], ps_big[:, t0:t0 + TH]))
            k12_ev = VE.v

            for p in range(H // 2):
                @on_sy
                def _(sync, p=p, half=half, _wg=wv_gate[0]):
                    if _wg is not None:
                        sync.wait_ge(PE.h, _wg)
                    sync.wait_ge(DWS.h, DWS.v)
                    for c in range(8):
                        DWS.dma(sync.dma_start(
                            out=wvs[:, 128 * c:128 * (c + 1)],
                            in_=Wv_d[128 * c:128 * (c + 1), 128 * p:128 * (p + 1)]))
                wv_loaded = DWS.v

                @on_pe
                def _(tensor, wv_loaded=wv_loaded, p=p, half=half, k12_ev=k12_ev):
                    tensor.wait_ge(DWS.h, wv_loaded)
                    if p == 0:
                        tensor.wait_ge(VE.h, k12_ev)
                    for c in range(8):
                        for nq in range(2):
                            n0l = 512 * nq
                            tensor.matmul(ps_o[:, n0l:n0l + 512],
                                          wvs[:, 128 * c:128 * (c + 1)],
                                          lnf[c][:, n0l:n0l + 512],
                                          start=(c == 0), stop=(c == 7))
                    PE.inc(tensor.matmul(ps_big[0:1, 256:257], triu[0:1, 0:1],
                                         triu[0:1, 0:1], start=True, stop=True))
                vp_pe = PE.v
                wv_gate[0] = vp_pe

                for nq in range(2):
                    @on_ve
                    def _(vector, nq=nq, vp_pe=vp_pe):
                        vector.wait_ge(PE.h, vp_pe)
                        if nq == 0:
                            vector.wait_ge(VE.h, vph_init)
                        vector.wait_ge(VE.h, VE.v)
                        VE.inc(vector.tensor_copy(ev32[:], ps_o[:, 512 * nq:512 * (nq + 1)]))
                    ev_done = VE.v
                    for tloc in range(4):
                        i = (NT // 2) * half + 4 * nq + tloc
                        hA, hB = 2 * p, 2 * p + 1

                        def evacV(vector, view, i=i, hA=hA, hB=hB):
                            vector.wait_ge(VE.h, VE.v)
                            VE.inc(vector.tensor_copy(
                                vph[i][:, 65 * hA:65 * hA + 64], view[:, 0:64]))
                            vector.wait_ge(VE.h, VE.v)
                            return VE.inc(vector.tensor_copy(
                                vph[i][:, 65 * hB:65 * hB + 64], view[:, 64:128]))
                        emit_transpose(ev32[:, 128 * tloc:128 * (tloc + 1)], idf[:], 128,
                                       pe_wait=(lambda t, v=ev_done: t.wait_ge(VE.h, v)),
                                       evac_fn=evacV)
            half_consumed = [PE.v]
        vprime_done = tp_last_ve()

        sRl.close()
        sRw.close()

        sRe = ExitStack()
        eRe = sRe.enter_context
        rep = [eRe(nc.sbuf_tensor(f"rep{r}", [128, 96], F32, side="right"))
               for r in range(2)]
        gmats = eRe(nc.sbuf_tensor("gmats", [128, 8 * 96], F32, side="right"))
        L32 = eRe(nc.sbuf_tensor("L32", [128, 96], F32, side="right"))
        Lsq = eRe(nc.sbuf_tensor("Lsq", [128, 96], F32, side="right"))
        Lss = eRe(nc.sbuf_tensor("Lss", [128, 16], F32, side="right"))
        Lrs = eRe(nc.sbuf_tensor("Lrs", [128, 16], F32, side="right"))
        Lt1 = eRe(nc.sbuf_tensor("Lt1", [128, 16], F32, side="right"))
        Lh_s = eRe(nc.sbuf_tensor("Lh_s", [128, 96], BF16, side="right"))
        Ll_s = eRe(nc.sbuf_tensor("Ll_s", [128, 96], BF16, side="right"))
        stk_big = eRe(nc.sbuf_tensor("stk_big", [128, 576], BF16, side="right"))

        lqstk = [e1(nc.sbuf_tensor(f"lqstk{g}", [128, T], BF16)) for g in range(6)]
        jkstk = [e1(nc.sbuf_tensor(f"jkstk{g}", [128, T], BF16)) for g in range(6)]

        @on_ve
        def _(vector):
            vector.wait_ge(VE.h, VE.v)
            VE.inc(vector.memset(stk_big[:], 0.0))
        stk_init = VE.v

        @on_sy
        def _(sync):
            sync.wait_ge(DWS.h, DWS.v)
            DWS.dma(sync.dma_start(out=gmats[:], in_=gmat_c))
        gm_ld = DWS.v

        def emit_exterior_tile(i):
            for which in ("q", "k"):
                fsrc = q12_f if which == "q" else k12_f
                goff = 0 if which == "q" else 4
                for r in range(2):
                    def evacT(vector, view, r=r):
                        vector.wait_ge(VE.h, VE.v)
                        return VE.inc(vector.tensor_copy(rep[r][:], view))
                    emit_transpose(fsrc[:, 128 * i:128 * (i + 1)], idf[:], 128,
                                   pe_wait=(lambda t: (twait(t, vprime_done),
                                                       t.wait_ge(DWS.h, gm_ld))),
                                   evac_fn=evacT, n_out=96,
                                   gmat=gmats[:, 96 * (goff + r):96 * (goff + r + 1)])
                tqk = tp_last_ve()

                @on_ve
                def _(vector, tqk=tqk):
                    twait(vector, tqk)
                    vector.wait_ge(VE.h, VE.v)
                    p1 = VE.inc(vector.tensor_mul(L32[:], rep[0][:], rep[1][:]))
                    vector.wait_ge(VE.h, p1)

                for r in range(2):
                    def evacT(vector, view, r=r):
                        vector.wait_ge(VE.h, VE.v)
                        return VE.inc(vector.tensor_copy(rep[r][:], view))
                    emit_transpose(fsrc[:, 128 * i:128 * (i + 1)], idf[:], 128,
                                   pe_wait=(lambda t: t.wait_ge(DWS.h, gm_ld)),
                                   evac_fn=evacT, n_out=96,
                                   gmat=gmats[:, 96 * (goff + r + 2):96 * (goff + r + 3)])
                tqk2 = tp_last_ve()

                @on_ve
                def _(vector, tqk2=tqk2):
                    twait(vector, tqk2)
                    vector.wait_ge(VE.h, VE.v)
                    p2 = VE.inc(vector.tensor_mul(Lsq[:], rep[0][:], rep[1][:]))
                    vector.wait_ge(VE.h, p2)
                    vector.wait_ge(VE.h, VE.v)
                    p3 = VE.inc(vector.tensor_sub(L32[:], L32[:], Lsq[:]))
                    vector.wait_ge(VE.h, p3)
                    vector.wait_ge(VE.h, VE.v)
                    w3 = VE.inc(vector.tensor_mul(Lsq[:], L32[:], L32[:]))
                    vector.wait_ge(VE.h, w3)
                    vector.wait_ge(VE.h, VE.v)
                    w4 = VE.inc(vector.reduce_sum(
                        Lss.rearrange("p (h u) -> p h u", u=1),
                        Lsq.rearrange("p (h c) -> p h c", c=6), axis=AX))
                    vector.wait_ge(VE.h, w4)
                    vector.wait_ge(VE.h, VE.v)
                    w6 = VE.inc(vector.tensor_scalar_max(Lss[:], Lss[:], 1e-24))
                    vector.wait_ge(VE.h, w6)
                ve_a = VE.v

                @on_se
                def _(scalar, ve_a=ve_a):
                    scalar.wait_ge(VE.h, ve_a)
                    s1_ = SE.inc(scalar.activation(Lrs[:], Lss[:], AF.Ln,
                                                   scale=2.0 ** 40))
                    scalar.wait_ge(SE.h, s1_)
                    SE.inc(scalar.activation(Lrs[:], Lrs[:], AF.Exp, scale=-0.5))
                se_a = SE.v

                @on_ve
                def _(vector, se_a=se_a, which=which):
                    vector.wait_ge(SE.h, se_a)
                    vector.wait_ge(VE.h, VE.v)
                    u_ = VE.inc(vector.tensor_scalar_mul(Lrs[:], Lrs[:], 2.0 ** 20))
                    vector.wait_ge(VE.h, u_)
                    for _ in range(1):
                        vector.wait_ge(VE.h, VE.v)
                        a2 = VE.inc(vector.tensor_mul(Lt1[:], Lrs[:], Lrs[:]))
                        vector.wait_ge(VE.h, a2)
                        vector.wait_ge(VE.h, VE.v)
                        b2 = VE.inc(vector.tensor_mul(Lt1[:], Lt1[:], Lss[:]))
                        vector.wait_ge(VE.h, b2)
                        vector.wait_ge(VE.h, VE.v)
                        c2 = VE.inc(vector.tensor_scalar(Lt1[:], Lt1[:], -0.5, 1.5,
                                                         ALU.mult, ALU.add))
                        vector.wait_ge(VE.h, c2)
                        vector.wait_ge(VE.h, VE.v)
                        d2 = VE.inc(vector.tensor_mul(Lrs[:], Lrs[:], Lt1[:]))
                        vector.wait_ge(VE.h, d2)
                    if which == "q":
                        vector.wait_ge(VE.h, VE.v)
                        s2 = VE.inc(vector.tensor_scalar_mul(Lrs[:], Lrs[:], SCALE_A))
                        vector.wait_ge(VE.h, s2)
                    vector.wait_ge(VE.h, VE.v)
                    jm = VE.inc(vector.tensor_mul(
                        L32.rearrange("p (h c) -> p h c", c=6),
                        L32.rearrange("p (h c) -> p h c", c=6),
                        Lrs.rearrange("p (h u) -> p h u", u=1)
                        .broadcast_to([128, 16, 6])))
                    vector.wait_ge(VE.h, jm)
                    vector.wait_ge(VE.h, VE.v)
                    wv_ = VE.inc(vector.tensor_copy(Lh_s[:], L32[:]))
                    vector.wait_ge(VE.h, wv_)
                    vector.wait_ge(VE.h, VE.v)
                    VE.inc(vector.tensor_sub(Ll_s[:], L32[:], Lh_s[:]))
                ve_b = VE.v

                blocks = [0, 0, 1, 1] if which == "q" else [0, 1, 0, 1]
                stks = lqstk if which == "q" else jkstk

                @on_ve
                def _(vector, ve_b=ve_b, blocks=blocks, _pg=tp_last_pe()):
                    vector.wait_ge(VE.h, max(ve_b, stk_init))
                    if _pg:
                        vector.wait_ge(PE.h, _pg)
                    dstv = stk_big.rearrange("p (gh s) -> p gh s", s=32)
                    for b_i, half_ in enumerate(blocks):
                        srcT = Lh_s if half_ == 0 else Ll_s
                        vector.wait_ge(VE.h, VE.v)
                        VE.inc(vector.tensor_copy(
                            dstv[:, 0:16, 6 * b_i:6 * b_i + 6],
                            srcT.rearrange("p (h c) -> p h c", c=6)))
                asm_ve = VE.v

                for g in range(6):
                    def evacS(vector, view, stks=stks, g=g, i=i):
                        vector.wait_ge(VE.h, VE.v)
                        return VE.inc(vector.tensor_copy(
                            stks[g][0:96, 128 * i:128 * (i + 1)], view))
                    emit_transpose(stk_big[:, 96 * g:96 * (g + 1)], idb[:], 96,
                                   pe_wait=(lambda t, v=asm_ve: t.wait_ge(VE.h, v)),
                                   evac_fn=evacS)

        for _i in range(NT):
            emit_exterior_tile(_i)
        stacks_done = tp_last_ve()
        ext_pe = tp_last_pe()
        sRe.close()
        sRq.close()

        sRb = ExitStack()
        eRb = sRb.enter_context
        o_f = [eRb(nc.sbuf_tensor(f"o_f{c}", [128, BAND], F32, side="right"))
               for c in range(ND)]
        rds = eRb(nc.sbuf_tensor("rds", [128, BAND], F32, side="right"))
        os_f = eRb(nc.sbuf_tensor("os_f", [64, BAND], F32, side="right"))
        wsc = eRb(nc.sbuf_tensor("wsc", [128, 128], F32, side="right"))
        bo_t = eRb(nc.sbuf_tensor("bo_t", [128, D], F32, side="right"))
        wof_s = [eRb(nc.sbuf_tensor(f"wof_s{p}", [128, 128], F32, side="right"))
                 for p in range(4)]

        @on_sy
        def _(sync):
            sync.wait_ge(PE.h, ext_pe)
            sync.wait_ge(DWS.h, DWS.v)
            DWS.dma(sync.dma_start(out=bo_t[0:1, :], in_=bo_d))
        bo_row = DWS.v

        @on_pe
        def _(tensor):
            tensor.wait_ge(DWS.h, bo_row)
            for halfq in range(2):
                PE.inc(tensor.matmul(ps_o[:, 1024 + 512 * halfq:1024 + 512 * (halfq + 1)],
                                     triu[0:1, :], bo_t[0:1, 512 * halfq:512 * (halfq + 1)],
                                     start=True, stop=True))
        bo_pe = PE.v

        @on_ve
        def _(vector):
            vector.wait_ge(PE.h, bo_pe)
            vector.wait_ge(VE.h, VE.v)
            VE.inc(vector.tensor_copy(bo_t[:, :], ps_o[:, 1024:2048]))
        bo_ready = VE.v

        switch_slots(SLOTS2)
        o_written = {}
        gitem = [0]
        msk_count_at = {}
        wo_gate = [{}]
        mv_gate = [None]

        for b in range(NB):
            q0 = BAND * b
            for h in range(H):
                g, s_ = h // 3, h % 3
                jmax = 2 * b + 1
                items = [(j, max(0, 128 * j - q0), BAND - max(0, 128 * j - q0))
                         for j in range(jmax + 1)]
                base_n = gitem[0]
                prev_o = o_written.get((b, h - 1) if h > 0 else (b - 1, H - 1))
                _mc = sem_msk.v
                for it, (j, off, N) in enumerate(items):
                    if 128 * j >= q0:
                        _mc += 1
                        msk_count_at[base_n + it] = _mc

                @on_pe
                def _(tensor, b=b, h=h, g=g, s_=s_, q0=q0, items=items,
                      base_n=base_n, prev_o=prev_o):
                    if b == 0 and h == 0:
                        twait(tensor, stacks_done)
                        twait(tensor, vprime_done)
                        tensor.wait_ge(VE.h, bo_ready)

                    def emit_S(it):
                        j, off, N = items[it]
                        n = base_n + it
                        par = n % 2
                        if n >= 2:
                            tensor.wait_ge(sem_exp.h, n - 1)
                        sem_s.inc(tensor.matmul(
                            ps_big[:, 512 * par:512 * par + N],
                            jkstk[g][32 * s_:32 * s_ + 24, 128 * j:128 * (j + 1)],
                            lqstk[g][32 * s_:32 * s_ + 24, q0 + off:q0 + BAND],
                            start=True, stop=True))

                    def emit_PV(it):
                        j, off, N = items[it]
                        n = base_n + it
                        par = n % 2
                        tensor.wait_ge(sem_exp.h, n + 1)
                        if 128 * j >= q0:
                            tensor.wait_ge(sem_msk.h, msk_count_at[n])
                        if it == 0 and prev_o is not None:
                            twait(tensor, prev_o)
                        sem_pv.inc(tensor.matmul(
                            ps_o[0:65, off:BAND],
                            vph[j][:, 65 * h:65 * h + 65],
                            pT[par][:, 0:N],
                            start=(it == 0), stop=(it == len(items) - 1)))

                    emit_S(0)
                    for it in range(1, len(items)):
                        emit_S(it)
                        emit_PV(it - 1)
                    emit_PV(len(items) - 1)
                gitem[0] += len(items)

                @on_se
                def _(scalar, items=items, base_n=base_n):
                    for it, (j, off, N) in enumerate(items):
                        n = base_n + it
                        par = n % 2
                        scalar.wait_ge(sem_s.h, n + 1)
                        if n >= 2:
                            scalar.wait_ge(sem_pv.h, n - 1)
                        sem_exp.inc(scalar.activation(
                            pT[par][:, 0:N], ps_big[:, 512 * par:512 * par + N],
                            AF.Exp))

                @on_ve
                def _(vector, items=items, base_n=base_n, q0=q0):
                    for it, (j, off, N) in enumerate(items):
                        n = base_n + it
                        if 128 * j < q0:
                            continue
                        par = n % 2
                        vector.wait_ge(sem_exp.h, n + 1)
                        vector.wait_ge(VE.h, VE.v)
                        sem_msk.inc(vector.tensor_mul(
                            pT[par][:, 0:128], pT[par][:, 0:128], triu[:]))

                @on_ve
                def _(vector):
                    vector.wait_ge(sem_pv.h, sem_pv.v)
                    vector.wait_ge(VE.h, VE.v)
                    r = VE.inc(vector.reciprocal(rds[64:65, :], ps_o[64:65, 0:BAND]))
                    vector.wait_ge(VE.h, r)
                rd_ready = VE.v

                @on_pe
                def _(tensor, rd_ready=rd_ready):
                    tensor.wait_ge(VE.h, rd_ready)
                    PE.inc(tensor.matmul(ps_o[0:64, 1024:1024 + BAND],
                                         triu[64:65, 64:128], rds[64:65, :],
                                         start=True, stop=True))
                bc_pe = PE.v

                @on_ve
                def _(vector, h=h, bc_pe=bc_pe):
                    vector.wait_ge(PE.h, bc_pe)
                    c = h // 2
                    rb = (h % 2) * 64
                    vector.wait_ge(VE.h, VE.v)
                    a = VE.inc(vector.tensor_copy(rds[0:64, :], ps_o[0:64, 1024:1024 + BAND]))
                    vector.wait_ge(VE.h, a)
                    vector.wait_ge(VE.h, VE.v)
                    m1 = VE.inc(vector.tensor_mul(
                        ev32[0:64, 0:BAND], ps_o[0:64, 0:BAND], rds[0:64, :]))
                    vector.wait_ge(VE.h, m1)
                    vector.wait_ge(VE.h, VE.v)
                    if rb == 0:
                        VE.inc(vector.tensor_scalar_add(
                            o_f[c][0:64, :], ev32[0:64, 0:BAND],
                            bv_hc[:, h:h + 1]))
                    else:
                        VE.inc(vector.tensor_scalar_add(
                            os_f[:], ev32[0:64, 0:BAND],
                            bv_hc[:, h:h + 1]))
                norm_ve = VE.v

                if h % 2 == 1:
                    mvv = ps_big[:, 1664:1792]

                    for hc in range(2):
                        @on_pe
                        def _(tensor, norm_ve=norm_ve, _tv=tp_last_ve(),
                              hc=hc, _mg=mv_gate[0]):
                            tensor.wait_ge(VE.h, norm_ve)
                            twait(tensor, _tv)
                            if _mg:
                                tensor.wait_ge(VE.h, _mg)
                            PE.inc(tensor.matmul(
                                mvv[64:128, 0:128], idf[0:64, 0:64],
                                os_f[:, 128 * hc:128 * (hc + 1)],
                                start=True, stop=True))
                        mv1 = PE.v

                        @on_ve
                        def _(vector, mv1=mv1, h=h, hc=hc):
                            c = h // 2
                            vector.wait_ge(PE.h, mv1)
                            vector.wait_ge(VE.h, VE.v)
                            VE.inc(vector.tensor_copy(
                                o_f[c][64:128, 128 * hc:128 * (hc + 1)],
                                mvv[64:128, 0:128]))
                        mv_gate[0] = VE.v
                o_written[(b, h)] = VE.v

            wo_ev_map = {}

            def emit_wo_T(f):
                fp = f % 2
                e0 = 1536 + 256 * fp if fp == 0 else 1280
                for tloc in range(BAND // 128):
                    i = (BAND // 128) * b + tloc

                    def evacW(vector, view, i=i, f=f):
                        vector.wait_ge(VE.h, VE.v)
                        t1 = VE.inc(vector.tensor_add(
                            wsc[:], view, bo_t[:, 128 * f:128 * (f + 1)]))
                        vector.wait_ge(VE.h, t1)
                        vector.wait_ge(VE.h, VE.v)
                        return VE.inc(vector.tensor_add(
                            xres[i][:, 128 * f:128 * (f + 1)],
                            xres[i][:, 128 * f:128 * (f + 1)], wsc[:]))
                    emit_transpose(ev32[:, 256 * fp + 128 * tloc:
                                        256 * fp + 128 * (tloc + 1)], idf[:], 128,
                                   pe_wait=(lambda t, v=wo_ev_map[f]:
                                            t.wait_ge(VE.h, v)),
                                   evac_fn=evacW)

            for f in range(ND):
                fp = f % 2
                r0 = 1536 if fp == 0 else 1280
                for c in range(8):
                    par = c % 4

                    @on_sy
                    def _(sync, f=f, c=c, par=par, b=b):
                        gate = wo_gate[0].get(par)
                        sync.wait_ge(PE.h, ext_pe if gate is None else gate)
                        sync.wait_ge(DWS.h, DWS.v)
                        DWS.dma(sync.dma_start(
                            out=wof_s[par][:],
                            in_=Wo_d[128 * c:128 * (c + 1), 128 * f:128 * (f + 1)]))
                    wo_ld = DWS.v

                    @on_pe
                    def _(tensor, wo_ld=wo_ld, c=c, par=par, b=b, f=f, r0=r0):
                        tensor.wait_ge(DWS.h, wo_ld)
                        if c == 0:
                            tensor.wait_ge(VE.h, o_written[(b, H - 1)])
                            if f >= 2:
                                tensor.wait_ge(VE.h, wo_ev_map[f - 2])
                        PE.inc(tensor.matmul(ps_o[:, r0:r0 + BAND],
                                             wof_s[par][:], o_f[c][:],
                                             start=(c == 0), stop=(c == 7)))
                    wo_gate[0][par] = PE.v
                wo_pe = PE.v

                @on_ve
                def _(vector, wo_pe=wo_pe, fp=fp, r0=r0):
                    vector.wait_ge(PE.h, wo_pe)
                    vector.wait_ge(VE.h, VE.v)
                    VE.inc(vector.tensor_copy(ev32[:, 256 * fp:256 * fp + BAND],
                                              ps_o[:, r0:r0 + BAND]))
                wo_ev_map[f] = VE.v

                if f >= 1:
                    emit_wo_T(f - 1)
            emit_wo_T(ND - 1)
        sRb.close()
        s1.close()


    attn_done_gates = []

    if n_stages >= 2 or n_stages == -2:
        switch_slots(SLOTS4)
        s2 = ExitStack()
        e2 = s2.enter_context
        wqT = [e2(nc.sbuf_tensor(f"wqT{c}", [128, D], BF16)) for c in range(ND)]
        amax_t2 = e2(nc.sbuf_tensor("amax_t2", [128, 1], F32))
        bf_t = e2(nc.sbuf_tensor("bf_t", [128, D], F32))

        sW = ExitStack()
        eW = sW.enter_context
        wstage = eW(nc.sbuf_tensor("wstage", [128, 8 * 1024], F32, side="right"))
        wqrows = eW(nc.sbuf_tensor("wqrows", [128, 8 * 1024], BF16, side="right"))
        qys = eW(nc.sbuf_tensor("qys", [128, 1024], F32, side="right"))
        qy2s = eW(nc.sbuf_tensor("qy2s", [128, 1024], F32, side="right"))
        qcls = eW(nc.sbuf_tensor("qcls", [128, 1024], F32, side="right"))
        qf8as = eW(nc.sbuf_tensor("qf8as", [128, 1024], F8E4, side="right"))
        qf8bhs = eW(nc.sbuf_tensor("qf8bhs", [128, 1024], F8E4, side="right"))
        qf8bs = eW(nc.sbuf_tensor("qf8bs", [128, 1024], F32, side="right"))
        qmasks = eW(nc.sbuf_tensor("qmasks", [128, 1024], mybir.dt.uint8, side="right"))

        @on_sy
        def _(sync):
            sync.wait_ge(DWS.h, DWS.v)
            sync.wait_ge(VE.h, VE.v)
            for c in range(8):
                DWS.dma(sync.dma_start(out=wstage[:, 1024 * c:1024 * (c + 1)],
                                       in_=Wf_d[128 * c:128 * (c + 1), :]))
        wf_loaded = DWS.v

        @on_ve
        def _(vector):
            vector.wait_ge(DWS.h, wf_loaded)
            vector.wait_ge(VE.h, VE.v)
            VE.inc(vector.reduce_max(amax_acc[:], wstage[:, 0:1024], axis=AX,
                                     apply_absolute_value=True))
            for c in range(1, 8):
                vector.wait_ge(VE.h, VE.v)
                VE.inc(vector.reduce_max(amax_t2[:], wstage[:, 1024 * c:1024 * (c + 1)],
                                         axis=AX, apply_absolute_value=True))
                vector.wait_ge(VE.h, VE.v)
                VE.inc(vector.tensor_max(amax_acc[:], amax_acc[:], amax_t2[:]))
        aw_ve = VE.v

        def evacAW(vector, view):
            vector.wait_ge(VE.h, VE.v)
            return VE.inc(vector.tensor_copy(sc_row[0:1, :], view))
        emit_transpose(amax_acc[:], idf[:], 1,
                       pe_wait=(lambda t: t.wait_ge(VE.h, aw_ve)),
                       evac_fn=evacAW, n_out=128)

        @on_ve
        def _(vector):
            vector.wait_ge(VE.h, VE.v)
            VE.inc(vector.reduce_max(scal[:, 1:2], sc_row[:], axis=AX))
            vector.wait_ge(VE.h, VE.v)
            VE.inc(vector.reciprocal(scal[:, 7:8], scal[:, 1:2]))
            vector.wait_ge(VE.h, VE.v)
            VE.inc(vector.tensor_scalar_mul(scal[:, 2:3], scal[:, 7:8], 448.0))
        sw_ve = VE.v

        @on_pe
        def _(tensor):
            tensor.wait_ge(VE.h, sw_ve)
            PE.inc(tensor.matmul(ps_o[:, 0:1], triu[0:1, :], scal[0:1, 2:3],
                                 start=True, stop=True))
        swb_pe = PE.v

        @on_ve
        def _(vector):
            vector.wait_ge(PE.h, swb_pe)
            vector.wait_ge(VE.h, VE.v)
            VE.inc(vector.tensor_copy(rd1[:], ps_o[:, 0:1]))
        sw_ready = VE.v

        @on_ve
        def _(vector):
            vector.wait_ge(VE.h, sw_ready)
            for c in range(8):
                W_ = wstage[:, 1024 * c:1024 * (c + 1)]
                Q_ = wqrows[:, 1024 * c:1024 * (c + 1)]
                vector.wait_ge(VE.h, VE.v)
                VE.inc(vector.tensor_scalar_mul(qys[:], W_, rd1[:]))
                vector.wait_ge(VE.h, VE.v)
                VE.inc(vector.tensor_scalar_mul(qy2s[:], qys[:], 0.5))
                vector.wait_ge(VE.h, VE.v)
                VE.inc(vector.tensor_scalar(qcls[:], qys[:], 1.0, -1.0, ALU.min, ALU.max))
                vector.wait_ge(VE.h, VE.v)
                VE.inc(vector.tensor_copy(qf8as[:], qcls[:]))
                vector.wait_ge(VE.h, VE.v)
                VE.inc(vector.tensor_copy(qf8bhs[:], qy2s[:]))
                vector.wait_ge(VE.h, VE.v)
                VE.inc(vector.tensor_scalar_mul(qf8bs[:], qf8bhs[:], 2.0))
                vector.wait_ge(VE.h, VE.v)
                VE.inc(vector.tensor_mul(qmasks[:], qys[:], qys[:]))
                vector.wait_ge(VE.h, VE.v)
                VE.inc(vector.tensor_single_scalar(qmasks[:], qmasks[:], 1.0, ALU.is_lt))
                vector.wait_ge(VE.h, VE.v)
                VE.inc(vector.tensor_copy(qcls[:], qf8as[:]))
                vector.wait_ge(VE.h, VE.v)
                VE.inc(vector.select(Q_, qmasks[:], qcls[:], qf8bs[:], add_drain=True))
        wq_quant = VE.v

        for c1 in range(8):
            for c2 in range(8):
                emit_transpose(wqrows[:, 1024 * c1 + 128 * c2:1024 * c1 + 128 * (c2 + 1)],
                               idb[:], 128,
                               pe_wait=(lambda t: t.wait_ge(VE.h, wq_quant)),
                               dst_ap=wqT[c2][:, 128 * c1:128 * (c1 + 1)],
                               se_evac=False)
        wqT_done = tp_last_ve()

        @on_sy
        def _(sync):
            sync.wait_ge(DWS.h, DWS.v)
            DWS.dma(sync.dma_start(out=bf_t[0:1, :], in_=bf_d))
        bfr = DWS.v

        @on_pe
        def _(tensor):
            tensor.wait_ge(DWS.h, bfr)
            twait(tensor, wqT_done)
            for halfq in range(2):
                PE.inc(tensor.matmul(ps_o[:, 1024 + 512 * halfq:1024 + 512 * (halfq + 1)],
                                     triu[0:1, :], bf_t[0:1, 512 * halfq:512 * (halfq + 1)],
                                     start=True, stop=True))
        bf_pe = PE.v

        @on_ve
        def _(vector):
            vector.wait_ge(PE.h, bf_pe)
            vector.wait_ge(VE.h, VE.v)
            VE.inc(vector.tensor_copy(bf_t[:, :], ps_o[:, 1024:2048]))
        bf_ready = VE.v
        sW.close()

        def emit_ffn(cc_idx):
            switch_slots(SLOTS4)
            sF = ExitStack()
            eF = sF.enter_context
            qy = eF(nc.sbuf_tensor(f"qy{cc_idx}", [128, D], F32, side="right"))
            qy2 = [eF(nc.sbuf_tensor(f"qy2{cc_idx}_{p}", [128, D], F32, side="right"))
                   for p in range(2)]
            qcl = [eF(nc.sbuf_tensor(f"qcl{cc_idx}_{p}", [128, D], F32, side="right"))
                   for p in range(2)]
            qf8a = eF(nc.sbuf_tensor(f"qf8a{cc_idx}", [128, D], F8E4, side="right"))
            qf8bh = eF(nc.sbuf_tensor(f"qf8bh{cc_idx}", [128, D], F8E4, side="right"))
            qf8b = [eF(nc.sbuf_tensor(f"qf8b{cc_idx}_{p}", [128, D], F32, side="right"))
                    for p in range(2)]
            qmask = [eF(nc.sbuf_tensor(f"qmask{cc_idx}_{p}", [128, D], mybir.dt.uint8,
                                       side="right")) for p in range(2)]
            negb = eF(nc.sbuf_tensor(f"negb{cc_idx}", [128, 2 * NT], F32, side="right"))
            xq_t = [eF(nc.sbuf_tensor(f"xq_t{cc_idx}_{i}", [128, D], BF16, side="right"))
                    for i in range(NT)]
            xq_f = [eF(nc.sbuf_tensor(f"xq_f{cc_idx}_{c}", [128, T], BF16, side="right"))
                    for c in range(ND)]

            ln_stats(first=(n_stages == -2 and cc_idx == 0))
            rstd_at = VE.v

            @on_ve
            def _(vector):
                vector.wait_ge(VE.h, rstd_at)
                vector.wait_ge(VE.h, VE.v)
                a = VE.inc(vector.tensor_mul(negb[:, 0:NT], mu[:], rstd[:]))
                vector.wait_ge(VE.h, a)
                vector.wait_ge(VE.h, VE.v)
                VE.inc(vector.tensor_scalar_mul(negb[:, 0:NT], negb[:, 0:NT], -1.0))
            negb_ve = VE.v

            red_tok = {}
            for k in range(2 * NT):
                i, sub = k // 2, k % 2
                par = k % 2
                qv = qy[:, 512 * par:512 * (par + 1)]

                @on_se
                def _(scalar, i=i, sub=sub, k=k, qv=qv):
                    if k == 0:
                        scalar.wait_ge(VE.h, negb_ve)
                    if k >= 2:
                        scalar.wait_ge(VE.h, red_tok[k - 2])
                    scalar.wait_ge(SE.h, SE.v)
                    SE.inc(scalar.activation(
                        qv, xres[i][:, 512 * sub:512 * (sub + 1)], AF.Identity,
                        scale=rstd[:, i:i + 1], bias=negb[:, i:i + 1]))
                ap_se = SE.v

                @on_ve
                def _(vector, k=k, ap_se=ap_se, qv=qv):
                    vector.wait_ge(SE.h, ap_se)
                    vector.wait_ge(VE.h, VE.v)
                    if k == 0:
                        r = VE.inc(vector.reduce_max(amax_acc[:], qv, axis=AX,
                                                     apply_absolute_value=True))
                    else:
                        r = VE.inc(vector.reduce_max(amax_t2[:], qv, axis=AX,
                                                     apply_absolute_value=True))
                        vector.wait_ge(VE.h, r)
                        vector.wait_ge(VE.h, VE.v)
                        r = VE.inc(vector.tensor_max(amax_acc[:], amax_acc[:],
                                                     amax_t2[:]))
                red_tok[k] = VE.v
            ax_ve = VE.v

            def evacAX(vector, view):
                vector.wait_ge(VE.h, VE.v)
                return VE.inc(vector.tensor_copy(sc_row[0:1, :], view))
            emit_transpose(amax_acc[:], idf[:], 1,
                           pe_wait=(lambda t: t.wait_ge(VE.h, ax_ve)),
                           evac_fn=evacAX, n_out=128)

            @on_ve
            def _(vector):
                vector.wait_ge(VE.h, VE.v)
                VE.inc(vector.reduce_max(scal[:, 3:4], sc_row[:], axis=AX))
                vector.wait_ge(VE.h, VE.v)
                VE.inc(vector.tensor_scalar_mul(sc_row[:], triu[0:1, :], scal[0:1, 3:4]))
            ccrow_ve = VE.v

            @on_sy
            def _(sync):
                sync.wait_ge(VE.h, ccrow_ve)
                sync.wait_ge(DCC.h, DCC.v)
                DCC.dma(sync.dma_start(out=cc_in[cc_idx].ap(), in_=sc_row[:]))
            ccin_dma = DCC.v

            @on_gp
            def _(gpsimd):
                gpsimd.wait_ge(DCC.h, ccin_dma)
                CC.inc(gpsimd.collective_compute(
                    "AllReduce", ALU.max,
                    replica_groups=[list(range(N_CORES))],
                    ins=[cc_in[cc_idx].ap()],
                    outs=[cc_out[cc_idx].ap()]))
            cc_at = CC.v

            @on_sy
            def _(sync):
                sync.wait_ge(CC.h, cc_at)
                sync.wait_ge(DCC.h, DCC.v)
                DCC.dma(sync.dma_start(out=sc_row[:], in_=cc_out[cc_idx].ap()))
            ccout_dma = DCC.v

            @on_ve
            def _(vector):
                vector.wait_ge(DCC.h, ccout_dma)
                vector.wait_ge(VE.h, VE.v)
                VE.inc(vector.reciprocal(scal[:, 7:8], sc_row[0:1, 0:1]))
                vector.wait_ge(VE.h, VE.v)
                VE.inc(vector.tensor_scalar_mul(scal[:, 4:5], scal[:, 7:8], 448.0))
                vector.wait_ge(VE.h, VE.v)
                VE.inc(vector.tensor_tensor(scal[:, 5:6], sc_row[0:1, 0:1],
                                            scal[:, 1:2], ALU.mult))
                vector.wait_ge(VE.h, VE.v)
                VE.inc(vector.tensor_scalar_mul(scal[:, 6:7], scal[:, 5:6],
                                                1.0 / (448.0 * 448.0)))
            sxrs_ve = VE.v

            @on_pe
            def _(tensor):
                tensor.wait_ge(VE.h, sxrs_ve)
                PE.inc(tensor.matmul(ps_o[:, 0:1], triu[0:1, :], scal[0:1, 4:5],
                                     start=True, stop=True))
                PE.inc(tensor.matmul(ps_o[:, 512:513], triu[0:1, :], scal[0:1, 6:7],
                                     start=True, stop=True))
            sxb_pe = PE.v

            @on_ve
            def _(vector):
                vector.wait_ge(PE.h, sxb_pe)
                vector.wait_ge(VE.h, VE.v)
                VE.inc(vector.tensor_copy(sx_t[:], ps_o[:, 0:1]))
                vector.wait_ge(VE.h, VE.v)
                VE.inc(vector.tensor_copy(rs_t[:], ps_o[:, 512:513]))
                vector.wait_ge(VE.h, VE.v)
                VE.inc(vector.tensor_scalar_mul(nsc[:], rstd[:], sx_t[:]))
                vector.wait_ge(VE.h, VE.v)
                b1 = VE.inc(vector.tensor_scalar_mul(negb[:, NT:2 * NT],
                                                     negb[:, 0:NT], sx_t[:]))
                vector.wait_ge(VE.h, b1)
            base_ve = VE.v

            clamp_tok = {}
            sel_tok = {}
            ep_gate = [None, None]
            for i in range(NT):
                par = i % 2

                @on_se
                def _(scalar, i=i, par=par):
                    scalar.wait_ge(VE.h, base_ve)
                    if i >= 1:
                        scalar.wait_ge(VE.h, clamp_tok[i - 1])
                    if i >= 2:
                        scalar.wait_ge(VE.h, sel_tok[i - 2])
                    scalar.wait_ge(SE.h, SE.v)
                    SE.inc(scalar.activation(qy[:], xres[i][:], AF.Identity,
                                             scale=nsc[:, i:i + 1],
                                             bias=negb[:, NT + i:NT + i + 1]))
                    scalar.wait_ge(SE.h, SE.v)
                    SE.inc(scalar.activation(qf8bh[:], qy[:], AF.Identity,
                                             scale=0.5))
                    scalar.wait_ge(SE.h, SE.v)
                    SE.inc(scalar.activation(qf8b[par][:], qf8bh[:], AF.Identity,
                                             scale=2.0))
                se_b = SE.v

                @on_ve
                def _(vector, i=i, par=par, se_b=se_b):
                    vector.wait_ge(SE.h, se_b)
                    vector.wait_ge(VE.h, VE.v)
                    sq = VE.inc(vector.tensor_mul(qy2[par][:], qy[:], qy[:]))
                    vector.wait_ge(VE.h, sq)
                    vector.wait_ge(VE.h, VE.v)
                    VE.inc(vector.tensor_scalar(qcl[par][:], qy[:], 1.0, -1.0,
                                                ALU.min, ALU.max))
                clamp_tok[i] = VE.v

                @on_se
                def _(scalar, i=i, par=par):
                    scalar.wait_ge(VE.h, clamp_tok[i])
                    scalar.wait_ge(SE.h, SE.v)
                    SE.inc(scalar.activation(qf8a[:], qcl[par][:], AF.Identity))
                    scalar.wait_ge(SE.h, SE.v)
                    SE.inc(scalar.activation(qcl[par][:], qf8a[:], AF.Identity))
                se_d = SE.v

                @on_ve
                def _(vector, i=i, par=par, se_d=se_d):
                    vector.wait_ge(SE.h, se_d)
                    vector.wait_ge(VE.h, VE.v)
                    m = VE.inc(vector.tensor_single_scalar(
                        qmask[par][:], qy2[par][:], 1.0, ALU.is_lt))
                    vector.wait_ge(VE.h, m)
                    vector.wait_ge(VE.h, VE.v)
                    VE.inc(vector.select(xq_t[i][:], qmask[par][:], qcl[par][:],
                                         qf8b[par][:], add_drain=True))
                sel_tok[i] = VE.v

                for c in range(ND):
                    emit_transpose(xq_t[i][:, 128 * c:128 * (c + 1)], idb[:], 128,
                                   pe_wait=(lambda t, v=sel_tok[i]:
                                            t.wait_ge(VE.h, v)),
                                   dst_ap=xq_f[c][:, 128 * i:128 * (i + 1)],
                                   se_evac=False)
                xqf_tok = tp_last_ve()

                o0 = 1024 * par

                @on_pe
                def _(tensor, i=i, par=par, o0=o0, xqf_tok=xqf_tok):
                    twait(tensor, xqf_tok)
                    g = ep_gate[par]
                    if g is not None:
                        tensor.wait_ge(VE.h, g)
                    for c in range(ND):
                        for half in range(2):
                            tensor.matmul(
                                ps_o[:, o0 + 512 * half:o0 + 512 * (half + 1)],
                                xq_f[c][:, 128 * i:128 * (i + 1)],
                                wqT[c][:, 512 * half:512 * (half + 1)],
                                start=(c == 0), stop=(c == ND - 1))
                    PE.inc(tensor.matmul(ps_big[0:1, 256:257], triu[0:1, 0:1],
                                         triu[0:1, 0:1], start=True, stop=True))
                mm_pe = PE.v

                @on_ve
                def _(vector, i=i, o0=o0, mm_pe=mm_pe):
                    vector.wait_ge(PE.h, mm_pe)
                    for half in range(2):
                        vector.wait_ge(VE.h, VE.v)
                        VE.inc(vector.scalar_tensor_tensor(
                            ev32[:], ps_o[:, o0 + 512 * half:o0 + 512 * (half + 1)],
                            rs_t[:], bf_t[:, 512 * half:512 * (half + 1)],
                            ALU.mult, ALU.add))
                        vector.wait_ge(VE.h, VE.v)
                        VE.inc(vector.tensor_add(
                            xres[i][:, 512 * half:512 * (half + 1)],
                            xres[i][:, 512 * half:512 * (half + 1)], ev32[:]))
                ep_gate[par] = VE.v
            sF.close()

        if n_stages != -3:
            emit_ffn(0)

    if n_stages >= 3 or n_stages == -3:
        switch_slots(SLOTS4)
        s3 = ExitStack()
        e3 = s3.enter_context
        ln3f = [e3(nc.sbuf_tensor(f"ln3f{c}", [128, T], F32)) for c in range(ND)]
        wqk_s = e3(nc.sbuf_tensor("wqk_s", [128, 1024], F32))
        hq_sp = e3(nc.sbuf_tensor("hq_sp", [128, T], BF16))
        hk_sp = e3(nc.sbuf_tensor("hk_sp", [128, T], BF16))
        hqd_h = e3(nc.sbuf_tensor("hqd_h", [128, T], BF16))
        hqd_l = e3(nc.sbuf_tensor("hqd_l", [128, T], BF16))
        hkstk = e3(nc.sbuf_tensor("hkstk", [128, T], BF16))
        vbs = [e3(nc.sbuf_tensor(f"vbs{k}", [128, NT * 65], F32)) for k in range(2)]

        ln_stats(first=(n_stages == -3))
        ln3_rstd = VE.v

        for i in range(NT):
            for sub in range(2):
                @on_ve
                def _(vector, i=i, sub=sub, _wp=tp_last_pe()):
                    if i == 0 and sub == 0:
                        vector.wait_ge(VE.h, ln3_rstd)
                    if _wp:
                        vector.wait_ge(PE.h, _wp)
                    vector.wait_ge(VE.h, VE.v)
                    VE.inc(vector.tensor_scalar(
                        ev32[:], xres[i][:, 512 * sub:512 * (sub + 1)],
                        mu[:, i:i + 1], rstd[:, i:i + 1],
                        ALU.subtract, ALU.mult))
                ve3 = VE.v
                for cl in range(4):
                    c = 4 * sub + cl
                    emit_transpose(ev32[:, 128 * cl:128 * (cl + 1)], idf[:], 128,
                                   pe_wait=(lambda t, v=ve3: t.wait_ge(VE.h, v)),
                                   dst_ap=ln3f[c][:, 128 * i:128 * (i + 1)],
                                   se_evac=False)
        ln3_done = tp_last_ve()

        wb_gate = [None]
        qk_ve_gate = [None]
        qk_pe_gate = [None]
        dup_gate = [None]
        gitem_b = [gitem[0] if n_stages >= 1 else 0]
        sem_s_base = [sem_s.v]
        sem_pv_base = [sem_pv.v]
        o_gate_b = [None]
        switch_slots(SLOTS2)
        for p in range(H // 2):
            hA, hB = 2 * p, 2 * p + 1
            @on_sy
            def _(sync, p=p, hA=hA, hB=hB, _pg=qk_pe_gate[0]):
                if _pg is not None:
                    sync.wait_ge(PE.h, _pg)
                if p == 0:
                    twait(sync, ln3_done)
                sync.wait_ge(DWS.h, DWS.v)
                for c in range(8):
                    DWS.dma(sync.dma_start(
                        out=wqk_s[:, 128 * c:128 * c + 64],
                        in_=Wqkv_d[128 * c:128 * (c + 1), 192 * hA + 128:192 * hA + 192]))
                    DWS.dma(sync.dma_start(
                        out=wqk_s[:, 128 * c + 64:128 * (c + 1)],
                        in_=Wqkv_d[128 * c:128 * (c + 1), 192 * hB + 128:192 * hB + 192]))
            wv_l = DWS.v

            @on_ve
            def _(vector, wv_l=wv_l, p=p, _pv=sem_pv.v, _pg=qk_pe_gate[0]):
                vector.wait_ge(DWS.h, wv_l)
                vector.wait_ge(sem_pv.h, _pv)
                if _pg is not None:
                    vector.wait_ge(PE.h, _pg)
                if p == 0:
                    twait(vector, ln3_done)
                vector.wait_ge(VE.h, VE.v)
                VE.inc(vector.memset(
                    vbs[0].rearrange("p (i u) -> p i u", u=65)[:, :, 64:65], 1.0))
                vector.wait_ge(VE.h, VE.v)
                VE.inc(vector.memset(
                    vbs[1].rearrange("p (i u) -> p i u", u=65)[:, :, 64:65], 1.0))
            wvb_split = VE.v

            @on_pe
            def _(tensor, wvb_split=wvb_split, wv_l=wv_l):
                tensor.wait_ge(VE.h, wvb_split)
                tensor.wait_ge(DWS.h, wv_l)
                for c in range(8):
                    for nq in range(4):
                        n0 = 512 * nq
                        tensor.matmul(ps_o[:, n0:n0 + 512],
                                      wqk_s[:, 128 * c:128 * (c + 1)],
                                      ln3f[c][:, n0:n0 + 512],
                                      start=(c == 0), stop=(c == 7))
                PE.inc(tensor.matmul(ps_big[0:1, 256:257], triu[0:1, 0:1],
                                     triu[0:1, 0:1], start=True, stop=True))
            vb_pe = PE.v
            qk_pe_gate[0] = vb_pe

            for nq in range(4):
                @on_ve
                def _(vector, nq=nq, vb_pe=vb_pe):
                    vector.wait_ge(PE.h, vb_pe)
                    vector.wait_ge(VE.h, VE.v)
                    VE.inc(vector.tensor_copy(ev32[:], ps_o[:, 512 * nq:512 * (nq + 1)]))
                evd = VE.v
                for tloc in range(4):
                    i = 4 * nq + tloc

                    def evacVB(vector, view, i=i):
                        vector.wait_ge(VE.h, VE.v)
                        VE.inc(vector.tensor_copy(
                            vbs[0][:, 65 * i:65 * i + 64], view[:, 0:64]))
                        vector.wait_ge(VE.h, VE.v)
                        return VE.inc(vector.tensor_copy(
                            vbs[1][:, 65 * i:65 * i + 64], view[:, 64:128]))
                    emit_transpose(ev32[:, 128 * tloc:128 * (tloc + 1)], idf[:], 128,
                                   pe_wait=(lambda t, v=evd: t.wait_ge(VE.h, v)),
                                   evac_fn=evacVB)
            vb_done = tp_last_ve()

            for hi, h in enumerate([hA, hB]):
                @on_sy
                def _(sync, h=h, vb_done=vb_done, _pg=qk_pe_gate[0]):
                    twait(sync, vb_done)
                    if _pg is not None:
                        sync.wait_ge(PE.h, _pg)
                    sync.wait_ge(DWS.h, DWS.v)
                    for c in range(8):
                        DWS.dma(sync.dma_start(
                            out=wqk_s[:, 128 * c:128 * (c + 1)],
                            in_=Wqkv_d[128 * c:128 * (c + 1), 192 * h:192 * h + 128]))
                wqk_ld = DWS.v

                @on_pe
                def _(tensor, wqk_ld=wqk_ld, _xv=sem_exp.v, _tv=tp_last_ve()):
                    tensor.wait_ge(DWS.h, wqk_ld)
                    tensor.wait_ge(sem_exp.h, _xv)
                    twait(tensor, _tv)
                    for c in range(8):
                        for nq in range(4):
                            n0 = 512 * nq
                            tensor.matmul(ps_big[:, n0:n0 + 512],
                                          wqk_s[:, 128 * c:128 * (c + 1)],
                                          ln3f[c][:, n0:n0 + 512],
                                          start=(c == 0), stop=(c == 7))
                    PE.inc(tensor.matmul(ps_o[0:1, 256:257], triu[0:1, 0:1],
                                         triu[0:1, 0:1], start=True, stop=True))
                qk_pe = PE.v
                qk_pe_gate[0] = qk_pe

                @on_ve
                def _(vector, qk_pe=qk_pe):
                    vector.wait_ge(PE.h, qk_pe)
                    vector.wait_ge(VE.h, VE.v)
                    a_ = VE.inc(vector.tensor_scalar_mul(hq_sp[0:64, :],
                                                         ps_big[0:64, :], 0.125))
                    vector.wait_ge(VE.h, a_)
                    vector.wait_ge(VE.h, VE.v)
                    VE.inc(vector.scalar_tensor_tensor(
                        hk_sp[0:64, :], ps_big[0:64, :], 0.125, hq_sp[0:64, :],
                        ALU.mult, ALU.subtract))
                    vector.wait_ge(VE.h, VE.v)
                    b_ = VE.inc(vector.tensor_copy(hk_sp[64:128, :],
                                                   ps_big[64:128, :]))
                    vector.wait_ge(VE.h, b_)
                    vector.wait_ge(VE.h, VE.v)
                    VE.inc(vector.tensor_sub(hq_sp[64:128, :], ps_big[64:128, :],
                                             hk_sp[64:128, :]))
                split_ve = VE.v

                for (dst, srcs) in [
                        (hqd_h, (hq_sp[0:64, :], hq_sp[0:64, :], idb[0:64, 0:64])),
                        (hqd_l, (hk_sp[0:64, :], hk_sp[0:64, :], idb[0:64, 0:64])),
                        (hkstk, (hk_sp[64:128, :], hq_sp[64:128, :], idb[64:128, 64:128])),
                ]:
                    src_top, src_bot, identd = srcs

                    @on_pe
                    def _(tensor, split_ve=split_ve, src_top=src_top,
                          src_bot=src_bot, identd=identd, _xv=sem_exp.v):
                        tensor.wait_ge(VE.h, max(split_ve, dup_gate[0] or 0))
                        tensor.wait_ge(sem_exp.h, _xv)
                        for nq in range(4):
                            n0 = 512 * nq
                            tensor.matmul(ps_big[0:64, n0:n0 + 512], identd,
                                          src_top[:, n0:n0 + 512],
                                          start=True, stop=True)
                            tensor.matmul(ps_big[64:128, n0:n0 + 512], identd,
                                          src_bot[:, n0:n0 + 512],
                                          start=True, stop=True,
                                          skip_group_check=True)
                        PE.inc(tensor.matmul(ps_o[0:1, 256:257], triu[0:1, 0:1],
                                             triu[0:1, 0:1], start=True, stop=True))
                    dup_pe = PE.v

                    @on_ve
                    def _(vector, dup_pe=dup_pe, dst=dst):
                        vector.wait_ge(PE.h, dup_pe)
                        vector.wait_ge(VE.h, VE.v)
                        VE.inc(vector.tensor_copy(dst[:], ps_big[:]))
                    dup_gate[0] = VE.v
                stk_ready = VE.v

                items_b = [(j, nchunk) for j in range(NT) for nchunk in range(4)]
                base_n = gitem_b[0]
                prev_o = o_gate_b[0]

                @on_pe
                def _(tensor, h=h, hi=hi, items_b=items_b, base_n=base_n,
                      prev_o=prev_o, stk_ready=stk_ready):
                    tensor.wait_ge(VE.h, stk_ready)

                    def emit_S(it):
                        j, nchunk = items_b[it]
                        n = base_n + it
                        par = n % 2
                        if n >= 2:
                            tensor.wait_ge(sem_exp.h, n - 1)
                        tensor.matmul(
                            ps_big[:, 512 * par:512 * (par + 1)],
                            hkstk[:, 128 * j:128 * (j + 1)],
                            hqd_h[:, 512 * nchunk:512 * (nchunk + 1)],
                            start=True, stop=False)
                        sem_s.inc(tensor.matmul(
                            ps_big[:, 512 * par:512 * (par + 1)],
                            hkstk[:, 128 * j:128 * (j + 1)],
                            hqd_l[:, 512 * nchunk:512 * (nchunk + 1)],
                            start=False, stop=True))

                    def emit_PV(it):
                        j, nchunk = items_b[it]
                        n = base_n + it
                        par = n % 2
                        tensor.wait_ge(sem_exp.h, n + 1)
                        if it == 0 and prev_o is not None:
                            twait(tensor, prev_o)
                        sem_pv.inc(tensor.matmul(
                            ps_o[0:65, 512 * nchunk:512 * nchunk + 512],
                            vbs[hi][:, 65 * j:65 * j + 65],
                            pT[par][:, 0:512],
                            start=(j == 0), stop=(j == NT - 1)))

                    emit_S(0)
                    for it in range(1, len(items_b)):
                        emit_S(it)
                        emit_PV(it - 1)
                    emit_PV(len(items_b) - 1)
                gitem_b[0] += len(items_b)

                @on_se
                def _(scalar, items_b=items_b, base_n=base_n):
                    for it in range(len(items_b)):
                        n = base_n + it
                        par = n % 2
                        scalar.wait_ge(sem_s.h, sem_s_base[0] + it + 1)
                        if n >= 2:
                            scalar.wait_ge(sem_pv.h, sem_pv_base[0] + it - 1)
                        sem_exp.inc(scalar.activation(
                            pT[par][:, 0:512], ps_big[:, 512 * par:512 * (par + 1)],
                            AF.Exp))
                sem_s_base[0] += len(items_b)
                sem_pv_base[0] += len(items_b)

                @on_ve
                def _(vector, h=h):
                    vector.wait_ge(sem_pv.h, sem_pv.v)
                    vector.wait_ge(VE.h, VE.v)
                o_pv_done = VE.v
                for qc in range(4):
                    @on_ve
                    def _(vector, qc=qc):
                        vector.wait_ge(sem_pv.h, sem_pv.v)
                        vector.wait_ge(VE.h, VE.v)
                        VE.inc(vector.tensor_copy(
                            ev32[0:65, :], ps_o[0:65, 512 * qc:512 * (qc + 1)]))
                    evo = VE.v
                    for tloc in range(4):
                        i = 4 * qc + tloc

                        def evacOB(vector, view, i=i, h=h):
                            vector.wait_ge(VE.h, VE.v)
                            r = VE.inc(vector.reciprocal(rd1[:], view[:, 64:65]))
                            vector.wait_ge(VE.h, r)
                            return VE.inc(vector.scalar_tensor_tensor(
                                xres[i][:, 64 * h:64 * h + 64], view[:, 0:64],
                                rd1[:], xres[i][:, 64 * h:64 * h + 64],
                                ALU.mult, ALU.add))
                        emit_transpose(ev32[0:65, 128 * tloc:128 * (tloc + 1)],
                                       idf[0:65, 0:65], 128,
                                       pe_wait=(lambda t, v=evo: t.wait_ge(VE.h, v)),
                                       evac_fn=evacOB, n_out=65)
                o_gate_b[0] = tp_last_ve()
            wb_gate[0] = VE.v
        s3.close()

    if n_stages >= 4:
        emit_ffn(1)
    if n_stages >= 2 or n_stages == -2:
        s2.close()

    @on_sy
    def _(sync):
        sync.wait_ge(VE.h, VE.v)
        sync.wait_ge(PE.h, PE.v)
        sync.wait_ge(SE.h, SE.v)
        sync.wait_ge(sem_pv.h, sem_pv.v)
        sync.wait_ge(sem_exp.h, sem_exp.v)
        sync.wait_ge(sem_msk.h, sem_msk.v)
        for i in range(NT):
            DCC.dma(sync.dma_start(out=out_d[128 * i:128 * (i + 1), :],
                                   in_=xres[i][:]))
        sync.wait_ge(DCC.h, DCC.v)

    st.close()
    return nc


def make_in_maps(inputs):
    x = np.ascontiguousarray(np.asarray(inputs["L0_attn_a_x"], dtype=np.float32))
    w = {
        "W1q": np.ascontiguousarray(inputs["L0_attn_a_W1_q"], dtype=np.float32),
        "W2q": np.ascontiguousarray(inputs["L0_attn_a_W2_q"], dtype=np.float32),
        "W1k": np.ascontiguousarray(inputs["L0_attn_a_W1_k"], dtype=np.float32),
        "W2k": np.ascontiguousarray(inputs["L0_attn_a_W2_k"], dtype=np.float32),
        "Wv": np.ascontiguousarray(inputs["L0_attn_a_W_v"], dtype=np.float32),
        "bv": np.ascontiguousarray(np.asarray(inputs["L0_attn_a_b_v"]).reshape(1, D), dtype=np.float32),
        "Wo": np.ascontiguousarray(inputs["L0_attn_a_W_o"], dtype=np.float32),
        "bo": np.ascontiguousarray(np.asarray(inputs["L0_attn_a_b_o"]).reshape(1, D), dtype=np.float32),
        "Wqkv": np.ascontiguousarray(inputs["L0_attn_b_MhWqkv"], dtype=np.float32),
        "Wf": np.ascontiguousarray(inputs["L0_ffn_weight"], dtype=np.float32),
        "bf": np.ascontiguousarray(np.asarray(inputs["L0_ffn_bias"]).reshape(1, D), dtype=np.float32),
    }
    return [dict(w, x=np.ascontiguousarray(x[c])) for c in range(N_CORES)]


def kernel(**inputs):
    nc = build_nc(4)
    in_maps = make_in_maps(inputs)
    res = run_bass_kernel_spmd(nc, in_maps, core_ids=list(range(N_CORES)))
    return np.stack([np.asarray(r["out"]) for r in res.results], axis=0)
